# version 12
# baseline (speedup 1.0000x reference)
import numpy as np
import ml_dtypes
import concourse.bass as bass
import concourse.mybir as mybir
from concourse.bass_utils import run_bass_kernel_spmd
from contextlib import ExitStack

F32 = mybir.dt.float32
BF16 = mybir.dt.bfloat16
I8 = mybir.dt.int8
I32 = mybir.dt.int32
AF = mybir.ActivationFunctionType
OP = mybir.AluOpType
AX = mybir.AxisListType.X

MAGIC = float(np.float32(1.5 * 2 ** 23))
FLAGV = 0x5A5A0000

D = 2048
IFF = 8192
TPC = 1024
NTT = 8
NH, DK, DV, NKH = 16, 64, 64, 8
CCH, NCHK = 64, 16
C2048 = 1.0 / float(np.sqrt(2048.0))
C8192 = 1.0 / float(np.sqrt(8192.0))

_PHASES = "all"
_ACUT = None
_CCUT = 16
_KCUT = 5
_P1MASK = 63
_DBG = False
_DBGA = False


def _hadamard(n):
    i = np.arange(n)
    a = i[:, None] & i[None, :]
    pc = np.zeros((n, n), dtype=np.int64)
    while a.any():
        pc += a & 1
        a >>= 1
    return np.where(pc % 2 == 0, 1.0, -1.0).astype(np.float32)


class Trk:
    COMP = ("v", "s", "t", "p")

    def __init__(self, nc, st):
        self.nc = nc
        self.eng = {"v": nc.vector, "s": nc.scalar, "t": nc.tensor, "p": nc.gpsimd,
                    "q0": nc.sync, "q1": nc.scalar, "q2": nc.gpsimd}
        self.sem = {}
        for d in ("v", "s", "t", "p", "q0", "q1", "q2"):
            self.sem[d] = st.enter_context(nc.semaphore(f"trk_{d}"))
        self.cnt = {d: 0 for d in self.sem}
        self.lastw = {}
        self.readers = {}
        self.waited = {}

    def _wait(self, we, dom, val):
        if val <= 0 or val <= self.waited.get((we, dom), 0):
            return
        self.eng[we].wait_ge(self.sem[dom], val)
        self.waited[(we, dom)] = val

    def pre(self, e, reads=(), writes=()):
        for k in reads:
            lw = self.lastw.get(k)
            if lw is not None and not (lw[0] == e == "t"):
                self._wait(e, lw[0], lw[1])
        for k in writes:
            lw = self.lastw.get(k)
            if lw is not None and not (lw[0] == e == "t"):
                self._wait(e, lw[0], lw[1])
            for (re, rv) in self.readers.get(k, ()):
                if not (re == e == "t"):
                    self._wait(e, re, rv)

    def post(self, e, inst, reads=(), writes=()):
        amt = 16 if e in ("q0", "q1", "q2") else 1
        self.cnt[e] += amt
        inst.then_inc(self.sem[e], amt)
        cur = (e, self.cnt[e])
        for k in writes:
            self.lastw[k] = cur
            self.readers[k] = []
        for k in reads:
            self.readers.setdefault(k, []).append(cur)

    def ve(self, fn, *args, reads=(), writes=(), **kw):
        self.pre("v", reads, writes)
        inst = fn(*args, **kw)
        self.post("v", inst, reads, writes)
        return inst

    def se(self, fn, *args, reads=(), writes=(), **kw):
        self.pre("s", reads, writes)
        inst = fn(*args, **kw)
        self.post("s", inst, reads, writes)
        return inst

    def po(self, fn, *args, reads=(), writes=(), **kw):
        self.pre("p", reads, writes)
        inst = fn(*args, **kw)
        self.post("p", inst, reads, writes)
        return inst

    class TGroup:
        def __init__(self, trk, reads, writes):
            self.trk, self.reads, self.writes = trk, tuple(reads), tuple(writes)
            self.last = None

        def __enter__(self):
            self.trk.pre("t", self.reads, self.writes)
            return self

        def add(self, inst):
            self.last = inst
            return inst

        def __exit__(self, et, ev, tb):
            if et is None:
                assert self.last is not None
                self.trk.post("t", self.last, self.reads, self.writes)

    def tg(self, reads=(), writes=()):
        return Trk.TGroup(self, reads, writes)

    def dma(self, q, out, in_, reads=(), writes=(), transpose=False):
        we = {"q0": "q0", "q1": "s", "q2": "p"}[q]
        self.pre(we, reads, writes)
        inst = self.eng[q].dma_start(out=out, in_=in_, transpose=transpose)
        self.post(q, inst, reads, writes)
        return inst

    def join(self):
        for e in self.COMP + ("q0",):
            for d, c in self.cnt.items():
                if d != e:
                    self._wait(e, d, c)

    def wait_all_on(self, e):
        for d, c in self.cnt.items():
            if d != e:
                self._wait(e, d, c)


def build_program(phases="all"):
    nc = bass.Bass("TRN2", target_bir_lowering=False, num_devices=8,
                   detect_race_conditions=False)

    xs = nc.dram_tensor("xs", [TPC, D], F32, kind="ExternalInput")
    xo = nc.dram_tensor("xo", [TPC, D], F32, kind="ExternalOutput")
    wg_raw = nc.dram_tensor("wg_raw", [IFF // 2, D], F32, kind="ExternalInput")
    wu_raw = nc.dram_tensor("wu_raw", [IFF // 2, D], F32, kind="ExternalInput")
    wd_raw = nc.dram_tensor("wd_raw", [D // 2, IFF], F32, kind="ExternalInput")
    wqkv_raw = nc.dram_tensor("wqkv_raw", [D // 2, D], F32, kind="ExternalInput")
    wz_raw = nc.dram_tensor("wz_raw", [512, D], F32, kind="ExternalInput")
    wout_raw = nc.dram_tensor("wout_raw", [D // 2, 1024], F32, kind="ExternalInput")
    wba_raw = nc.dram_tensor("wba_raw", [32, D], F32, kind="ExternalInput")
    convw = nc.dram_tensor("convw", [D, 4], F32, kind="ExternalInput")
    smalls = nc.dram_tensor("smalls", [3, 64], F32, kind="ExternalInput")
    h2048c = nc.dram_tensor("h2048c", [D, D], BF16, kind="ExternalInput")
    b1c = nc.dram_tensor("b1c", [128, 128], BF16, kind="ExternalInput")
    h128c = nc.dram_tensor("h128c", [128, 128], BF16, kind="ExternalInput")
    idtc = nc.dram_tensor("idtc", [128, 128], BF16, kind="ExternalInput")
    attc = nc.dram_tensor("attc", [6, 64 * 64], F32, kind="ExternalInput")
    pid_t = nc.partition_id_tensor

    shWg = nc.dram_tensor("shWg", [IFF, D], BF16, addr_space="Shared")
    shWu = nc.dram_tensor("shWu", [IFF, D], BF16, addr_space="Shared")
    shWd = nc.dram_tensor("shWd", [D, IFF], BF16, addr_space="Shared")
    shWgT = nc.dram_tensor("shWgT", [D, IFF], BF16, addr_space="Shared")
    shWuT = nc.dram_tensor("shWuT", [D, IFF], BF16, addr_space="Shared")
    shWdT = nc.dram_tensor("shWdT", [IFF, D], BF16, addr_space="Shared")
    shWqkv = nc.dram_tensor("shWqkv", [D, D], BF16, addr_space="Shared")
    shWz = nc.dram_tensor("shWz", [1024, D], BF16, addr_space="Shared")
    shWout = nc.dram_tensor("shWout", [D, 1024], BF16, addr_space="Shared")
    shWba = nc.dram_tensor("shWba", [32, D], BF16, addr_space="Shared")
    shScale = nc.dram_tensor("shScale", [2, 16], F32, addr_space="Shared")
    shFlg = nc.dram_tensor("shFlg", [2, 32], I32, addr_space="Shared")
    shHalo = nc.dram_tensor("shHalo", [128, 48], F32, addr_space="Shared")
    shS16 = nc.dram_tensor("shS16", [64, 1024], F32, addr_space="Shared")
    if _DBGA:
        d_lnT = nc.dram_tensor("d_lnT", [128, 1024], BF16, kind="ExternalOutput")
        d_q = nc.dram_tensor("d_q", [64, 8192], BF16, kind="ExternalOutput")
        d_k = nc.dram_tensor("d_k", [64, 8192], BF16, kind="ExternalOutput")
        d_vcm = nc.dram_tensor("d_vcm", [64, 16384], BF16, kind="ExternalOutput")
        d_kcm = nc.dram_tensor("d_kcm", [64, 8192], BF16, kind="ExternalOutput")
        d_gb = nc.dram_tensor("d_gb", [16, 2048], F32, kind="ExternalOutput")
        d_cum = nc.dram_tensor("d_cum", [64, 1280], F32, kind="ExternalOutput")
        d_cumT = nc.dram_tensor("d_cumT", [128, 128], F32, kind="ExternalOutput")
        d_y = nc.dram_tensor("d_y", [128, 8192], BF16, kind="ExternalOutput")
        d_S = nc.dram_tensor("d_S", [64, 1024], F32, kind="ExternalOutput")
        d_z = nc.dram_tensor("d_z", [128, 1024], BF16, kind="ExternalOutput")
        d_c0 = nc.dram_tensor("d_c0", [128, 8192], F32, kind="ExternalOutput")
        d_c1 = nc.dram_tensor("d_c1", [128, 8192], F32, kind="ExternalOutput")
        d_cU = nc.dram_tensor("d_cU", [64, 16384], F32, kind="ExternalOutput")
        d_cS2 = nc.dram_tensor("d_cS2", [64, 16384], F32, kind="ExternalOutput")
        d_cE1 = nc.dram_tensor("d_cE1", [64, 16384], F32, kind="ExternalOutput")
        d_cE2 = nc.dram_tensor("d_cE2", [64, 16384], F32, kind="ExternalOutput")
        d_cP = nc.dram_tensor("d_cP", [64, 16384], F32, kind="ExternalOutput")
        d_cQ = nc.dram_tensor("d_cQ", [64, 16384], F32, kind="ExternalOutput")
    if _DBG:
        d_swcol = nc.dram_tensor("d_swcol", [128, 8], F32, kind="ExternalOutput")
        d_i1T = nc.dram_tensor("d_i1T", [128, 1024], BF16, kind="ExternalOutput")
        d_inv1 = nc.dram_tensor("d_inv1", [128, 8], F32, kind="ExternalOutput")
        d_h0 = nc.dram_tensor("d_h0", [128, 1024], BF16, kind="ExternalOutput")
        d_h2 = nc.dram_tensor("d_h2", [128, 1024], BF16, kind="ExternalOutput")
        d_i2T = nc.dram_tensor("d_i2T", [128, 512], I8, kind="ExternalOutput")
        d_wg = nc.dram_tensor("d_wg", [128, D], BF16, kind="ExternalOutput")
        d_wd = nc.dram_tensor("d_wd", [128, 1024], BF16, kind="ExternalOutput")
        d_xn = nc.dram_tensor("d_xn", [128, D], BF16, kind="ExternalOutput")
        d_xnT = nc.dram_tensor("d_xnT", [128, D], BF16, kind="ExternalOutput")
        d_f1 = nc.dram_tensor("d_f1", [128, D], F32, kind="ExternalOutput")
        d_hh = nc.dram_tensor("d_hh", [128, 3 * D], BF16, kind="ExternalOutput")
        d_pg = nc.dram_tensor("d_pg", [128, 512], F32, kind="ExternalOutput")
        d_pu = nc.dram_tensor("d_pu", [128, 512], F32, kind="ExternalOutput")
        d_cg = nc.dram_tensor("d_cg", [128, 8], F32, kind="ExternalOutput")
        d_wgs = nc.dram_tensor("d_wgs", [128, 2048], BF16, kind="ExternalOutput")
    xsp1 = nc.dram_tensor("xsp1", [TPC, D], F32)
    xsp2 = nc.dram_tensor("xsp2", [TPC, D], F32)
    scrd = nc.dram_tensor("scal_scr", [1, 64], F32)
    scrd2 = nc.dram_tensor("scrd2", [1, 2048], F32)
    zsp = nc.dram_tensor("zsp", [1024, 1024], BF16)

    st_top = ExitStack()
    _uid = [0]

    def un(base):
        _uid[0] += 1
        return f"{base}_{_uid[0]}"

    with st_top:
        trk = Trk(nc, st_top)
        g = nc.gpsimd

        def sb(name, shape, dt=F32):
            return st_top.enter_context(nc.sbuf_tensor(name, shape, dt))

        st_top.enter_context(nc.Block())

        i1Tbox = {}
        cB1 = sb("cB1", [128, 128], BF16)
        cH128 = sb("cH128", [128, 128], BF16)
        cIDT = sb("cIDT", [128, 128], BF16)
        inv1 = sb("inv1", [128, 8], F32)
        inv2 = sb("inv2", [128, 8], F32)
        cgc = sb("cgc", [128, 8], F32)
        cuc = sb("cuc", [128, 8], F32)
        cdc = sb("cdc", [128, 8], F32)
        swcol = sb("swcol", [128, 8], F32)
        onesf = sb("onesf", [128, 1], F32)
        ppacc = sb("ppacc", [128, 4], F32)
        pp1 = sb("pp1", [1, 64], F32)
        ftile = sb("ftile", [1, 32], I32)
        pfl = sb("pfl", [1, 32], I32)
        pidsb = sb("pidsb", [1, 1], mybir.dt.uint32)

        r_loop = g.alloc_register("r_loop")
        r_val = g.alloc_register("r_val")
        r_pw = g.alloc_register("r_pw")
        r_pid = g.alloc_register("r_pid")
        r_par = g.alloc_register("r_par")
        pollsem = st_top.enter_context(nc.semaphore("pollsem"))
        g.reg_mov(r_pw, 0)

        trk.dma("q2", pidsb[:], pid_t[0:1, 0:1], writes=[("pid",)])
        trk.pre("p", reads=[("pid",)])
        g.reg_load(r_pid, pidsb[0:1, 0:1])
        g.reg_alu(r_par, r_pid, 1, OP.bitwise_and)

        trk.dma("q0", cB1[:], b1c[:, :], writes=[("cB1",)])
        trk.dma("q0", cH128[:], h128c[:, :], writes=[("cH128",)])
        trk.dma("q0", cIDT[:], idtc[:, :], writes=[("cIDT",)])
        trk.ve(nc.vector.memset, onesf[:], 1.0, writes=[("ones",)])

        def gwm():
            g.reg_add(r_pw, r_pw, 16)
            g.wait_ge(pollsem, r_pw)

        def set_flag(widx):
            trk.ve(nc.vector.memset, ftile[0:1, widx:widx + 1], FLAGV + widx,
                   writes=[("ftile", widx)])
            trk.pre("p", reads=[("ftile", widx)])
            with g.If_eq(r_par, 0):
                g.dma_start(out=shFlg[0, widx:widx + 1], in_=ftile[0:1, widx:widx + 1]).then_inc(pollsem, 16)
            with g.Else():
                g.dma_start(out=shFlg[1, widx:widx + 1], in_=ftile[0:1, widx:widx + 1]).then_inc(pollsem, 16)
            gwm()

        def poll_flag(widx, key):
            g.reg_mov(r_loop, 1)
            with g.While(r_loop):
                with g.If_eq(r_par, 0):
                    g.dma_start(out=pfl[0:1, widx:widx + 1], in_=shFlg[1, widx:widx + 1]).then_inc(pollsem, 16)
                with g.Else():
                    g.dma_start(out=pfl[0:1, widx:widx + 1], in_=shFlg[0, widx:widx + 1]).then_inc(pollsem, 16)
                g.reg_add(r_pw, r_pw, 16)
                g.wait_ge(pollsem, r_pw)
                g.reg_load(r_val, pfl[0:1, widx:widx + 1])
                g.reg_alu(r_loop, r_val, FLAGV + widx, OP.bitwise_xor)
            inst = g.engine_nop()
            trk.post("p", inst, writes=[key])

        def pair_write(even_ap, odd_ap, src_ap, reads):
            trk.pre("p", reads=list(reads))
            with g.If_eq(r_par, 0):
                g.dma_start(out=even_ap, in_=src_ap).then_inc(pollsem, 16)
            with g.Else():
                g.dma_start(out=odd_ap, in_=src_ap).then_inc(pollsem, 16)
            gwm()
            inst = g.engine_nop()
            trk.post("p", inst, reads=list(reads))

        def pair_read_peer(dst_ap, even_ap, odd_ap, writes):
            trk.pre("p", writes=list(writes))
            with g.If_eq(r_par, 0):
                g.dma_start(out=dst_ap, in_=odd_ap).then_inc(pollsem, 16)
            with g.Else():
                g.dma_start(out=dst_ap, in_=even_ap).then_inc(pollsem, 16)
            gwm()
            inst = g.engine_nop()
            trk.post("p", inst, writes=list(writes))

        def half_write_tiles(pairs, reads):
            trk.pre("p", reads=list(reads))
            with g.If_eq(r_par, 0):
                for de, do, sr in pairs:
                    g.dma_start(out=de, in_=sr).then_inc(pollsem, 16)
            with g.Else():
                for de, do, sr in pairs:
                    g.dma_start(out=do, in_=sr).then_inc(pollsem, 16)
            g.reg_add(r_pw, r_pw, 16 * len(pairs))
            g.wait_ge(pollsem, r_pw)
            inst = g.engine_nop()
            trk.post("p", inst, reads=list(reads))

        def half_write(dst_even_ap, dst_odd_ap, src_ap, reads):
            trk.pre("p", reads=list(reads))
            with g.If_eq(r_par, 0):
                g.dma_start(out=dst_even_ap, in_=src_ap).then_inc(pollsem, 16)
            with g.Else():
                g.dma_start(out=dst_odd_ap, in_=src_ap).then_inc(pollsem, 16)
            gwm()
            inst = g.engine_nop()
            trk.post("p", inst, reads=list(reads))

        stI1 = ExitStack()
        i1T_early = stI1.enter_context(nc.sbuf_tensor(un("i1T_early"), [128, 16 * 1024], BF16))
        stPP = ExitStack()
        raws = stPP.enter_context(nc.sbuf_tensor(un("pp_raw"), [128, 2 * 2048], F32))
        qf = stPP.enter_context(nc.sbuf_tensor(un("pp_qf"), [128, 2048], F32))
        qout = stPP.enter_context(nc.sbuf_tensor(un("pp_q"), [128, 2 * 2048], BF16))
        ppsum = stPP.enter_context(nc.psum_tensor(un("pp_ps"), [1, 8], F32))
        ppsT = [stPP.enter_context(nc.psum_tensor(un("pp_psT"), [128, 128], BF16)) for _ in range(2)]
        tT = [stPP.enter_context(nc.sbuf_tensor(un("pp_tT"), [128, 2048], BF16)) for _ in range(2)]

        def pp_tiles(widx, raw, n_rows, n_cols):
            out = []
            for i in range(n_rows // 128):
                for j in range(n_cols // 2048):
                    out.append((i * 128, j * 2048))
            return out

        def pp_scale(widx, raw, n_rows, n_cols, flg_scale):
            tiles = pp_tiles(widx, raw, n_rows, n_cols)
            trk.ve(nc.vector.memset, ppacc[:, widx:widx + 1], 0.0, writes=[("ppacc", widx)])
            for i, (r0, c0) in enumerate(tiles):
                buf = i % 2
                rt = raws[:, buf * 2048:(buf + 1) * 2048]
                trk.dma("q0", rt, raw[r0:r0 + 128, c0:c0 + 2048], writes=[("ppraw", buf)])
                trk.ve(nc.vector.tensor_reduce, ppacc[:, 3:4], rt, axis=AX, op=OP.add,
                       apply_absolute_value=True, reads=[("ppraw", buf)], writes=[("ppt",)])
                trk.ve(nc.vector.tensor_tensor, ppacc[:, widx:widx + 1], ppacc[:, widx:widx + 1],
                       ppacc[:, 3:4], op=OP.add,
                       reads=[("ppt",), ("ppacc", widx)], writes=[("ppacc", widx)])
            with trk.tg(reads=[("ppacc", widx), ("ones",)], writes=[("pps", widx)]) as t:
                t.add(nc.tensor.matmul(ppsum[0:1, widx:widx + 1], ppacc[:, widx:widx + 1],
                                       onesf[:, 0:1], start=True, stop=True))
            trk.se(nc.scalar.activation, pp1[0:1, widx:widx + 1], ppsum[0:1, widx:widx + 1], AF.Copy,
                   reads=[("pps", widx)], writes=[("pp1", widx)])
            pair_write(shScale[0, widx:widx + 1], shScale[1, widx:widx + 1],
                       pp1[0:1, widx:widx + 1], reads=[("pp1", widx)])
            set_flag(flg_scale)
            poll_flag(flg_scale, ("flg", flg_scale))
            pair_read_peer(pp1[0:1, 16 + widx:17 + widx], shScale[0, widx:widx + 1],
                           shScale[1, widx:widx + 1], writes=[("pp1p", widx)])
            n_el = float(2 * n_rows * n_cols)
            trk.ve(nc.vector.tensor_tensor, pp1[0:1, 8 + widx:9 + widx], pp1[0:1, widx:widx + 1],
                   pp1[0:1, 16 + widx:17 + widx], op=OP.add,
                   reads=[("pp1", widx), ("pp1p", widx)], writes=[("swsum", widx)])
            trk.ve(nc.vector.tensor_scalar, pp1[0:1, 8 + widx:9 + widx], pp1[0:1, 8 + widx:9 + widx],
                   1.0 / n_el, 1e-5, op0=OP.mult, op1=OP.max,
                   reads=[("swsum", widx)], writes=[("sw", widx)])
            trk.ve(nc.vector.reciprocal, pp1[0:1, 24 + widx:25 + widx], pp1[0:1, 8 + widx:9 + widx],
                   reads=[("sw", widx)], writes=[("swi", widx)])
            trk.dma("q2", scrd[0:1, 2 * widx:2 * widx + 1], pp1[0:1, 8 + widx:9 + widx],
                    reads=[("sw", widx)], writes=[("scr_sw", widx)])
            trk.dma("q2", scrd[0:1, 2 * widx + 1:2 * widx + 2], pp1[0:1, 24 + widx:25 + widx],
                    reads=[("swi", widx)], writes=[("scr_swi", widx)])
            trk.dma("q2", swcol[:, widx:widx + 1], bass.AP(scrd, 2 * widx, [[0, 128], [1, 1]]),
                    reads=[("scr_sw", widx)], writes=[("swb", widx)])
            trk.dma("q2", swcol[:, 4 + widx:5 + widx], bass.AP(scrd, 2 * widx + 1, [[0, 128], [1, 1]]),
                    reads=[("scr_swi", widx)], writes=[("swib", widx)])

        def pp_quant_gu(widx, raw, sh_out, n_rows, flg_q, shkey):
            tiles = pp_tiles(widx, raw, n_rows, D)
            for i, (r0, c0) in enumerate(tiles):
                buf = i % 2
                rt = raws[:, buf * 2048:(buf + 1) * 2048]
                qt = qout[:, buf * 2048:(buf + 1) * 2048]
                trk.dma("q0", rt, raw[r0:r0 + 128, c0:c0 + 2048], writes=[("ppraw", buf)])
                trk.se(nc.scalar.activation, qf[:], rt, AF.Copy,
                       scale=swcol[:, 4 + widx:5 + widx], bias=MAGIC,
                       reads=[("ppraw", buf), ("swib", widx)], writes=[("ppqf",)])
                trk.ve(nc.vector.tensor_scalar, qf[:], qf[:], -MAGIC, 1.0, op0=OP.add, op1=OP.min,
                       reads=[("ppqf",)], writes=[("ppqf",)])
                trk.ve(nc.vector.tensor_scalar, qt, qf[:], -1.0, None, op0=OP.max,
                       reads=[("ppqf",)], writes=[("ppq", buf)])
                for k in range(16):
                    pb = k % 2
                    with trk.tg(reads=[("ppq", buf), ("cIDT",)], writes=[("ppsT", pb)]) as t:
                        t.add(nc.tensor.transpose(ppsT[pb][:], qt[:, k * 128:(k + 1) * 128], cIDT[:]))
                    trk.se(nc.scalar.activation, tT[buf][:, k * 128:(k + 1) * 128], ppsT[pb][:],
                           AF.Copy, reads=[("ppsT", pb)], writes=[("ppT", buf)])
                half_write_tiles(
                    [(sh_out[k * 128:(k + 1) * 128, r0:r0 + 128],
                      sh_out[k * 128:(k + 1) * 128, n_rows + r0:n_rows + r0 + 128],
                      tT[buf][:, k * 128:(k + 1) * 128]) for k in range(16)],
                    reads=[("ppT", buf)])
            set_flag(flg_q)
            poll_flag(flg_q, ("flg", flg_q))
            trk.lastw[shkey] = ("p", trk.cnt["p"])

        def pp_quant_wd(widx, raw, sh_out, n_rows, flg_q, shkey):
            stWd = ExitStack()
            rawD = stWd.enter_context(nc.sbuf_tensor(un("pp_rawD"), [128, IFF], F32))
            tD = stWd.enter_context(nc.sbuf_tensor(un("pp_tD"), [128, IFF], F32))
            qD = stWd.enter_context(nc.sbuf_tensor(un("pp_qD"), [128, IFF], BF16))
            for rb in range(n_rows // 128):
                r0 = rb * 128
                for j in range(4):
                    trk.dma("q0", rawD[:, j * 2048:(j + 1) * 2048],
                            raw[r0:r0 + 128, j * 2048:(j + 1) * 2048], writes=[("ppD", j)])
                trk.se(nc.scalar.activation, tD[:], rawD[:], AF.Copy,
                       scale=swcol[:, 4 + widx:5 + widx], bias=MAGIC,
                       reads=[("ppD", 0), ("ppD", 1), ("ppD", 2), ("ppD", 3), ("swib", widx)],
                       writes=[("pptD",)])
                trk.ve(nc.vector.tensor_scalar, tD[:], tD[:], -MAGIC, 1.0, op0=OP.add, op1=OP.min,
                       reads=[("pptD",)], writes=[("pptD",)])
                dst = bass.AP(qD, 0, [[IFF, 128], [1, 128], [128, 64]])
                src = bass.AP(tD, 0, [[IFF, 128], [64, 128], [1, 64]])
                trk.ve(nc.vector.tensor_scalar, dst, src, -1.0, None, op0=OP.max,
                       reads=[("pptD",)], writes=[("ppqD",)])
                for grp in range(4):
                    gb = grp % 2
                    for k in range(16):
                        kk = grp * 16 + k
                        pb = k % 2
                        with trk.tg(reads=[("ppqD",), ("cIDT",)], writes=[("ppsT", pb)]) as t:
                            t.add(nc.tensor.transpose(ppsT[pb][:], qD[:, kk * 128:(kk + 1) * 128],
                                                      cIDT[:]))
                        trk.se(nc.scalar.activation, tT[gb][:, k * 128:(k + 1) * 128], ppsT[pb][:],
                               AF.Copy, reads=[("ppsT", pb)], writes=[("ppT", gb)])
                    half_write_tiles(
                        [(sh_out[(grp * 16 + k) * 128:(grp * 16 + k + 1) * 128, r0:r0 + 128],
                          sh_out[(grp * 16 + k) * 128:(grp * 16 + k + 1) * 128,
                                 n_rows + r0:n_rows + r0 + 128],
                          tT[gb][:, k * 128:(k + 1) * 128]) for k in range(16)],
                        reads=[("ppT", gb)])
            trk.join()
            stWd.close()
            set_flag(flg_q)
            poll_flag(flg_q, ("flg", flg_q))
            trk.lastw[shkey] = ("p", trk.cnt["p"])

        def pp_cast_attn():
            jobs = [(wqkv_raw, shWqkv, 1024, D), (wz_raw, shWz, 512, D),
                    (wout_raw, shWout, 1024, 1024)]
            for ji, (raw, sh_outt, n_rows, n_cols) in enumerate(jobs):
                nrr = n_rows if n_cols == 2048 else n_rows // 2
                for i in range(max(1, (n_rows * n_cols) // (128 * 2048))):
                    buf = i % 2
                    rt = raws[:, buf * 2048:(buf + 1) * 2048]
                    qt = qout[:, buf * 2048:(buf + 1) * 2048]
                    if n_cols == 2048:
                        src = raw[i * 128:(i + 1) * 128, :]
                    else:
                        src = raw.ap().rearrange("(a b) c -> a (b c)", b=2)[i * 128:(i + 1) * 128, :]
                    trk.dma("q0", rt, src, writes=[("ppraw", buf)])
                    trk.ve(nc.vector.tensor_copy, qt, rt, reads=[("ppraw", buf)], writes=[("ppq", buf)])
                    if n_cols == 2048:
                        de = sh_outt[i * 128:(i + 1) * 128, :]
                        do = sh_outt[n_rows + i * 128:n_rows + (i + 1) * 128, :]
                    else:
                        shv = sh_outt.ap().rearrange("(a b) c -> a (b c)", b=2)
                        de = shv[i * 128:(i + 1) * 128, :]
                        do = shv[(n_rows + i * 128) // 2 * 2 // 2:0] if False else shv[n_rows // 2 + i * 128:n_rows // 2 + (i + 1) * 128, :]
                    half_write(de, do, qt, reads=[("ppq", buf)])
            trk.dma("q0", raws[:, 0:2048].tensor[0:32, 0:2048] if False else raws[0:32, 0:2048],
                    wba_raw[:, :], writes=[("ppraw", 0)])
            trk.ve(nc.vector.tensor_copy, qout[0:32, 0:2048], raws[0:32, 0:2048],
                   reads=[("ppraw", 0)], writes=[("ppq", 0)])
            trk.pre("p", reads=[("ppq", 0)])
            g.dma_start(out=shWba[:, :], in_=qout[0:32, 0:2048]).then_inc(pollsem, 16)
            gwm()
            inst = g.engine_nop()
            trk.post("p", inst, reads=[("ppq", 0)])
            set_flag(6)
            poll_flag(6, ("flg", 6))
            for kk in (("shWqkvb",), ("shWzb",), ("shWoutb",), ("shWbab",)):
                trk.lastw[kk] = ("p", trk.cnt["p"])

        def prepass_all():
            pp_scale(0, wg_raw, IFF // 2, D, 0)
            pp_quant_gu(0, wg_raw, shWgT, IFF // 2, 1, ("shWgq",))
            pp_scale(1, wu_raw, IFF // 2, D, 2)
            pp_quant_gu(1, wu_raw, shWuT, IFF // 2, 3, ("shWuq",))
            pp_scale(2, wd_raw, D // 2, IFF, 4)
            pp_cast_attn()
            pp_quant_wd(2, wd_raw, shWdT, D // 2, 5, ("shWdq",))
            trk.join()
            stPP.close()
            trk.join()

        def stage_A(xsrc, src_key):
            stA = ExitStack()
            with stA:
                xt = stA.enter_context(nc.sbuf_tensor(un("A_xt"), [128, 2 * D], F32))
                sq = stA.enter_context(nc.sbuf_tensor(un("A_sq"), [128, D], F32))
                f1 = stA.enter_context(nc.sbuf_tensor(un("A_f1"), [128, D], F32))
                xn = stA.enter_context(nc.sbuf_tensor(un("A_xn"), [128, D], BF16))
                xnT = stA.enter_context(nc.sbuf_tensor(un("A_xnT"), [128, D], BF16))
                hh = stA.enter_context(nc.sbuf_tensor(un("A_h2048"), [128, 16 * D], BF16))
                lns = stA.enter_context(nc.sbuf_tensor(un("A_lns"), [128, 8], F32))
                psT = [stA.enter_context(nc.psum_tensor(un(f"A_psT{i}"), [128, 512], BF16)) for i in range(2)]
                psF = [stA.enter_context(nc.psum_tensor(un(f"A_psF{i}"), [128, 512], F32)) for i in range(3)]
                for k in range(16):
                    trk.dma("q1", hh[:, k * D:(k + 1) * D], h2048c[k * 128:(k + 1) * 128, :],
                            writes=[("hh", k)])
                rd_hh = [("hh", k) for k in range(16)]
                for tt in range(NTT):
                    xb = tt % 2
                    xv = xt[:, xb * D:(xb + 1) * D]
                    trk.dma("q0", xv, xsrc[tt * 128:(tt + 1) * 128, :],
                            reads=([(src_key, tt)] if src_key else []), writes=[("A_xt", xb)])
                    trk.ve(nc.vector.tensor_reduce, lns[:, 0:1], xv, axis=AX, op=OP.add,
                           reads=[("A_xt", xb)], writes=[("ln0",)])
                    trk.se(nc.scalar.activation, sq[:], xv, AF.Square,
                           reads=[("A_xt", xb)], writes=[("A_sq",)])
                    trk.ve(nc.vector.tensor_reduce, lns[:, 1:2], sq[:], axis=AX, op=OP.add,
                           reads=[("A_sq",)], writes=[("ln1",)])
                    trk.se(nc.scalar.activation, lns[:, 2:3], lns[:, 0:1], AF.Copy, scale=-1.0 / D,
                           reads=[("ln0",)], writes=[("negm",)])
                    trk.ve(nc.vector.tensor_scalar, lns[:, 3:4], lns[:, 1:2], 1.0 / D, None, op0=OP.mult,
                           reads=[("ln1",)], writes=[("ex2",)])
                    trk.ve(nc.vector.tensor_tensor, lns[:, 4:5], lns[:, 2:3], lns[:, 2:3], op=OP.mult,
                           reads=[("negm",)], writes=[("m2",)])
                    trk.ve(nc.vector.tensor_tensor, lns[:, 5:6], lns[:, 3:4], lns[:, 4:5], op=OP.subtract,
                           reads=[("ex2",), ("m2",)], writes=[("var",)])
                    trk.ve(nc.vector.tensor_scalar, lns[:, 5:6], lns[:, 5:6], 1e-5, None, op0=OP.add,
                           reads=[("var",)], writes=[("var",)])
                    trk.ve(nc.vector.reciprocal, lns[:, 6:7], lns[:, 5:6],
                           reads=[("var",)], writes=[("rv",)])
                    trk.se(nc.scalar.activation, lns[:, 7:8], lns[:, 6:7], AF.Sqrt,
                           reads=[("rv",)], writes=[("rstd",)])
                    trk.ve(nc.vector.tensor_scalar, xn[:], xv, lns[:, 2:3], lns[:, 7:8],
                           op0=OP.add, op1=OP.mult,
                           reads=[("A_xt", xb), ("negm",), ("rstd",)],
                           writes=[("A_xn",), ("A_i1",)])
                    for kb in range(4):
                        pb = kb % 2
                        with trk.tg(reads=[("A_xn",), ("cIDT",)], writes=[("A_psT", pb)]) as t:
                            for j in range(4):
                                k = kb * 4 + j
                                t.add(nc.tensor.transpose(psT[pb][:, j * 128:(j + 1) * 128],
                                                          xn[:, k * 128:(k + 1) * 128], cIDT[:]))
                        trk.ve(nc.vector.tensor_copy, xnT[:, kb * 512:(kb + 1) * 512], psT[pb][:],
                               reads=[("A_psT", pb)], writes=[("A_xnT", kb)])
                    rd_xnT = [("A_xnT", kb) for kb in range(4)]
                    for ncn in range(4):
                        nb3 = ncn % 3
                        with trk.tg(reads=rd_xnT + rd_hh, writes=[("A_psF", nb3)]) as t:
                            for k in range(16):
                                t.add(nc.tensor.matmul(psF[nb3][:],
                                                       xnT[:, k * 128:(k + 1) * 128],
                                                       hh[:, k * D + ncn * 512:k * D + (ncn + 1) * 512],
                                                       start=(k == 0), stop=(k == 15)))
                        trk.se(nc.scalar.activation, f1[:, ncn * 512:(ncn + 1) * 512], psF[nb3][:], AF.Copy,
                               reads=[("A_psF", nb3)], writes=[("A_f1", ncn)])
                    if _DBG and tt == 6:
                        trk.dma("q2", d_hh[:, 0:D], hh[:, 0:D], reads=[("hh", 0)])
                        trk.dma("q2", d_hh[:, D:2 * D], hh[:, 14 * D:15 * D], reads=[("hh", 14)])
                        trk.dma("q2", d_hh[:, 2 * D:3 * D], hh[:, 15 * D:16 * D], reads=[("hh", 15)])
                        trk.dma("q2", d_xn[:, :], xn[:], reads=[("A_xn",)])
                        trk.dma("q2", d_xnT[:, :], xnT[:], reads=[("A_xnT", kb2) for kb2 in range(4)])
                        trk.dma("q2", d_f1[:, :], f1[:], reads=[("A_f1", i2) for i2 in range(4)])
                    rd_f1 = [("A_f1", i) for i in range(4)]
                    trk.ve(nc.vector.tensor_reduce, lns[:, 0:1], f1[:], axis=AX, op=OP.max,
                           apply_absolute_value=True, reads=rd_f1, writes=[("mx",)])
                    trk.ve(nc.vector.tensor_scalar, lns[:, 1:2], lns[:, 0:1], C2048, 1e-5,
                           op0=OP.mult, op1=OP.max, reads=[("mx",)], writes=[("mxc",)])
                    trk.ve(nc.vector.reciprocal, lns[:, 2:3], lns[:, 1:2],
                           reads=[("mxc",)], writes=[("rec",)])
                    trk.ve(nc.vector.tensor_scalar, lns[:, 3:4], lns[:, 2:3], 127.0 * C2048, None,
                           op0=OP.mult, reads=[("rec",)], writes=[("seff",)])
                    trk.ve(nc.vector.tensor_scalar, inv1[:, tt:tt + 1], lns[:, 1:2], 1.0 / 127.0, None,
                           op0=OP.mult, reads=[("mxc",)], writes=[("inv1", tt)])
                    trk.se(nc.scalar.activation, f1[:], f1[:], AF.Copy,
                           scale=lns[:, 3:4], bias=MAGIC,
                           reads=rd_f1 + [("seff",)], writes=[("A_t1",)] + rd_f1)
                    trk.se(nc.scalar.activation, xn[:], f1[:], AF.Copy, bias=-MAGIC,
                           reads=[("A_t1",)], writes=[("A_i1",), ("A_xn",)])
                    for kb in range(4):
                        pb = kb % 2
                        with trk.tg(reads=[("A_i1",), ("cIDT",)], writes=[("A_psT", pb)]) as t:
                            for j in range(4):
                                k = kb * 4 + j
                                t.add(nc.tensor.transpose(psT[pb][:, j * 128:(j + 1) * 128],
                                                          xn[:, k * 128:(k + 1) * 128], cIDT[:]))
                        src = bass.AP(psT[pb], 0, [[512, 128], [128, 4], [1, 128]])
                        dst = bass.AP(i1Tbox["t"], (kb * 4) * 1024 + tt * 128,
                                      [[16 * 1024, 128], [1024, 4], [1, 128]])
                        trk.ve(nc.vector.tensor_copy, dst, src,
                               reads=[("A_psT", pb)],
                               writes=[("i1T", kb * 4 + j2) for j2 in range(4)])
            trk.join()

        def stage_BCD(th, xsrc, xdst, out_key, src_key):
            stH = ExitStack()
            with stH:
                i2T = stH.enter_context(nc.sbuf_tensor(un("C_i2T"), [128, 64 * 512], I8))
                stHh = ExitStack()
                h = [stHh.enter_context(nc.sbuf_tensor(un(f"B_h{m}"), [128, IFF], BF16)) for m in range(4)]
                stB = ExitStack()
                with stB:
                    wgS = [stB.enter_context(nc.sbuf_tensor(un(f"B_wg{i}"), [128, 16 * 512], BF16)) for i in range(2)]
                    wuS = [stB.enter_context(nc.sbuf_tensor(un(f"B_wu{i}"), [128, 16 * 512], BF16)) for i in range(2)]
                    gsc = stB.enter_context(nc.sbuf_tensor(un("B_gs"), [128, 512], F32))
                    usc = stB.enter_context(nc.sbuf_tensor(un("B_us"), [128, 512], F32))
                    psG = [stB.enter_context(nc.psum_tensor(un(f"B_psG{i}"), [128, 512], F32)) for i in range(2)]
                    psU = [stB.enter_context(nc.psum_tensor(un(f"B_psU{i}"), [128, 512], F32)) for i in range(2)]
                    rd_i1 = [("i1T", k) for k in range(16)]
                    for gg in range(16):
                        sbi = gg % 2
                        for k in range(16):
                            trk.dma("q0", wgS[sbi][:, k * 512:(k + 1) * 512],
                                    shWgT[k * 128:(k + 1) * 128, gg * 512:(gg + 1) * 512],
                                    reads=[("shWgq",)], writes=[("wgS", sbi, k)])
                            trk.dma("q0", wuS[sbi][:, k * 512:(k + 1) * 512],
                                    shWuT[k * 128:(k + 1) * 128, gg * 512:(gg + 1) * 512],
                                    reads=[("shWuq",)], writes=[("wuS", sbi, k)])
                        for m in range(4):
                            tt = th * 4 + m
                            pb = m % 2
                            rdg = [("wgS", sbi, k) for k in range(16)] + rd_i1
                            with trk.tg(reads=rdg, writes=[("psG", pb)]) as t:
                                for k in range(16):
                                    t.add(nc.tensor.matmul(psG[pb][:],
                                                           i1Tbox["t"][:, k * 1024 + tt * 128:k * 1024 + (tt + 1) * 128],
                                                           wgS[sbi][:, k * 512:(k + 1) * 512],
                                                           start=(k == 0), stop=(k == 15)))
                            rdu = [("wuS", sbi, k) for k in range(16)] + rd_i1
                            with trk.tg(reads=rdu, writes=[("psU", pb)]) as t:
                                for k in range(16):
                                    t.add(nc.tensor.matmul(psU[pb][:],
                                                           i1Tbox["t"][:, k * 1024 + tt * 128:k * 1024 + (tt + 1) * 128],
                                                           wuS[sbi][:, k * 512:(k + 1) * 512],
                                                           start=(k == 0), stop=(k == 15)))
                            if _DBG and th == 0 and gg == 0 and m == 0:
                                trk.dma("q2", d_wgs[:, :], wgS[0][:, 0:2048],
                                        reads=[("wgS", 0, kx) for kx in range(4)])
                                trk.ve(nc.vector.tensor_copy, gsc[:], psG[pb][:],
                                       reads=[("psG", pb)], writes=[("gs",)])
                                trk.dma("q2", d_pg[:, :], gsc[:], reads=[("gs",)])
                                trk.ve(nc.vector.tensor_copy, usc[:], psU[pb][:],
                                       reads=[("psU", pb)], writes=[("us",)])
                                trk.dma("q2", d_pu[:, :], usc[:], reads=[("us",)])
                                trk.dma("q2", d_cg[:, :], cgc[:, 0:8], reads=[("cg", tt)])
                            trk.se(nc.scalar.activation, gsc[:], psG[pb][:], AF.Silu,
                                   scale=cgc[:, tt:tt + 1],
                                   reads=[("psG", pb), ("cg", tt)], writes=[("gs",)])
                            trk.ve(nc.vector.tensor_scalar, usc[:], psU[pb][:], cuc[:, tt:tt + 1], None,
                                   op0=OP.mult, reads=[("psU", pb), ("cu", tt)], writes=[("us",)])
                            trk.ve(nc.vector.tensor_tensor, h[m][:, gg * 512:(gg + 1) * 512],
                                   gsc[:], usc[:], op=OP.mult,
                                   reads=[("gs",), ("us",)], writes=[("h", m, gg)])
                if _DBG and th == 0:
                    trk.dma("q2", d_h0[:, :], h[0][:, 0:1024], reads=[("h", 0, 0), ("h", 0, 1)])
                trk.join()
                stC = ExitStack()
                with stC:
                    h2 = stC.enter_context(nc.sbuf_tensor(un("C_h2"), [128, IFF], BF16))
                    t2s = stC.enter_context(nc.sbuf_tensor(un("C_t2s"), [128, IFF], F32))
                    sA = [stC.enter_context(nc.sbuf_tensor(un(f"C_sA{i}"), [128, 128], BF16)) for i in range(2)]
                    sB = [stC.enter_context(nc.sbuf_tensor(un(f"C_sB{i}"), [128, 128], BF16)) for i in range(2)]
                    lns = stC.enter_context(nc.sbuf_tensor(un("C_lns"), [128, 8], F32))
                    ps1 = [stC.enter_context(nc.psum_tensor(un(f"C_ps1{i}"), [128, 128], BF16)) for i in range(2)]
                    ps2 = [stC.enter_context(nc.psum_tensor(un(f"C_ps2{i}"), [128, 128], F32)) for i in range(2)]
                    ps3 = [stC.enter_context(nc.psum_tensor(un(f"C_ps3{i}"), [128, 256], BF16)) for i in range(2)]
                    for m in range(4):
                        tt = th * 4 + m
                        for q in range(64):
                            pb = q % 2
                            with trk.tg(reads=[("h", m, q // 4), ("cIDT",)], writes=[("ps1", pb)]) as t:
                                t.add(nc.tensor.transpose(ps1[pb][:], h[m][:, q * 128:(q + 1) * 128], cIDT[:]))
                            trk.ve(nc.vector.tensor_copy, sA[pb][:], ps1[pb][:],
                                   reads=[("ps1", pb)], writes=[("sA", pb)])
                            with trk.tg(reads=[("sA", pb), ("cB1",)], writes=[("ps2", pb)]) as t:
                                t.add(nc.tensor.matmul(ps2[pb][:], cB1[:], sA[pb][:], start=True, stop=True))
                            trk.se(nc.scalar.activation, sB[pb][:], ps2[pb][:], AF.Copy,
                                   reads=[("ps2", pb)], writes=[("sB", pb)])
                            with trk.tg(reads=[("sB", pb), ("cIDT",)], writes=[("ps3", pb)]) as t:
                                t.add(nc.tensor.transpose(ps3[pb][:, 0:128], sB[pb][:], cIDT[:]))
                            trk.ve(nc.vector.tensor_copy, h[m][:, q * 128:(q + 1) * 128], ps3[pb][:, 0:128],
                                   reads=[("ps3", pb)], writes=[("h", m, q // 4)])
                        rd_all_h = [("h", m, q) for q in range(16)]
                        for jp in range(64):
                            pb = jp % 2
                            src = bass.AP(h[m], jp, [[IFF, 128], [64, 128]])
                            with trk.tg(reads=rd_all_h + [("cIDT",)], writes=[("ps1", pb)]) as t:
                                t.add(nc.tensor.transpose(ps1[pb][:], src, cIDT[:]))
                            trk.ve(nc.vector.tensor_copy, sA[pb][:], ps1[pb][:],
                                   reads=[("ps1", pb)], writes=[("sA", pb)])
                            with trk.tg(reads=[("sA", pb), ("cH128",)], writes=[("ps2", pb)]) as t:
                                t.add(nc.tensor.matmul(ps2[pb][:], sA[pb][:], cH128[:], start=True, stop=True))
                            trk.se(nc.scalar.activation, h2[:, jp * 128:(jp + 1) * 128], ps2[pb][:], AF.Copy,
                                   reads=[("ps2", pb)], writes=[("h2", jp)])
                        rd_h2 = [("h2", j) for j in range(64)]
                        trk.ve(nc.vector.tensor_reduce, lns[:, 0:1], h2[:], axis=AX, op=OP.max,
                               apply_absolute_value=True, reads=rd_h2, writes=[("mx2",)])
                        trk.ve(nc.vector.tensor_scalar, lns[:, 1:2], lns[:, 0:1], C8192, 1e-5,
                               op0=OP.mult, op1=OP.max, reads=[("mx2",)], writes=[("mxc2",)])
                        trk.ve(nc.vector.reciprocal, lns[:, 2:3], lns[:, 1:2],
                               reads=[("mxc2",)], writes=[("rec2",)])
                        trk.ve(nc.vector.tensor_scalar, lns[:, 3:4], lns[:, 2:3], 127.0 * C8192, None,
                               op0=OP.mult, reads=[("rec2",)], writes=[("seff2",)])
                        trk.ve(nc.vector.tensor_scalar, inv2[:, tt:tt + 1], lns[:, 1:2], 1.0 / 127.0, None,
                               op0=OP.mult, reads=[("mxc2",)], writes=[("inv2", tt)])
                        trk.se(nc.scalar.activation, t2s[:], h2[:], AF.Copy,
                               scale=lns[:, 3:4], bias=MAGIC,
                               reads=rd_h2 + [("seff2",)], writes=[("t2s",)])
                        trk.se(nc.scalar.activation, h2[:], t2s[:], AF.Copy, bias=-MAGIC,
                               reads=[("t2s",)], writes=[("i2m",)] + rd_h2)
                        for k2 in range(32):
                            pb = k2 % 2
                            with trk.tg(reads=[("i2m",), ("cIDT",)], writes=[("ps3", pb)]) as t:
                                for j2 in range(2):
                                    k = k2 * 2 + j2
                                    t.add(nc.tensor.transpose(ps3[pb][:, j2 * 128:(j2 + 1) * 128],
                                                              h2[:, k * 128:(k + 1) * 128], cIDT[:]))
                            trk.ve(nc.vector.tensor_copy,
                                   bass.AP(i2T, (k2 * 2) * 512 + m * 128, [[64 * 512, 128], [512, 2], [1, 128]]),
                                   ps3[pb][:],
                                   reads=[("ps3", pb)], writes=[("i2T", k2 * 2, m), ("i2T", k2 * 2 + 1, m)])
                if _DBG and th == 0:
                    trk.dma("q2", d_h2[:, :], h2[:, 0:1024], reads=[("i2m",)])
                    trk.dma("q2", d_i2T[:, :], i2T[:, 0:512], reads=[("i2T", 0, mm2) for mm2 in range(4)])
                trk.join()
                stHh.close()
                stD = ExitStack()
                with stD:
                    wdS = [stD.enter_context(nc.sbuf_tensor(un(f"D_wd{i}"), [128, 16 * 1024], BF16)) for i in range(2)]
                    cch = [stD.enter_context(nc.sbuf_tensor(un(f"D_cc{i}"), [128, 16 * 128], BF16)) for i in range(2)]
                    xrs = [stD.enter_context(nc.sbuf_tensor(un(f"D_xr{i}"), [128, D], F32)) for i in range(4)]
                    xot = [stD.enter_context(nc.sbuf_tensor(un(f"D_xo{i}"), [128, D], F32)) for i in range(4)]
                    psD = [stD.enter_context(nc.psum_tensor(un(f"D_ps{i}"), [128, 1024], F32)) for i in range(4)]
                    for m in range(4):
                        tt = th * 4 + m
                        trk.ve(nc.vector.tensor_tensor, cdc[:, tt:tt + 1], inv2[:, tt:tt + 1],
                               swcol[:, 2:3], op=OP.mult,
                               reads=[("inv2", tt), ("swb", 2)], writes=[("cd0", tt)])
                        trk.ve(nc.vector.tensor_scalar, cdc[:, tt:tt + 1], cdc[:, tt:tt + 1], 0.5, None,
                               op0=OP.mult, reads=[("cd0", tt)], writes=[("cd", tt)])
                        trk.dma("q1", xrs[m][:], xsrc[tt * 128:(tt + 1) * 128, :],
                                reads=([(src_key, tt)] if src_key else []), writes=[("xr", m)])
                    for ogh in range(2):
                        for kq in range(4):
                            sbi = kq % 2
                            for k in range(16):
                                kk = kq * 16 + k
                                trk.dma("q0",
                                        wdS[sbi][:, k * 1024:(k + 1) * 1024],
                                        shWdT[kk * 128:(kk + 1) * 128, ogh * 1024:(ogh + 1) * 1024],
                                        reads=[("shWdq",)], writes=[("wdS", sbi, k)])
                            for m in range(4):
                                cb = cch[m % 2]
                                src = bass.AP(i2T, (kq * 16) * 512 + m * 128,
                                              [[64 * 512, 128], [512, 16], [1, 128]])
                                dst = bass.AP(cb, 0, [[16 * 128, 128], [128, 16], [1, 128]])
                                trk.ve(nc.vector.tensor_copy, dst, src,
                                       reads=[("i2T", kq * 16 + kx, m) for kx in range(16)],
                                       writes=[("cch", m % 2)])
                                rd = [("wdS", sbi, k) for k in range(16)] + [("cch", m % 2)]
                                with trk.tg(reads=rd, writes=[("psD", m)]) as t:
                                    for k in range(16):
                                        st0 = (kq == 0 and k == 0)
                                        sp0 = (kq == 3 and k == 15)
                                        t.add(nc.tensor.matmul(psD[m][:, 0:512],
                                                               cb[:, k * 128:(k + 1) * 128],
                                                               wdS[sbi][:, k * 1024:k * 1024 + 512],
                                                               start=st0, stop=sp0))
                                        t.add(nc.tensor.matmul(psD[m][:, 512:1024],
                                                               cb[:, k * 128:(k + 1) * 128],
                                                               wdS[sbi][:, k * 1024 + 512:(k + 1) * 1024],
                                                               start=st0, stop=sp0))
                        for m in range(4):
                            tt = th * 4 + m
                            trk.se(nc.scalar.activation,
                                   xot[m][:, ogh * 1024:(ogh + 1) * 1024],
                                   psD[m][:], AF.Copy, scale=cdc[:, tt:tt + 1],
                                   reads=[("psD", m), ("cd", tt)], writes=[("xot", m, ogh)])
                        for m in range(4):
                            trk.ve(nc.vector.tensor_tensor,
                                   xot[m][:, ogh * 1024:(ogh + 1) * 1024],
                                   xot[m][:, ogh * 1024:(ogh + 1) * 1024],
                                   xrs[m][:, ogh * 1024:(ogh + 1) * 1024], op=OP.add,
                                   reads=[("xot", m, ogh), ("xr", m)],
                                   writes=[("xot", m, ogh)])
                    for m in range(4):
                        tt = th * 4 + m
                        trk.dma("q2", xdst[tt * 128:(tt + 1) * 128, :], xot[m][:],
                                reads=[("xot", m, 0), ("xot", m, 1)], writes=[(out_key, tt)])
                trk.join()

        def ffn(xsrc, xdst, out_key, src_key, pre=False):
            if pre:
                stF0 = stI1
                i1Tbox["t"] = i1T_early
            else:
                stF0 = ExitStack()
                i1Tbox["t"] = stF0.enter_context(nc.sbuf_tensor(un("i1T"), [128, 16 * 1024], BF16))
            stage_A(xsrc, src_key)
            if pre:
                prepass_all()
            for tt in range(NTT):
                trk.ve(nc.vector.tensor_tensor, cgc[:, tt:tt + 1], inv1[:, tt:tt + 1],
                       swcol[:, 0:1], op=OP.mult,
                       reads=[("inv1", tt), ("swb", 0)], writes=[("cg", tt)])
                trk.ve(nc.vector.tensor_tensor, cuc[:, tt:tt + 1], inv1[:, tt:tt + 1],
                       swcol[:, 1:2], op=OP.mult,
                       reads=[("inv1", tt), ("swb", 1)], writes=[("cu", tt)])
            if _DBG and out_key == "XO" or _DBG and out_key == "X1":
                trk.dma("q2", d_i1T[:, :], i1T[:, 0:1024], reads=[("i1T", 0)])
                trk.dma("q2", d_inv1[:, :], inv1[:, 0:8], reads=[("inv1", tt2) for tt2 in range(8)])
            for th in range(2):
                stage_BCD(th, xsrc, xdst, out_key, src_key)
            trk.join()
            stF0.close()
            trk.join()



        def attention(xsrc, src_key, xdst, out_key):
            stAt = ExitStack()
            with stAt:
                A_ = lambda name, shape, dt=F32: stAt.enter_context(nc.sbuf_tensor(un(name), shape, dt))
                q_f64 = A_("at_qf64", [64, 8 * 1024], BF16)
                k_f64 = A_("at_kf64", [64, 8 * 1024], BF16)
                v_cm = A_("at_vcm", [64, 16 * 1024], BF16)
                k_cm = A_("at_kcm", [64, 16 * 512], BF16)
                g_fm = A_("at_g", [16, 1024], F32)
                b_fm = A_("at_b", [16, 1024], F32)
                gT2 = A_("at_gT2", [64, 256], F32)
                bT2 = A_("at_bT2", [64, 256], F32)
                cum2 = A_("at_cum2", [64, 256], F32)
                crev2 = A_("at_crev2", [64, 256], F32)
                gam2 = A_("at_gam2", [64, 256], F32)
                bg2 = A_("at_bg2", [64, 256], F32)
                ecc2 = A_("at_ecc2", [64, 256], F32)
                cumT2 = A_("at_cumT2", [128, 128], F32)
                mUI2 = A_("at_mUI", [64, 64], F32)
                mSL2 = A_("at_mSL", [64, 64], F32)
                bd2 = A_("at_bd2", [128, 2], F32)
                b2up = A_("at_b2up", [2, 128], F32)
                onesr = A_("at_ones", [1, 128], F32)
                l64u = A_("at_l64u", [64, 64], F32)
                l64s = A_("at_l64s", [64, 64], F32)
                idf = A_("at_idf", [128, 128], F32)
                dtb_sb = A_("at_dtb", [16, 1], F32)
                alog_sb = A_("at_alog", [16, 1], F32)
                negea = A_("at_negea", [16, 1], F32)
                nw_sb = A_("at_nw", [128, 1], F32)
                cwsb = A_("at_cw", [128, 64], F32)
                halo_sb = A_("at_halo", [128, 48], F32)
                halo_st = A_("at_halost", [128, 48], F32)
                cfl_sb = A_("at_cfl", [1, 1024], F32)
                eccr = A_("at_eccr", [1, 256], F32)

                scol = {}
                cnt0, cnt1 = 0, 0
                for h in range(16):
                    if (h // 2) % 2 == 0:
                        scol[h] = cnt0; cnt0 += 1
                    else:
                        scol[h] = cnt1; cnt1 += 1

                def bhof(h):
                    return ((h // 2) % 2) * 64

                attv = attc.ap().rearrange("a (r c) -> a r c", r=64)
                trk.dma("q1", mUI2[0:64, :], attv[3], writes=[("mui",)])
                trk.dma("q1", mSL2[0:64, :], attv[0], writes=[("msl",)])
                trk.dma("q1", bd2[:, :], bass.AP(attc, 4 * 4096, [[2, 128], [1, 2]]), writes=[("bd2",)])
                trk.dma("q1", b2up[:, :], bass.AP(attc, 5 * 4096, [[128, 2], [1, 128]]), writes=[("b2up",)])
                trk.dma("q1", l64u[:, :], attv[3], writes=[("l64u",)])
                trk.dma("q1", l64s[:, :], attv[0], writes=[("l64s",)])
                trk.ve(nc.vector.memset, onesr[:], 1.0, writes=[("onesr",)])
                trk.se(nc.scalar.activation, idf[:], cIDT[:], AF.Copy, reads=[("cIDT",)], writes=[("idf",)])
                trk.dma("q1", dtb_sb[:, :], bass.AP(smalls, 0, [[1, 16], [1, 1]]), writes=[("dtb",)])
                trk.dma("q1", alog_sb[:, :], bass.AP(smalls, 64, [[1, 16], [1, 1]]), writes=[("alog",)])
                trk.dma("q1", nw_sb[0:64, :], bass.AP(smalls, 128, [[1, 64], [1, 1]]), writes=[("nw",)])
                trk.dma("q1", nw_sb[64:128, :], bass.AP(smalls, 128, [[1, 64], [1, 1]]), writes=[("nw",)])
                for gi in range(16):
                    trk.dma("q1", cwsb[:, gi * 4:(gi + 1) * 4], convw[gi * 128:(gi + 1) * 128, :],
                            writes=[("cw", gi)])
                trk.pre("p", writes=[("halo_sb",)])
                g.memset(halo_sb[:], 0.0)
                with g.If_eq(r_par, 1):
                    g.reg_mov(r_loop, 1)
                    with g.While(r_loop):
                        g.dma_start(out=pfl[0:1, 7:8], in_=shFlg[0, 7:8]).then_inc(pollsem, 16)
                        g.reg_add(r_pw, r_pw, 16)
                        g.wait_ge(pollsem, r_pw)
                        g.reg_load(r_val, pfl[0:1, 7:8])
                        g.reg_alu(r_loop, r_val, FLAGV + 7, OP.bitwise_xor)
                    g.dma_start(out=halo_sb[:, :], in_=shHalo[:, :]).then_inc(pollsem, 16)
                    g.reg_add(r_pw, r_pw, 16)
                    g.wait_ge(pollsem, r_pw)
                inst = g.engine_nop()
                trk.post("p", inst, writes=[("halo_sb",)])

                stLN = ExitStack()
                ln2xT = stLN.enter_context(nc.sbuf_tensor(un("at_lnT"), [128, 16 * 1024], BF16))
                with ExitStack() as stL:
                    xt = stL.enter_context(nc.sbuf_tensor(un("at_xt"), [128, 2 * D], F32))
                    sq = stL.enter_context(nc.sbuf_tensor(un("at_sq"), [128, D], F32))
                    xn = stL.enter_context(nc.sbuf_tensor(un("at_xn"), [128, D], BF16))
                    lns = stL.enter_context(nc.sbuf_tensor(un("at_lns"), [128, 8], F32))
                    psT = [stL.enter_context(nc.psum_tensor(un("at_psT"), [128, 512], BF16)) for _ in range(2)]
                    for tt in range(NTT):
                        xb = tt % 2
                        xv = xt[:, xb * D:(xb + 1) * D]
                        trk.dma("q0", xv, xsrc[tt * 128:(tt + 1) * 128, :],
                                reads=([(src_key, tt)] if src_key else []), writes=[("at_xt", xb)])
                        trk.ve(nc.vector.tensor_reduce, lns[:, 0:1], xv, axis=AX, op=OP.add,
                               reads=[("at_xt", xb)], writes=[("aln0",)])
                        trk.se(nc.scalar.activation, sq[:], xv, AF.Square,
                               reads=[("at_xt", xb)], writes=[("at_sq",)])
                        trk.ve(nc.vector.tensor_reduce, lns[:, 1:2], sq[:], axis=AX, op=OP.add,
                               reads=[("at_sq",)], writes=[("aln1",)])
                        trk.se(nc.scalar.activation, lns[:, 2:3], lns[:, 0:1], AF.Copy, scale=-1.0 / D,
                               reads=[("aln0",)], writes=[("anegm",)])
                        trk.ve(nc.vector.tensor_scalar, lns[:, 3:4], lns[:, 1:2], 1.0 / D, None, op0=OP.mult,
                               reads=[("aln1",)], writes=[("aex2",)])
                        trk.ve(nc.vector.tensor_tensor, lns[:, 4:5], lns[:, 2:3], lns[:, 2:3], op=OP.mult,
                               reads=[("anegm",)], writes=[("am2",)])
                        trk.ve(nc.vector.tensor_tensor, lns[:, 5:6], lns[:, 3:4], lns[:, 4:5], op=OP.subtract,
                               reads=[("aex2",), ("am2",)], writes=[("avar",)])
                        trk.ve(nc.vector.tensor_scalar, lns[:, 5:6], lns[:, 5:6], 1e-5, None, op0=OP.add,
                               reads=[("avar",)], writes=[("avar",)])
                        trk.ve(nc.vector.reciprocal, lns[:, 6:7], lns[:, 5:6],
                               reads=[("avar",)], writes=[("arv",)])
                        trk.se(nc.scalar.activation, lns[:, 7:8], lns[:, 6:7], AF.Sqrt,
                               reads=[("arv",)], writes=[("arstd",)])
                        trk.ve(nc.vector.tensor_scalar, xn[:], xv, lns[:, 2:3], lns[:, 7:8],
                               op0=OP.add, op1=OP.mult,
                               reads=[("at_xt", xb), ("anegm",), ("arstd",)], writes=[("at_xn",)])
                        for kb in range(4):
                            pb = kb % 2
                            with trk.tg(reads=[("at_xn",), ("cIDT",)], writes=[("at_psT", pb)]) as t:
                                for j in range(4):
                                    k = kb * 4 + j
                                    t.add(nc.tensor.transpose(psT[pb][:, j * 128:(j + 1) * 128],
                                                              xn[:, k * 128:(k + 1) * 128], cIDT[:]))
                            src = bass.AP(psT[pb], 0, [[512, 128], [128, 4], [1, 128]])
                            dst = bass.AP(ln2xT, (kb * 4) * 1024 + tt * 128,
                                          [[16 * 1024, 128], [1024, 4], [1, 128]])
                            trk.ve(nc.vector.tensor_copy, dst, src,
                                   reads=[("at_psT", pb)],
                                   writes=[("lnT", kb * 4 + j2) for j2 in range(4)])
                trk.join()
                if _ACUT == 1:
                    return

                rd_lnT = [("lnT", k) for k in range(16)]
                with ExitStack() as stP:
                    qk_fm = stP.enter_context(nc.sbuf_tensor(un("at_qk"), [128, 8 * 1024], BF16))
                    wsl = [stP.enter_context(nc.sbuf_tensor(un("at_wsl"), [128, 16 * 256], BF16)) for _ in range(2)]
                    cin = stP.enter_context(nc.sbuf_tensor(un("at_cin"), [128, 1028], F32))
                    cvo = stP.enter_context(nc.sbuf_tensor(un("at_cvo"), [128, 1024], F32))
                    sil = stP.enter_context(nc.sbuf_tensor(un("at_sil"), [128, 1024], F32))
                    sq2 = stP.enter_context(nc.sbuf_tensor(un("at_sq2"), [128, 1024], F32))
                    nr2 = stP.enter_context(nc.sbuf_tensor(un("at_nr2"), [2, 1024], F32))
                    vt_s = stP.enter_context(nc.sbuf_tensor(un("at_vts"), [128, 1024], BF16))
                    zt_s = stP.enter_context(nc.sbuf_tensor(un("at_zts"), [128, 1024], BF16))
                    psP = [stP.enter_context(nc.psum_tensor(un("at_psP"), [128, 1024], F32)) for _ in range(2)]
                    psN = stP.enter_context(nc.psum_tensor(un("at_psN"), [2, 512], F32))
                    psB = stP.enter_context(nc.psum_tensor(un("at_psB"), [128, 512], F32))
                    psV = stP.enter_context(nc.psum_tensor(un("at_psV"), [128, 512], BF16))
                    groups = [("qkv", shWqkv, gi, ("shWqkvb",)) for gi in range(8)] + \
                             [("z", shWz, gi, ("shWzb",)) for gi in range(4)]
                    for grpidx, (kind, wsh, gi, wkey) in enumerate(groups):
                        rot = grpidx % 2
                        for k in range(16):
                            trk.dma("q0", wsl[rot][:, k * 256:(k + 1) * 256],
                                    wsh[gi * 256:(gi + 1) * 256, k * 128:(k + 1) * 128],
                                    reads=[wkey], writes=[("wsl", rot, k)], transpose=True)
                        for ct in range(2):
                            gti0 = gi * 2 + ct
                            pp = psP[(grpidx * 2 + ct) % 2]
                            ppk = ("psP", (grpidx * 2 + ct) % 2)
                            rdw = [("wsl", rot, k) for k in range(16)]
                            with trk.tg(reads=rdw + rd_lnT, writes=[ppk]) as t:
                                for k in range(16):
                                    for th2 in range(2):
                                        t.add(nc.tensor.matmul(
                                            pp[:, th2 * 512:(th2 + 1) * 512],
                                            wsl[rot][:, k * 256 + ct * 128:k * 256 + (ct + 1) * 128],
                                            ln2xT[:, k * 1024 + th2 * 512:k * 1024 + (th2 + 1) * 512],
                                            start=(k == 0), stop=(k == 15)))
                            if kind == "z":
                                zi = gi * 2 + ct
                                trk.se(nc.scalar.activation, zt_s[:], pp[:], AF.Silu,
                                       reads=[ppk], writes=[("zt",)])
                                trk.dma("q2", zsp[zi * 128:(zi + 1) * 128, :], zt_s[:],
                                        reads=[("zt",)], writes=[("zsp", zi)])
                                continue
                            gti = gti0
                            trk.se(nc.scalar.activation, cin[:, 3:1027], pp[:], AF.Copy,
                                   reads=[ppk], writes=[("cin",)])
                            trk.ve(nc.vector.tensor_copy, halo_st[:, gti * 3:(gti + 1) * 3],
                                   cin[:, 1024:1027], reads=[("cin",)], writes=[("hst", gti)])
                            trk.ve(nc.vector.tensor_copy, cin[:, 0:3], halo_sb[:, gti * 3:(gti + 1) * 3],
                                   reads=[("halo_sb",)], writes=[("cin",)])
                            trk.ve(nc.vector.tensor_scalar, cvo[:], cin[:, 0:1024],
                                   cwsb[:, gti * 4:gti * 4 + 1], None, op0=OP.mult,
                                   reads=[("cin",), ("cw", gti)], writes=[("cvo",)])
                            for tap in range(1, 4):
                                trk.ve(nc.vector.scalar_tensor_tensor, cvo[:], cin[:, tap:1024 + tap],
                                       cwsb[:, gti * 4 + tap:gti * 4 + tap + 1], cvo[:],
                                       op0=OP.mult, op1=OP.add,
                                       reads=[("cin",), ("cw", gti), ("cvo",)], writes=[("cvo",)])
                            trk.se(nc.scalar.activation, sil[:], cvo[:], AF.Silu,
                                   reads=[("cvo",)], writes=[("sil",)])
                            if gti < 8:
                                trk.se(nc.scalar.activation, sq2[:], sil[:], AF.Square,
                                       reads=[("sil",)], writes=[("sq2",)])
                                for th2 in range(2):
                                    with trk.tg(reads=[("sq2",), ("bd2",)], writes=[("psN",)]) as t:
                                        t.add(nc.tensor.matmul(psN[:, :], bd2[:],
                                                               sq2[:, th2 * 512:(th2 + 1) * 512],
                                                               start=True, stop=True))
                                    trk.ve(nc.vector.tensor_scalar, nr2[:, 0:512], psN[:], 1e-6, None,
                                           op0=OP.add, reads=[("psN",)], writes=[("nr2",)])
                                    trk.ve(nc.vector.reciprocal, nr2[:, 0:512], nr2[:, 0:512],
                                           reads=[("nr2",)], writes=[("nr2",)])
                                    scl = 0.015625 if gti < 4 else 1.0
                                    trk.se(nc.scalar.activation, nr2[:, 0:512], nr2[:, 0:512], AF.Sqrt,
                                           scale=scl, reads=[("nr2",)], writes=[("nr2",)])
                                    with trk.tg(reads=[("nr2",), ("b2up",)], writes=[("psB",)]) as t:
                                        t.add(nc.tensor.matmul(psB[:, :], b2up[:], nr2[:, 0:512],
                                                               start=True, stop=True))
                                    trk.ve(nc.vector.tensor_tensor,
                                           qk_fm[:, gti * 1024 + th2 * 512:gti * 1024 + (th2 + 1) * 512],
                                           sil[:, th2 * 512:(th2 + 1) * 512], psB[:], op=OP.mult,
                                           reads=[("sil",), ("psB",)], writes=[("qk", gti)])
                                dstf = q_f64 if gti < 4 else k_f64
                                dk0 = ("qf64",) if gti < 4 else ("kf64",)
                                hb2 = (gti % 4) * 2
                                trk.ve(nc.vector.tensor_copy, dstf[:, hb2 * 1024:(hb2 + 1) * 1024],
                                       qk_fm[0:64, gti * 1024:(gti + 1) * 1024],
                                       reads=[("qk", gti)], writes=[dk0])
                                for th2 in range(2):
                                    with trk.tg(reads=[("qk", gti), ("cIDT",)], writes=[("psB",)]) as t:
                                        t.add(nc.tensor.matmul(
                                            psB[0:64, :], cIDT[64:128, 64:128],
                                            bass.AP(qk_fm, 64 * 8192 + gti * 1024 + th2 * 512,
                                                    [[8192, 64], [1, 512]]),
                                            start=True, stop=True))
                                    trk.ve(nc.vector.tensor_copy,
                                           dstf[:, (hb2 + 1) * 1024 + th2 * 512:(hb2 + 1) * 1024 + (th2 + 1) * 512],
                                           psB[0:64, :], reads=[("psB",)], writes=[dk0])
                                if gti >= 4:
                                    kt = gti - 4
                                    for cc in range(16):
                                        with trk.tg(reads=[("qk", gti), ("cIDT",)], writes=[("psV",)]) as t:
                                            t.add(nc.tensor.transpose(
                                                psV[0:64, 0:128],
                                                qk_fm[:, gti * 1024 + cc * 64:gti * 1024 + (cc + 1) * 64],
                                                cIDT[:]))
                                        trk.ve(nc.vector.tensor_copy,
                                               k_cm[:, cc * 512 + kt * 128:cc * 512 + (kt + 1) * 128],
                                               psV[0:64, 0:128],
                                               reads=[("psV",)], writes=[("kcm", cc, kt)])
                            else:
                                vi = gti - 8
                                trk.ve(nc.vector.tensor_copy, vt_s[:], sil[:],
                                       reads=[("sil",)], writes=[("vts",)])
                                for cc in range(16):
                                    with trk.tg(reads=[("vts",), ("cIDT",)], writes=[("psV",)]) as t:
                                        t.add(nc.tensor.transpose(psV[0:64, 0:128],
                                                                  vt_s[:, cc * 64:(cc + 1) * 64], cIDT[:]))
                                    trk.ve(nc.vector.tensor_copy,
                                           v_cm[:, cc * 1024 + vi * 128:cc * 1024 + (vi + 1) * 128],
                                           psV[0:64, 0:128],
                                           reads=[("psV",)], writes=[("vcm", cc, vi)])
                    trk.pre("p", reads=[("hst", gtj) for gtj in range(16)])
                    with g.If_eq(r_par, 0):
                        g.dma_start(out=shHalo[:, :], in_=halo_st[:, :]).then_inc(pollsem, 16)
                        g.reg_add(r_pw, r_pw, 16)
                        g.wait_ge(pollsem, r_pw)
                        g.memset(ftile[0:1, 7:8], FLAGV + 7)
                        g.dma_start(out=shFlg[0, 7:8], in_=ftile[0:1, 7:8]).then_inc(pollsem, 16)
                        g.reg_add(r_pw, r_pw, 16)
                        g.wait_ge(pollsem, r_pw)
                    inst = g.engine_nop()
                    trk.post("p", inst, reads=[("hst", gtj) for gtj in range(16)])
                    for k in range(16):
                        trk.dma("q0", wsl[0][:, k * 256:k * 256 + 32],
                                shWba[0:32, k * 128:(k + 1) * 128],
                                reads=[("shWbab",)], writes=[("wsl", 0, k)], transpose=True)
                    with trk.tg(reads=[("wsl", 0, k) for k in range(16)] + rd_lnT, writes=[("psP", 0)]) as t:
                        for k in range(16):
                            for th2 in range(2):
                                t.add(nc.tensor.matmul(psP[0][0:16, th2 * 512:(th2 + 1) * 512],
                                                       wsl[0][:, k * 256:k * 256 + 16],
                                                       ln2xT[:, k * 1024 + th2 * 512:k * 1024 + (th2 + 1) * 512],
                                                       start=(k == 0), stop=(k == 15)))
                    with trk.tg(reads=[("wsl", 0, k) for k in range(16)] + rd_lnT, writes=[("psP", 1)]) as t:
                        for k in range(16):
                            for th2 in range(2):
                                t.add(nc.tensor.matmul(psP[1][0:16, th2 * 512:(th2 + 1) * 512],
                                                       wsl[0][:, k * 256 + 16:k * 256 + 32],
                                                       ln2xT[:, k * 1024 + th2 * 512:k * 1024 + (th2 + 1) * 512],
                                                       start=(k == 0), stop=(k == 15)))
                    trk.se(nc.scalar.activation, b_fm[:], psP[0][0:16, :], AF.Sigmoid,
                           reads=[("psP", 0)], writes=[("bfm",)])
                    trk.ve(nc.vector.tensor_scalar, g_fm[:], psP[1][0:16, :], dtb_sb[:, 0:1], None,
                           op0=OP.add, reads=[("psP", 1), ("dtb",)], writes=[("ga",)])
                    trk.se(nc.scalar.activation, sq2[0:16, :], g_fm[:], AF.Abs,
                           reads=[("ga",)], writes=[("gau",)])
                    trk.se(nc.scalar.activation, sq2[0:16, :], sq2[0:16, :], AF.Exp, scale=-1.0,
                           reads=[("gau",)], writes=[("gau",)])
                    trk.se(nc.scalar.activation, sq2[0:16, :], sq2[0:16, :], AF.Ln, bias=1.0,
                           reads=[("gau",)], writes=[("gau",)])
                    trk.ve(nc.vector.tensor_scalar, sil[0:16, :], g_fm[:], 0.0, None, op0=OP.max,
                           reads=[("ga",)], writes=[("gr",)])
                    trk.ve(nc.vector.tensor_tensor, g_fm[:], sil[0:16, :], sq2[0:16, :], op=OP.add,
                           reads=[("gr",), ("gau",)], writes=[("gfm",)])
                    trk.se(nc.scalar.activation, negea[:], alog_sb[:], AF.Exp,
                           reads=[("alog",)], writes=[("nea",)])
                    trk.ve(nc.vector.tensor_scalar, negea[:], negea[:], -1.0, None, op0=OP.mult,
                           reads=[("nea",)], writes=[("nea",)])
                    trk.ve(nc.vector.tensor_scalar, g_fm[:], g_fm[:], negea[:, 0:1], None, op0=OP.mult,
                           reads=[("gsp",), ("nea",)], writes=[("gfm",)])
                trk.join()
                stLN.close()
                trk.join()
                if _DBGA:
                    trk.dma("q2", d_lnT[:, :], ln2xT[:, 0:1024], reads=[("lnT", 0)])
                    trk.dma("q2", d_q[:, :], q_f64[0:64, :], reads=[("qf64",)])
                    trk.dma("q2", d_k[:, :], k_f64[0:64, :], reads=[("kf64",)])
                    trk.dma("q2", d_vcm[:, :], v_cm[0:64, :],
                            reads=[("vcm", cc2, vi2) for cc2 in range(16) for vi2 in range(8)])
                    trk.dma("q2", d_kcm[:, :], k_cm[0:64, :],
                            reads=[("kcm", cc2, kt2) for cc2 in range(16) for kt2 in range(4)])
                    trk.dma("q2", d_gb[:, 0:1024], g_fm[:, :], reads=[("gfm",)])
                    trk.dma("q2", d_gb[:, 1024:2048], b_fm[:, :], reads=[("bfm",)])
                    trk.dma("q2", d_z[:, :], zsp[0:128, :], reads=[("zsp", 0)])
                if _ACUT == 2:
                    return

                with ExitStack() as stQ:
                    psC = stQ.enter_context(nc.psum_tensor(un("at_psC"), [128, 256], F32))
                    psC2 = stQ.enter_context(nc.psum_tensor(un("at_psC2"), [128, 256], F32))
                    for c in range(16):
                        with trk.tg(reads=[("gfm",), ("idf",)], writes=[("psC",)]) as t:
                            t.add(nc.tensor.transpose(psC[0:64, c * 16:(c + 1) * 16],
                                                      g_fm[:, c * 64:(c + 1) * 64], idf[0:16, 0:16]))
                        with trk.tg(reads=[("bfm",), ("idf",)], writes=[("psC2",)]) as t:
                            t.add(nc.tensor.transpose(psC2[0:64, c * 16:(c + 1) * 16],
                                                      b_fm[:, c * 64:(c + 1) * 64], idf[0:16, 0:16]))
                    trk.ve(nc.vector.tensor_copy, gT2[:], psC[0:64, :], reads=[("psC",)], writes=[("gT2",)])
                    trk.ve(nc.vector.tensor_copy, bT2[:], psC2[0:64, :], reads=[("psC2",)], writes=[("bT2",)])
                    with trk.tg(reads=[("gT2",), ("l64u",)], writes=[("psC",)]) as t:
                        t.add(nc.tensor.matmul(psC[0:64, :], l64u[:],
                                               gT2[0:64, :], start=True, stop=True))
                    with trk.tg(reads=[("gT2",), ("l64s",)], writes=[("psC2",)]) as t:
                        t.add(nc.tensor.matmul(psC2[0:64, :], l64s[:],
                                               gT2[0:64, :], start=True, stop=True))
                    trk.ve(nc.vector.tensor_copy, cum2[:], psC[0:64, :],
                           reads=[("psC",)], writes=[("cum2",)])
                    trk.se(nc.scalar.activation, crev2[:], psC2[0:64, :], AF.Exp,
                           reads=[("psC2",)], writes=[("crev2",)])
                    trk.se(nc.scalar.activation, gam2[:], cum2[:], AF.Exp,
                           reads=[("cum2",)], writes=[("gam2",)])
                    trk.ve(nc.vector.tensor_tensor, bg2[:], bT2[:], gam2[:], op=OP.mult,
                           reads=[("bT2",), ("gam2",)], writes=[("bg2",)])
                    trk.dma("q2", scrd2[0:1, 0:256], cum2[63:64, :], reads=[("cum2",)], writes=[("scr2",)])
                    trk.dma("q2", eccr[:, :], scrd2[0:1, 0:256], reads=[("scr2",)], writes=[("eccr",)])
                    with trk.tg(reads=[("eccr",), ("onesr",)], writes=[("psC",)]) as t:
                        t.add(nc.tensor.matmul(psC[:, :], onesr[:], eccr[:], start=True, stop=True))
                    trk.se(nc.scalar.activation, ecc2[:], psC[0:64, :], AF.Exp,
                           reads=[("psC",)], writes=[("ecc2",)])
                    for half in range(2):
                        with trk.tg(reads=[("cum2",), ("idf",)], writes=[("psC2",)]) as t:
                            t.add(nc.tensor.transpose(psC2[:, half * 64:(half + 1) * 64],
                                                      cum2[0:64, half * 128:(half + 1) * 128],
                                                      idf[0:64, 0:64]))
                    trk.ve(nc.vector.tensor_copy, cumT2[:], psC2[:, 0:128],
                           reads=[("psC2",)], writes=[("cumT2",)])
                trk.join()
                if _DBGA:
                    trk.dma("q2", d_cum[:, 0:256], cum2[0:64, :], reads=[("cum2",)])
                    trk.dma("q2", d_cum[:, 256:512], crev2[0:64, :], reads=[("crev2",)])
                    trk.dma("q2", d_cum[:, 512:768], gam2[0:64, :], reads=[("gam2",)])
                    trk.dma("q2", d_cum[:, 768:1024], bg2[0:64, :], reads=[("bg2",)])
                    trk.dma("q2", d_cum[:, 1024:1280], ecc2[0:64, :], reads=[("ecc2",)])
                    trk.dma("q2", d_cumT[:, :], cumT2[:, :], reads=[("cumT2",)])
                if _ACUT == 3:
                    return

                y_fm = A_("at_y", [128, 8 * 1024], BF16)
                S_run = A_("at_Srun", [64, 16 * 64], F32)
                S_bf = A_("at_Sbf", [64, 16 * 64], BF16)
                ssc_bf = A_("at_ssc", [64, 16 * 64], BF16)
                trk.ve(nc.vector.memset, S_run[:], 0.0, writes=[("Srun",)])
                trk.pre("p", writes=[("Srun",)])
                with g.If_eq(r_par, 1):
                    g.reg_mov(r_loop, 1)
                    with g.While(r_loop):
                        g.dma_start(out=pfl[0:1, 8:9], in_=shFlg[0, 8:9]).then_inc(pollsem, 16)
                        g.reg_add(r_pw, r_pw, 16)
                        g.wait_ge(pollsem, r_pw)
                        g.reg_load(r_val, pfl[0:1, 8:9])
                        g.reg_alu(r_loop, r_val, FLAGV + 8, OP.bitwise_xor)
                    g.dma_start(out=S_run[0:64, :], in_=shS16[0:64, :]).then_inc(pollsem, 16)
                    g.reg_add(r_pw, r_pw, 16)
                    g.wait_ge(pollsem, r_pw)
                inst = g.engine_nop()
                trk.post("p", inst, writes=[("Srun",)])
                trk.ve(nc.vector.tensor_copy, S_bf[:], S_run[:], reads=[("Srun",)], writes=[("Sbf",)])

                if _ACUT == 4:
                    return
                with ExitStack() as stK:
                    K_ = lambda name, shape, dt=F32: stK.enter_context(nc.sbuf_tensor(un(name), shape, dt))
                    rX = [K_("at_rX", [128, 16 * 128], BF16) for _ in range(4)]
                    rP = [K_("at_rP", [128, 16 * 64], BF16) for _ in range(4)]
                    rQg = [K_("at_rQg", [128, 16 * 64], BF16) for _ in range(4)]
                    rKd = [K_("at_rKd", [128, 16 * 64], BF16) for _ in range(4)]
                    sc1 = K_("at_sc1", [128, 1024], F32)
                    cpR = K_("at_cpR", [64, 1024], F32)
                    ssc_f = K_("at_sscf", [64, 1024], F32)
                    e1_sb = K_("at_e1", [128, 1024], F32)
                    e2_sb = K_("at_e2", [128, 1024], F32)
                    gam_sb = K_("at_gamrep", [128, 1024], F32)
                    mkk_sb = K_("at_mkk", [128, 512], BF16)
                    mkq_sb = K_("at_mkq", [128, 512], BF16)
                    negA = K_("at_negA", [128, 16 * 64], BF16)
                    negAT = K_("at_negAT", [128, 16 * 64], BF16)
                    nbuf = K_("at_nb", [128, 16 * 64], BF16)
                    ntbuf = K_("at_ntb", [128, 16 * 64], BF16)
                    tkgT = K_("at_tkgT", [128, 16 * 64], BF16)
                    u_sb = K_("at_u", [128, 16 * 64], BF16)
                    psR = stK.enter_context(nc.psum_tensor(un("at_psR"), [128, 1024], F32))
                    psM = stK.enter_context(nc.psum_tensor(un("at_psM"), [128, 512], F32))
                    psM2 = stK.enter_context(nc.psum_tensor(un("at_psM2"), [128, 512], F32))
                    psS1 = stK.enter_context(nc.psum_tensor(un("at_psS1"), [128, 512], F32))
                    psS2 = stK.enter_context(nc.psum_tensor(un("at_psS2"), [128, 512], F32))
                    psX = [stK.enter_context(nc.psum_tensor(un("at_psX"), [128, 512], F32)) for _ in range(2)]
                    psS1b = psS1.bitcast(BF16)
                    psS2b = psS2.bitcast(BF16)
                    for c in range(_CCUT):
                        p = 0
                        tb = c // 2
                        rb = c % 4
                        chb = c * 16
                        half = c // 8
                        if _P1MASK & 1:
                            trk.dma("q2", scrd2[0:1, 256:1280],
                                    bass.AP(cumT2, ((c % 8) * 16) * 128 + half * 64, [[128, 16], [1, 64]]),
                                    reads=[("cumT2",)], writes=[("cflr",)])
                            trk.dma("q2", cfl_sb[:, :], scrd2[0:1, 256:1280],
                                    reads=[("cflr",)], writes=[("cfl",)])
                        if _P1MASK & 2:
                            with trk.tg(reads=[("cfl",), ("onesr",)], writes=[("psR",)]) as t:
                                for nn2 in range(2):
                                    t.add(nc.tensor.matmul(psR[:, nn2 * 512:(nn2 + 1) * 512], onesr[:],
                                                           cfl_sb[:, nn2 * 512:(nn2 + 1) * 512],
                                                           start=True, stop=True))
                        if _P1MASK & 8:
                            trk.ve(nc.vector.tensor_copy, cpR[:],
                                   bass.AP(psR, 0, [[1024, 64], [1, 1024]]),
                                   reads=[("psR",)], writes=[("cpR",)])
                            trk.se(nc.scalar.activation, gam_sb[0:64, :], cpR[:], AF.Exp,
                                   reads=[("cpR",)], writes=[("gamr",)])
                        edp = e2_sb[p:p + 64, :]
                        mui = bass.AP(mUI2, p * 64, [[64, 64], [0, 16], [1, 64]])
                        msl = bass.AP(mSL2, p * 64, [[64, 64], [0, 16], [1, 64]])
                        if _P1MASK & 4:
                            in1 = bass.AP(cum2, p * 256 + chb, [[256, 64], [1, 16], [0, 64]])
                            trk.ve(nc.vector.tensor_tensor, e2_sb[p:p + 64, :],
                                   cpR[:], in1, op=OP.subtract,
                                   reads=[("cpR",), ("cum2",)], writes=[("e2",)])
                        if _P1MASK & 16:
                            trk.ve(nc.vector.tensor_tensor, sc1[p:p + 64, :], edp, mui, op=OP.mult,
                                   reads=[("e2",), ("mui",)], writes=[("sc1",)])
                            trk.se(nc.scalar.activation, e1_sb[p:p + 64, :], sc1[p:p + 64, :], AF.Exp,
                                   reads=[("sc1",)], writes=[("e1",)])
                        if _P1MASK & 32:
                            trk.ve(nc.vector.scalar_tensor_tensor, sc1[p:p + 64, :], edp, -1.0, msl,
                                   op0=OP.mult, op1=OP.mult,
                                   reads=[("e2",), ("msl",)], writes=[("sc1",)])
                            trk.se(nc.scalar.activation, e2_sb[p:p + 64, :], sc1[p:p + 64, :], AF.Exp,
                                   reads=[("sc1",)], writes=[("e2",)])
                        if _KCUT < 2:
                            continue
                        with trk.tg(reads=[("kf64",)], writes=[("psM",)]) as t:
                            for kh in range(8):
                                ksl = k_f64[:, kh * 1024 + c * 64:kh * 1024 + (c + 1) * 64]
                                t.add(nc.tensor.matmul(psM[0:64, kh * 64:(kh + 1) * 64], ksl, ksl,
                                                      start=True, stop=True))
                        with trk.tg(reads=[("kf64",), ("qf64",)], writes=[("psM2",)]) as t:
                            for kh in range(8):
                                ksl = k_f64[:, kh * 1024 + c * 64:kh * 1024 + (c + 1) * 64]
                                qsl = q_f64[:, kh * 1024 + c * 64:kh * 1024 + (c + 1) * 64]
                                t.add(nc.tensor.matmul(psM2[0:64, kh * 64:(kh + 1) * 64], ksl, qsl,
                                                      start=True, stop=True))
                        trk.ve(nc.vector.tensor_copy, mkk_sb[p:p + 64, :], psM[p:p + 64, :],
                               reads=[("psM",)], writes=[("mkk",)])
                        trk.ve(nc.vector.tensor_copy, mkq_sb[p:p + 64, :], psM2[p:p + 64, :],
                               reads=[("psM2",)], writes=[("mkq",)])
                        mkqb = bass.AP(mkq_sb, 0, [[512, 64], [64, 8], [1, 64]])
                        trk.ve(nc.vector.tensor_tensor, sc1[p:p + 64, :], e1_sb[p:p + 64, :], mui, op=OP.mult,
                               reads=[("e1",), ("mui",)], writes=[("sc1",)])
                        for hl in range(2):
                            trk.ve(nc.vector.tensor_tensor,
                                   bass.AP(rP[rb], hl * 64, [[1024, 64], [128, 8], [1, 64]]),
                                   bass.AP(sc1, hl * 64, [[1024, 64], [128, 8], [1, 64]]), mkqb,
                                   op=OP.mult, reads=[("sc1",), ("mkq",)], writes=[("rP", rb)])
                        btb = bass.AP(bT2, p * 256 + chb, [[256, 64], [1, 16], [0, 64]])
                        trk.ve(nc.vector.tensor_tensor, sc1[p:p + 64, :], e2_sb[p:p + 64, :], btb, op=OP.mult,
                               reads=[("e2",), ("bT2",)], writes=[("sc1",)])
                        trk.ve(nc.vector.tensor_tensor, sc1[p:p + 64, :], sc1[p:p + 64, :], msl, op=OP.mult,
                               reads=[("sc1",), ("msl",)], writes=[("sc1",)])
                        mkkb = bass.AP(mkk_sb, 0, [[512, 64], [64, 8], [1, 64]])
                        for hl in range(2):
                            trk.ve(nc.vector.scalar_tensor_tensor,
                                   bass.AP(negA, hl * 64, [[1024, 64], [128, 8], [1, 64]]),
                                   bass.AP(sc1, hl * 64, [[1024, 64], [128, 8], [1, 64]]),
                                   -1.0, mkkb, op0=OP.mult, op1=OP.mult,
                                   reads=[("sc1",), ("mkk",)], writes=[("negAk",)])
                        with trk.tg(reads=[("negAk",), ("cIDT",)], writes=[("psS1",)]) as t:
                            for h in range(16):
                                t.add(nc.tensor.transpose(psS1b[p:p + 64, h * 64:(h + 1) * 64],
                                                          negA[p:p + 64, h * 64:(h + 1) * 64],
                                                          cIDT[p:p + 64, p:p + 64]))
                        trk.ve(nc.vector.tensor_copy, negAT[p:p + 64, :], psS1b[p:p + 64, :],
                               reads=[("psS1",)], writes=[("negATk",)])
                        vsl = bass.AP(v_cm, c * 1024, [[16 * 1024, 64], [1, 1024]])
                        btc = bass.AP(bT2, p * 256 + chb, [[256, 64], [1, 16], [0, 64]])
                        trk.ve(nc.vector.tensor_tensor,
                               bass.AP(rX[rb], p * 2048, [[2048, 64], [128, 16], [1, 64]]),
                               vsl, btc, op=OP.mult,
                               reads=[("vcm", c, vi2) for vi2 in range(8)] + [("bT2",)],
                               writes=[("rX", rb)])
                        ksl2 = bass.AP(k_cm, c * 512, [[16 * 512, 64], [64, 8], [1, 64]])
                        for hl in range(2):
                            trk.ve(nc.vector.tensor_tensor,
                                   bass.AP(rX[rb], 64 + hl * 128, [[2048, 64], [256, 8], [1, 64]]),
                                   ksl2, bass.AP(bg2, chb + hl, [[256, 64], [2, 8], [0, 64]]),
                                   op=OP.mult,
                                   reads=[("kcm", c, kt2) for kt2 in range(4)] + [("bg2",)],
                                   writes=[("rX", rb)])
                            trk.ve(nc.vector.tensor_tensor,
                                   bass.AP(rKd[rb], hl * 64, [[1024, 64], [128, 8], [1, 64]]),
                                   ksl2, bass.AP(crev2, chb + hl, [[256, 64], [2, 8], [0, 64]]),
                                   op=OP.mult,
                                   reads=[("kcm", c, kt2) for kt2 in range(4)] + [("crev2",)],
                                   writes=[("rKd", rb)])
                            trk.ve(nc.vector.tensor_tensor,
                                   bass.AP(rQg[rb], hl * 64, [[1024, 64], [128, 8], [1, 64]]),
                                   bass.AP(q_f64, c * 64, [[8192, 64], [1024, 8], [1, 64]]),
                                   bass.AP(gam_sb, hl * 64, [[1024, 64], [128, 8], [1, 64]]),
                                   op=OP.mult,
                                   reads=[("qf64",), ("gamr",)], writes=[("rQg", rb)])
                        if _KCUT < 3:
                            continue
                        curN, curNT = negA, negAT
                        ckN, ckNT = ("negAk",), ("negATk",)
                        for lvl in range(6):
                            for wave in range(2):
                                for hh in range(8):
                                    h = wave * 8 + hh
                                    bank = hh // 4
                                    win = (hh % 4) * 128
                                    with trk.tg(reads=[ckN, ckNT, ("rX", rb)],
                                                writes=[("psX", bank)]) as t:
                                        t.add(nc.tensor.matmul(psX[bank][p:p + 64, win:win + 128],
                                                               curNT[p:p + 64, h * 64:(h + 1) * 64],
                                                               rX[rb][p:p + 64, h * 128:(h + 1) * 128],
                                                               start=True, stop=True))
                                for bank in range(2):
                                    trk.ve(nc.vector.tensor_tensor,
                                           rX[rb][p:p + 64, wave * 1024 + bank * 512:wave * 1024 + (bank + 1) * 512],
                                           psX[bank][p:p + 64, :],
                                           rX[rb][p:p + 64, wave * 1024 + bank * 512:wave * 1024 + (bank + 1) * 512],
                                           op=OP.add,
                                           reads=[("psX", bank), ("rX", rb)], writes=[("rX", rb)])
                            if lvl < 5:
                                if lvl % 2 == 0:
                                    nn, nt, knn, knt = nbuf, ntbuf, ("nb", 0), ("ntb", 0)
                                else:
                                    nn, nt, knn, knt = negA, negAT, ("negAk",), ("negATk",)
                                with trk.tg(reads=[ckN, ckNT], writes=[("psS1",)]) as t:
                                    for h in range(8):
                                        t.add(nc.tensor.matmul(psS1[p:p + 64, h * 64:(h + 1) * 64],
                                                               curNT[p:p + 64, h * 64:(h + 1) * 64],
                                                               curN[p:p + 64, h * 64:(h + 1) * 64],
                                                               start=True, stop=True))
                                with trk.tg(reads=[ckN, ckNT], writes=[("psS2",)]) as t:
                                    for h in range(8, 16):
                                        t.add(nc.tensor.matmul(psS2[p:p + 64, (h - 8) * 64:(h - 7) * 64],
                                                               curNT[p:p + 64, h * 64:(h + 1) * 64],
                                                               curN[p:p + 64, h * 64:(h + 1) * 64],
                                                               start=True, stop=True))
                                with trk.tg(reads=[ckN, ckNT], writes=[("psM",)]) as t:
                                    for h in range(8):
                                        t.add(nc.tensor.matmul(psM[p:p + 64, h * 64:(h + 1) * 64],
                                                               curN[p:p + 64, h * 64:(h + 1) * 64],
                                                               curNT[p:p + 64, h * 64:(h + 1) * 64],
                                                               start=True, stop=True))
                                with trk.tg(reads=[ckN, ckNT], writes=[("psM2",)]) as t:
                                    for h in range(8, 16):
                                        t.add(nc.tensor.matmul(psM2[p:p + 64, (h - 8) * 64:(h - 7) * 64],
                                                               curN[p:p + 64, h * 64:(h + 1) * 64],
                                                               curNT[p:p + 64, h * 64:(h + 1) * 64],
                                                               start=True, stop=True))
                                trk.ve(nc.vector.tensor_copy, nn[p:p + 64, 0:512], psS1[p:p + 64, :],
                                       reads=[("psS1",)], writes=[knn])
                                trk.ve(nc.vector.tensor_copy, nn[p:p + 64, 512:1024], psS2[p:p + 64, :],
                                       reads=[("psS2",)], writes=[knn])
                                trk.ve(nc.vector.tensor_copy, nt[p:p + 64, 0:512], psM[p:p + 64, :],
                                       reads=[("psM",)], writes=[knt])
                                trk.ve(nc.vector.tensor_copy, nt[p:p + 64, 512:1024], psM2[p:p + 64, :],
                                       reads=[("psM2",)], writes=[knt])
                                curN, curNT = nn, nt
                                ckN, ckNT = knn, knt
                        if _KCUT < 4:
                            continue
                        with trk.tg(reads=[("rX", rb), ("cIDT",)], writes=[("psS1",)]) as t:
                            for h in range(16):
                                t.add(nc.tensor.transpose(
                                    psS1b[0:64, h * 64:(h + 1) * 64],
                                    rX[rb][0:64, h * 128 + 64:(h + 1) * 128],
                                    cIDT[0:64, 0:64]))
                        trk.se(nc.scalar.activation, tkgT[0:64, :], psS1b[0:64, :], AF.Copy, scale=-1.0,
                               reads=[("psS1",)], writes=[("tkgTk",)])
                        for wave in range(2):
                            for hh in range(8):
                                h = wave * 8 + hh
                                bh = bhof(h)
                                bank = hh // 4
                                win = (hh % 4) * 64
                                with trk.tg(reads=[("tkgTk",), ("Sbf",)],
                                            writes=[("psX", bank)]) as t:
                                    t.add(nc.tensor.matmul(psX[bank][0:64, win:win + 64],
                                                           tkgT[0:64, h * 64:(h + 1) * 64],
                                                           S_bf[0:64, h * 64:(h + 1) * 64],
                                                           start=True, stop=True))
                            for bank in range(2):
                                h0 = wave * 8 + bank * 4
                                trk.ve(nc.vector.tensor_tensor,
                                       bass.AP(u_sb, (wave * 512 + bank * 256),
                                               [[1024, 64], [64, 4], [1, 64]]),
                                       bass.AP(psX[bank], 0, [[512, 64], [64, 4], [1, 64]]),
                                       bass.AP(rX[rb], h0 * 128, [[2048, 64], [128, 4], [1, 64]]),
                                       op=OP.add,
                                       reads=[("psX", bank), ("rX", rb)], writes=[("u",)])
                        with trk.tg(reads=[("Sbf",), ("rQg", rb), ("u",), ("rP", rb)], writes=[("psS1",)]) as t:
                            for h in range(0, 16, 2):
                                win = (h // 2) * 64
                                t.add(nc.tensor.matmul(psS1[0:64, win:win + 64],
                                                       S_bf[0:64, h * 64:(h + 1) * 64],
                                                       rQg[rb][0:64, h * 64:(h + 1) * 64],
                                                       start=True, stop=False))
                                t.add(nc.tensor.matmul(psS1[0:64, win:win + 64],
                                                       u_sb[0:64, h * 64:(h + 1) * 64],
                                                       rP[rb][0:64, h * 64:(h + 1) * 64],
                                                       start=False, stop=True))
                        with trk.tg(reads=[("Sbf",), ("rQg", rb), ("u",), ("rP", rb)], writes=[("psS2",)]) as t:
                            for h in range(1, 16, 2):
                                win = (h // 2) * 64
                                t.add(nc.tensor.matmul(psS2[64:128, win:win + 64],
                                                       S_bf[0:64, h * 64:(h + 1) * 64],
                                                       rQg[rb][0:64, h * 64:(h + 1) * 64],
                                                       start=True, stop=False))
                                t.add(nc.tensor.matmul(psS2[64:128, win:win + 64],
                                                       u_sb[0:64, h * 64:(h + 1) * 64],
                                                       rP[rb][0:64, h * 64:(h + 1) * 64],
                                                       start=False, stop=True))
                        src0 = bass.AP(psS1, 0, [[512, 64], [64, 8], [1, 64]])
                        dst0 = bass.AP(y_fm, c * 64, [[8192, 64], [1024, 8], [1, 64]])
                        trk.ve(nc.vector.tensor_copy, dst0, src0,
                               reads=[("psS1",)], writes=[("y", c, 0)])
                        src1 = bass.AP(psS2, 64 * 512, [[512, 64], [64, 8], [1, 64]])
                        dst1 = bass.AP(y_fm, 64 * 8192 + c * 64, [[8192, 64], [1024, 8], [1, 64]])
                        trk.ve(nc.vector.tensor_copy, dst1, src1,
                               reads=[("psS2",)], writes=[("y", c, 1)])
                        if _KCUT < 5:
                            continue
                        trk.ve(nc.vector.tensor_tensor, ssc_f[0:64, :], S_run[0:64, :],
                               bass.AP(ecc2, chb, [[256, 64], [1, 16], [0, 64]]), op=OP.mult,
                               reads=[("Srun",), ("ecc2",)], writes=[("ssc",)])
                        with trk.tg(reads=[("rKd", rb), ("u",)], writes=[("psM",)]) as t:
                            for h in range(8):
                                win = h * 64
                                t.add(nc.tensor.matmul(psM[0:64, win:win + 64],
                                                       rKd[rb][0:64, h * 64:(h + 1) * 64],
                                                       u_sb[0:64, h * 64:(h + 1) * 64],
                                                       start=True, stop=True))
                        with trk.tg(reads=[("rKd", rb), ("u",)], writes=[("psM2",)]) as t:
                            for h in range(8, 16):
                                win = h * 64
                                t.add(nc.tensor.matmul(psM2[0:64, win - 512:win - 512 + 64],
                                                       rKd[rb][0:64, h * 64:(h + 1) * 64],
                                                       u_sb[0:64, h * 64:(h + 1) * 64],
                                                       start=True, stop=True))
                        trk.ve(nc.vector.tensor_tensor, S_run[0:64, 0:512], psM[0:64, :],
                               ssc_f[0:64, 0:512], op=OP.add,
                               reads=[("psM",), ("ssc",)], writes=[("Srun",)])
                        trk.ve(nc.vector.tensor_tensor, S_run[0:64, 512:1024], psM2[0:64, :],
                               ssc_f[0:64, 512:1024], op=OP.add,
                               reads=[("psM2",), ("ssc",)], writes=[("Srun",)])
                        trk.ve(nc.vector.tensor_copy, S_bf[0:64, :], S_run[0:64, :],
                               reads=[("Srun",)], writes=[("Sbf",)])
                        if _DBGA:
                            trk.dma("q2", d_cU[:, c * 1024:(c + 1) * 1024], u_sb[0:64, :], reads=[("u",)])
                            trk.dma("q2", d_cS2[:, c * 1024:(c + 1) * 1024], S_run[0:64, :], reads=[("Srun",)])
                            trk.dma("q2", d_cE1[:, c * 1024:(c + 1) * 1024], e1_sb[0:64, :], reads=[("e1",)])
                            trk.dma("q2", d_cE2[:, c * 1024:(c + 1) * 1024], e2_sb[0:64, :], reads=[("e2",)])
                            trk.dma("q2", d_cP[:, c * 1024:(c + 1) * 1024], rP[rb][0:64, :], reads=[("rP", rb)])
                            trk.dma("q2", d_cQ[:, c * 1024:(c + 1) * 1024], rQg[rb][0:64, :], reads=[("rQg", rb)])
                        if _DBGA and c in (0, 1):
                            d_cx = d_c0 if c == 0 else d_c1
                            trk.dma("q2", d_cx[:, 0:1024], u_sb[:, :], reads=[("u",)])
                            trk.dma("q2", d_cx[:, 1024:2048], tkgT[:, :], reads=[("tkgTk",)])
                            trk.dma("q2", d_cx[:, 2048:3072], negA[:, :], reads=[("negAk",)])
                            trk.dma("q2", d_cx[:, 3072:4096], negAT[:, :], reads=[("negATk",)])
                            trk.dma("q2", d_cx[:, 4096:5120], rKd[rb][:, :], reads=[("rKd", rb)])
                            trk.dma("q2", d_cx[:, 5120:6144], rQg[rb][:, :], reads=[("rQg", rb)])
                            trk.dma("q2", d_cx[:, 6144:7168], rP[rb][:, :], reads=[("rP", rb)])
                            trk.dma("q2", d_cx[0:64, 7168:8192], S_run[0:64, :], reads=[("Srun",)])
                    trk.pre("p", reads=[("Srun",)])
                    with g.If_eq(r_par, 0):
                        g.dma_start(out=shS16[0:64, :], in_=S_run[0:64, :]).then_inc(pollsem, 16)
                        g.reg_add(r_pw, r_pw, 16)
                        g.wait_ge(pollsem, r_pw)
                        g.memset(ftile[0:1, 8:9], FLAGV + 8)
                        g.dma_start(out=shFlg[0, 8:9], in_=ftile[0:1, 8:9]).then_inc(pollsem, 16)
                        g.reg_add(r_pw, r_pw, 16)
                        g.wait_ge(pollsem, r_pw)
                    inst = g.engine_nop()
                    trk.post("p", inst, reads=[("Srun",)])
                trk.join()
                if _DBGA:
                    trk.dma("q2", d_y[:, :], y_fm[:, :],
                            reads=[("y", c2, ph2) for c2 in range(16) for ph2 in range(2)])
                    trk.dma("q2", d_S[:, :], S_run[0:64, :], reads=[("Srun",)])
                if _ACUT == 5:
                    return

                with ExitStack() as stF:
                    wos = stF.enter_context(nc.sbuf_tensor(un("at_wos"), [128, 8 * 2048], BF16))
                    yq = stF.enter_context(nc.sbuf_tensor(un("at_yq"), [128, 8 * 1024], BF16))
                    sqf = stF.enter_context(nc.sbuf_tensor(un("at_sqf"), [128, 1024], F32))
                    nrf = stF.enter_context(nc.sbuf_tensor(un("at_nrf"), [2, 512], F32))
                    t1f = stF.enter_context(nc.sbuf_tensor(un("at_t1f"), [128, 512], F32))
                    zr = stF.enter_context(nc.sbuf_tensor(un("at_zr"), [128, 1024], BF16))
                    xrf = stF.enter_context(nc.sbuf_tensor(un("at_xrf"), [128, D], F32))
                    xof = stF.enter_context(nc.sbuf_tensor(un("at_xof"), [128, D], F32))
                    psF2 = stF.enter_context(nc.psum_tensor(un("at_psF2"), [2, 512], F32))
                    psB2 = stF.enter_context(nc.psum_tensor(un("at_psB2"), [128, 512], F32))
                    psW = [stF.enter_context(nc.psum_tensor(un("at_psW"), [128, 1024], F32)) for _ in range(2)]
                    rd_y = [("y", c2, ph2) for c2 in range(16) for ph2 in range(2)]
                    for yt in range(8):
                        ysl = y_fm[:, yt * 1024:(yt + 1) * 1024]
                        trk.se(nc.scalar.activation, sqf[:], ysl, AF.Square,
                               reads=rd_y, writes=[("sqf",)])
                        trk.dma("q1", zr[:, :], zsp[yt * 128:(yt + 1) * 128, :],
                                reads=[("zsp", yt)], writes=[("zr",)])
                        for th2 in range(2):
                            with trk.tg(reads=[("sqf",), ("bd2",)], writes=[("psF2",)]) as t:
                                t.add(nc.tensor.matmul(psF2[:, :], bd2[:],
                                                       sqf[:, th2 * 512:(th2 + 1) * 512],
                                                       start=True, stop=True))
                            trk.ve(nc.vector.tensor_scalar, nrf[:, :], psF2[:, :], 1.0 / 64.0, 1e-6,
                                   op0=OP.mult, op1=OP.add, reads=[("psF2",)], writes=[("nrf",)])
                            trk.ve(nc.vector.reciprocal, nrf[:, :], nrf[:, :],
                                   reads=[("nrf",)], writes=[("nrf",)])
                            trk.se(nc.scalar.activation, nrf[:, :], nrf[:, :], AF.Sqrt,
                                   reads=[("nrf",)], writes=[("nrf",)])
                            with trk.tg(reads=[("nrf",), ("b2up",)], writes=[("psB2",)]) as t:
                                t.add(nc.tensor.matmul(psB2[:, :], b2up[:], nrf[:, :],
                                                       start=True, stop=True))
                            trk.ve(nc.vector.tensor_tensor, t1f[:, :],
                                   ysl[:, th2 * 512:(th2 + 1) * 512] if False else y_fm[:, yt * 1024 + th2 * 512:yt * 1024 + (th2 + 1) * 512],
                                   psB2[:, :], op=OP.mult,
                                   reads=rd_y + [("psB2",)], writes=[("t1f",)])
                            trk.ve(nc.vector.tensor_scalar, t1f[:, :], t1f[:, :], nw_sb[:, 0:1], None,
                                   op0=OP.mult, reads=[("t1f",), ("nw",)], writes=[("t1f",)])
                            trk.ve(nc.vector.tensor_tensor,
                                   yq[:, yt * 1024 + th2 * 512:yt * 1024 + (th2 + 1) * 512],
                                   t1f[:, :], zr[:, th2 * 512:(th2 + 1) * 512],
                                   op=OP.mult, reads=[("t1f",), ("zr",)], writes=[("yqk", yt)])
                    for k in range(8):
                        trk.dma("q0", wos[:, k * 2048:(k + 1) * 2048],
                                shWout[0:2048, k * 128:(k + 1) * 128],
                                reads=[("shWoutb",)], writes=[("wos", k)], transpose=True)
                    for m in range(8):
                        pw = psW[m % 2]
                        pwk = ("psW", m % 2)
                        rdw = [("wos", k2) for k2 in range(8)] + [("yqk", y2) for y2 in range(8)]
                        trk.dma("q1", xrf[:, :], xsrc[m * 128:(m + 1) * 128, :],
                                reads=([(src_key, m)] if src_key else []), writes=[("xrf",)])
                        for half in range(2):
                            with trk.tg(reads=rdw, writes=[pwk]) as t:
                                for k in range(8):
                                    for nn2 in range(2):
                                        t.add(nc.tensor.matmul(
                                            pw[:, nn2 * 512:(nn2 + 1) * 512],
                                            yq[:, k * 1024 + m * 128:k * 1024 + (m + 1) * 128],
                                            wos[:, k * 2048 + half * 1024 + nn2 * 512:k * 2048 + half * 1024 + (nn2 + 1) * 512],
                                            start=(k == 0), stop=(k == 7)))
                            trk.ve(nc.vector.tensor_tensor, xof[:, half * 1024:(half + 1) * 1024],
                                   pw[:], xrf[:, half * 1024:(half + 1) * 1024], op=OP.add,
                                   reads=[pwk, ("xrf",)], writes=[("xofk", half)])
                        trk.dma("q2", xdst[m * 128:(m + 1) * 128, :], xof[:],
                                reads=[("xofk", 0), ("xofk", 1)], writes=[(out_key, m)])
                trk.join()

        if _DBG:
            trk.dma("q2", d_swcol[:, :], swcol[:, 0:8], reads=[("swb", 0), ("swb", 1), ("swb", 2)])
            trk.dma("q2", d_wg[:, :], shWg[0:128, :], reads=[("shWgq",)])
            trk.dma("q2", d_wd[:, :], shWd[0:128, 0:1024], reads=[("shWdq",)])
        if phases == "ffn1":
            ffn(xs, xo, "XO", None, pre=True)
        elif phases == "cast":
            ffn(xs, xo, "XO", None, pre=True)
        elif phases == "attn":
            ffn(xs, xsp1, "X1", None, pre=True)
            attention(xsp1, "X1", xo, "XO")
        elif phases == "attnonly":
            prepass_all()
            attention(xsp1, None, xo, "XO")
        elif phases == "noattn":
            ffn(xs, xsp1, "X1", None, pre=True)
            ffn(xsp1, xo, "XO", "X1")
        else:
            ffn(xs, xsp1, "X1", None, pre=True)
            attention(xsp1, "X1", xsp2, "X2")
            ffn(xsp2, xo, "XO", "X2")
        trk.wait_all_on("p")

    return nc


def kernel(L0_attn_x, L0_attn_W_qkv, L0_attn_W_z, L0_attn_W_b, L0_attn_W_a,
           L0_attn_W_out, L0_attn_conv_w, L0_attn_dt_bias, L0_attn_A_log,
           L0_attn_norm_weight, L0_ffn_w_gate, L0_ffn_w_up, L0_ffn_w_down):
    x = np.ascontiguousarray(np.asarray(L0_attn_x, dtype=np.float32)).reshape(8192, D)
    wg = np.ascontiguousarray(np.asarray(L0_ffn_w_gate, dtype=np.float32))
    wu = np.ascontiguousarray(np.asarray(L0_ffn_w_up, dtype=np.float32))
    wd = np.ascontiguousarray(np.asarray(L0_ffn_w_down, dtype=np.float32))
    wqkv = np.ascontiguousarray(np.asarray(L0_attn_W_qkv, dtype=np.float32))
    wz = np.ascontiguousarray(np.asarray(L0_attn_W_z, dtype=np.float32))
    wb = np.asarray(L0_attn_W_b, dtype=np.float32)
    wa = np.asarray(L0_attn_W_a, dtype=np.float32)
    wout = np.ascontiguousarray(np.asarray(L0_attn_W_out, dtype=np.float32))
    cw = np.ascontiguousarray(np.asarray(L0_attn_conv_w, dtype=np.float32).reshape(D, 4))
    sm = np.zeros((3, 64), np.float32)
    sm[0, :16] = np.asarray(L0_attn_dt_bias, dtype=np.float32)
    sm[1, :16] = np.asarray(L0_attn_A_log, dtype=np.float32)
    sm[2, :64] = np.asarray(L0_attn_norm_weight, dtype=np.float32)
    wba = np.ascontiguousarray(np.concatenate([wb, wa], axis=0))

    H2048 = _hadamard(2048).astype(ml_dtypes.bfloat16)
    B1 = np.kron(np.eye(2), _hadamard(64)).astype(ml_dtypes.bfloat16)
    H128 = _hadamard(128).astype(ml_dtypes.bfloat16)
    IDT = np.eye(128).astype(ml_dtypes.bfloat16)
    attcn = np.zeros((6, 64 * 64), np.float32)
    attcn[0] = np.tril(np.ones((64, 64)), -1).reshape(-1)
    attcn[1] = np.tril(np.ones((64, 64)), 0).reshape(-1)
    attcn[2] = np.triu(np.ones((64, 64)), 1).reshape(-1)
    attcn[3] = np.triu(np.ones((64, 64)), 0).reshape(-1)
    bd2c = np.zeros((128, 2), np.float32)
    bd2c[0:64, 0] = 1.0
    bd2c[64:128, 1] = 1.0
    attcn[4, 0:256] = bd2c.reshape(-1)
    attcn[5, 0:256] = bd2c.T.reshape(-1)

    in_maps = []
    for c in range(8):
        par = c % 2
        m = dict(
            xs=x[c * TPC:(c + 1) * TPC],
            wg_raw=wg[par * (IFF // 2):(par + 1) * (IFF // 2)],
            wu_raw=wu[par * (IFF // 2):(par + 1) * (IFF // 2)],
            wd_raw=wd[par * (D // 2):(par + 1) * (D // 2)],
            wqkv_raw=wqkv[par * 1024:(par + 1) * 1024],
            wz_raw=wz[par * 512:(par + 1) * 512],
            wout_raw=wout[par * 1024:(par + 1) * 1024],
            wba_raw=wba,
            convw=cw,
            smalls=sm,
            h2048c=H2048, b1c=B1, h128c=H128, idtc=IDT, attc=attcn,
        )
        in_maps.append(m)

    nc = build_program(_PHASES)
    res = run_bass_kernel_spmd(nc, in_maps, core_ids=list(range(8)))
    global _LAST_RES
    _LAST_RES = res
    out = np.concatenate([np.asarray(r["xo"]) for r in res.results], axis=0)
    return out.reshape(4, 2048, D).astype(np.float32)


# version 13
# speedup vs baseline: 1.0419x; 1.0419x over previous
import numpy as np
import ml_dtypes
import concourse.bass as bass
import concourse.mybir as mybir
from concourse.bass_utils import run_bass_kernel_spmd
from contextlib import ExitStack

F32 = mybir.dt.float32
BF16 = mybir.dt.bfloat16
I8 = mybir.dt.int8
I32 = mybir.dt.int32
AF = mybir.ActivationFunctionType
OP = mybir.AluOpType
AX = mybir.AxisListType.X

MAGIC = float(np.float32(1.5 * 2 ** 23))
FLAGV = 0x5A5A0000

D = 2048
IFF = 8192
TPC = 1024
NTT = 8
NH, DK, DV, NKH = 16, 64, 64, 8
CCH, NCHK = 64, 16
C2048 = 1.0 / float(np.sqrt(2048.0))
C8192 = 1.0 / float(np.sqrt(8192.0))

_PHASES = "all"
_ACUT = None
_CCUT = 16
_KCUT = 5
_P1MASK = 63
_DBG = False
_DBGA = False


def _hadamard(n):
    i = np.arange(n)
    a = i[:, None] & i[None, :]
    pc = np.zeros((n, n), dtype=np.int64)
    while a.any():
        pc += a & 1
        a >>= 1
    return np.where(pc % 2 == 0, 1.0, -1.0).astype(np.float32)


class Trk:
    COMP = ("v", "s", "t", "p")

    def __init__(self, nc, st):
        self.nc = nc
        self.eng = {"v": nc.vector, "s": nc.scalar, "t": nc.tensor, "p": nc.gpsimd,
                    "q0": nc.sync, "q1": nc.scalar, "q2": nc.gpsimd}
        self.sem = {}
        for d in ("v", "s", "t", "p", "q0", "q1", "q2"):
            self.sem[d] = st.enter_context(nc.semaphore(f"trk_{d}"))
        self.cnt = {d: 0 for d in self.sem}
        self.lastw = {}
        self.readers = {}
        self.waited = {}

    def _wait(self, we, dom, val):
        if val <= 0 or val <= self.waited.get((we, dom), 0):
            return
        self.eng[we].wait_ge(self.sem[dom], val)
        self.waited[(we, dom)] = val

    def pre(self, e, reads=(), writes=()):
        for k in reads:
            lw = self.lastw.get(k)
            if lw is not None and not (lw[0] == e == "t"):
                self._wait(e, lw[0], lw[1])
        for k in writes:
            lw = self.lastw.get(k)
            if lw is not None and not (lw[0] == e == "t"):
                self._wait(e, lw[0], lw[1])
            for (re, rv) in self.readers.get(k, ()):
                if not (re == e == "t"):
                    self._wait(e, re, rv)

    def post(self, e, inst, reads=(), writes=()):
        amt = 16 if e in ("q0", "q1", "q2") else 1
        self.cnt[e] += amt
        inst.then_inc(self.sem[e], amt)
        cur = (e, self.cnt[e])
        for k in writes:
            self.lastw[k] = cur
            self.readers[k] = []
        for k in reads:
            self.readers.setdefault(k, []).append(cur)

    def ve(self, fn, *args, reads=(), writes=(), **kw):
        self.pre("v", reads, writes)
        inst = fn(*args, **kw)
        self.post("v", inst, reads, writes)
        return inst

    def se(self, fn, *args, reads=(), writes=(), **kw):
        self.pre("s", reads, writes)
        inst = fn(*args, **kw)
        self.post("s", inst, reads, writes)
        return inst

    def po(self, fn, *args, reads=(), writes=(), **kw):
        self.pre("p", reads, writes)
        inst = fn(*args, **kw)
        self.post("p", inst, reads, writes)
        return inst

    class TGroup:
        def __init__(self, trk, reads, writes):
            self.trk, self.reads, self.writes = trk, tuple(reads), tuple(writes)
            self.last = None

        def __enter__(self):
            self.trk.pre("t", self.reads, self.writes)
            return self

        def add(self, inst):
            self.last = inst
            return inst

        def __exit__(self, et, ev, tb):
            if et is None:
                assert self.last is not None
                self.trk.post("t", self.last, self.reads, self.writes)

    def tg(self, reads=(), writes=()):
        return Trk.TGroup(self, reads, writes)

    def dma(self, q, out, in_, reads=(), writes=(), transpose=False):
        we = {"q0": "q0", "q1": "s", "q2": "p"}[q]
        self.pre(we, reads, writes)
        inst = self.eng[q].dma_start(out=out, in_=in_, transpose=transpose)
        self.post(q, inst, reads, writes)
        return inst

    def join(self):
        for e in self.COMP + ("q0",):
            for d, c in self.cnt.items():
                if d != e:
                    self._wait(e, d, c)

    def wait_all_on(self, e):
        for d, c in self.cnt.items():
            if d != e:
                self._wait(e, d, c)


def build_program(phases="all"):
    nc = bass.Bass("TRN2", target_bir_lowering=False, num_devices=8,
                   detect_race_conditions=False)

    xs = nc.dram_tensor("xs", [TPC, D], F32, kind="ExternalInput")
    xo = nc.dram_tensor("xo", [TPC, D], F32, kind="ExternalOutput")
    wg_raw = nc.dram_tensor("wg_raw", [IFF // 2, D], F32, kind="ExternalInput")
    wu_raw = nc.dram_tensor("wu_raw", [IFF // 2, D], F32, kind="ExternalInput")
    wd_raw = nc.dram_tensor("wd_raw", [D // 2, IFF], F32, kind="ExternalInput")
    wqkv_raw = nc.dram_tensor("wqkv_raw", [D // 2, D], F32, kind="ExternalInput")
    wz_raw = nc.dram_tensor("wz_raw", [512, D], F32, kind="ExternalInput")
    wout_raw = nc.dram_tensor("wout_raw", [D // 2, 1024], F32, kind="ExternalInput")
    wba_raw = nc.dram_tensor("wba_raw", [32, D], F32, kind="ExternalInput")
    convw = nc.dram_tensor("convw", [D, 4], F32, kind="ExternalInput")
    smalls = nc.dram_tensor("smalls", [3, 64], F32, kind="ExternalInput")
    h2048c = nc.dram_tensor("h2048c", [D, D], BF16, kind="ExternalInput")
    b1c = nc.dram_tensor("b1c", [128, 128], BF16, kind="ExternalInput")
    h128c = nc.dram_tensor("h128c", [128, 128], BF16, kind="ExternalInput")
    idtc = nc.dram_tensor("idtc", [128, 128], BF16, kind="ExternalInput")
    attc = nc.dram_tensor("attc", [6, 64 * 64], F32, kind="ExternalInput")
    pid_t = nc.partition_id_tensor

    shWg = nc.dram_tensor("shWg", [IFF, D], BF16, addr_space="Shared")
    shWu = nc.dram_tensor("shWu", [IFF, D], BF16, addr_space="Shared")
    shWd = nc.dram_tensor("shWd", [D, IFF], BF16, addr_space="Shared")
    shWgT = nc.dram_tensor("shWgT", [D, IFF], BF16, addr_space="Shared")
    shWuT = nc.dram_tensor("shWuT", [D, IFF], BF16, addr_space="Shared")
    shWdT = nc.dram_tensor("shWdT", [IFF, D], BF16, addr_space="Shared")
    shWqkv = nc.dram_tensor("shWqkv", [D, D], BF16, addr_space="Shared")
    shWz = nc.dram_tensor("shWz", [1024, D], BF16, addr_space="Shared")
    shWout = nc.dram_tensor("shWout", [D, 1024], BF16, addr_space="Shared")
    shWba = nc.dram_tensor("shWba", [32, D], BF16, addr_space="Shared")
    shScale = nc.dram_tensor("shScale", [2, 16], F32, addr_space="Shared")
    shFlg = nc.dram_tensor("shFlg", [2, 32], I32, addr_space="Shared")
    shHalo = nc.dram_tensor("shHalo", [128, 48], F32, addr_space="Shared")
    shS16 = nc.dram_tensor("shS16", [64, 1024], F32, addr_space="Shared")
    if _DBGA:
        d_lnT = nc.dram_tensor("d_lnT", [128, 1024], BF16, kind="ExternalOutput")
        d_q = nc.dram_tensor("d_q", [64, 8192], BF16, kind="ExternalOutput")
        d_k = nc.dram_tensor("d_k", [64, 8192], BF16, kind="ExternalOutput")
        d_vcm = nc.dram_tensor("d_vcm", [64, 16384], BF16, kind="ExternalOutput")
        d_kcm = nc.dram_tensor("d_kcm", [64, 8192], BF16, kind="ExternalOutput")
        d_gb = nc.dram_tensor("d_gb", [16, 2048], F32, kind="ExternalOutput")
        d_cum = nc.dram_tensor("d_cum", [64, 1280], F32, kind="ExternalOutput")
        d_cumT = nc.dram_tensor("d_cumT", [128, 128], F32, kind="ExternalOutput")
        d_y = nc.dram_tensor("d_y", [128, 8192], BF16, kind="ExternalOutput")
        d_S = nc.dram_tensor("d_S", [64, 1024], F32, kind="ExternalOutput")
        d_z = nc.dram_tensor("d_z", [128, 1024], BF16, kind="ExternalOutput")
        d_c0 = nc.dram_tensor("d_c0", [128, 8192], F32, kind="ExternalOutput")
        d_c1 = nc.dram_tensor("d_c1", [128, 8192], F32, kind="ExternalOutput")
        d_cU = nc.dram_tensor("d_cU", [64, 16384], F32, kind="ExternalOutput")
        d_cS2 = nc.dram_tensor("d_cS2", [64, 16384], F32, kind="ExternalOutput")
        d_cE1 = nc.dram_tensor("d_cE1", [64, 16384], F32, kind="ExternalOutput")
        d_cE2 = nc.dram_tensor("d_cE2", [64, 16384], F32, kind="ExternalOutput")
        d_cP = nc.dram_tensor("d_cP", [64, 16384], F32, kind="ExternalOutput")
        d_cQ = nc.dram_tensor("d_cQ", [64, 16384], F32, kind="ExternalOutput")
    if _DBG:
        d_swcol = nc.dram_tensor("d_swcol", [128, 8], F32, kind="ExternalOutput")
        d_i1T = nc.dram_tensor("d_i1T", [128, 1024], BF16, kind="ExternalOutput")
        d_inv1 = nc.dram_tensor("d_inv1", [128, 8], F32, kind="ExternalOutput")
        d_h0 = nc.dram_tensor("d_h0", [128, 1024], BF16, kind="ExternalOutput")
        d_h2 = nc.dram_tensor("d_h2", [128, 1024], BF16, kind="ExternalOutput")
        d_i2T = nc.dram_tensor("d_i2T", [128, 512], I8, kind="ExternalOutput")
        d_wg = nc.dram_tensor("d_wg", [128, D], BF16, kind="ExternalOutput")
        d_wd = nc.dram_tensor("d_wd", [128, 1024], BF16, kind="ExternalOutput")
        d_xn = nc.dram_tensor("d_xn", [128, D], BF16, kind="ExternalOutput")
        d_xnT = nc.dram_tensor("d_xnT", [128, D], BF16, kind="ExternalOutput")
        d_f1 = nc.dram_tensor("d_f1", [128, D], F32, kind="ExternalOutput")
        d_hh = nc.dram_tensor("d_hh", [128, 3 * D], BF16, kind="ExternalOutput")
        d_pg = nc.dram_tensor("d_pg", [128, 512], F32, kind="ExternalOutput")
        d_pu = nc.dram_tensor("d_pu", [128, 512], F32, kind="ExternalOutput")
        d_cg = nc.dram_tensor("d_cg", [128, 8], F32, kind="ExternalOutput")
        d_wgs = nc.dram_tensor("d_wgs", [128, 2048], BF16, kind="ExternalOutput")
    xsp1 = nc.dram_tensor("xsp1", [TPC, D], F32)
    xsp2 = nc.dram_tensor("xsp2", [TPC, D], F32)
    scrd = nc.dram_tensor("scal_scr", [1, 64], F32)
    scrd2 = nc.dram_tensor("scrd2", [1, 2048], F32)
    zsp = nc.dram_tensor("zsp", [1024, 1024], BF16)

    st_top = ExitStack()
    _uid = [0]

    def un(base):
        _uid[0] += 1
        return f"{base}_{_uid[0]}"

    with st_top:
        trk = Trk(nc, st_top)
        g = nc.gpsimd

        def sb(name, shape, dt=F32):
            return st_top.enter_context(nc.sbuf_tensor(name, shape, dt))

        st_top.enter_context(nc.Block())

        i1Tbox = {}
        cB1 = sb("cB1", [128, 128], BF16)
        cH128 = sb("cH128", [128, 128], BF16)
        cIDT = sb("cIDT", [128, 128], BF16)
        inv1 = sb("inv1", [128, 8], F32)
        inv2 = sb("inv2", [128, 8], F32)
        cgc = sb("cgc", [128, 8], F32)
        cuc = sb("cuc", [128, 8], F32)
        cdc = sb("cdc", [128, 8], F32)
        swcol = sb("swcol", [128, 8], F32)
        onesf = sb("onesf", [128, 1], F32)
        ppacc = sb("ppacc", [128, 4], F32)
        pp1 = sb("pp1", [1, 64], F32)
        ftile = sb("ftile", [1, 32], I32)
        pfl = sb("pfl", [1, 32], I32)
        pidsb = sb("pidsb", [1, 1], mybir.dt.uint32)

        r_loop = g.alloc_register("r_loop")
        r_val = g.alloc_register("r_val")
        r_pw = g.alloc_register("r_pw")
        r_pid = g.alloc_register("r_pid")
        r_par = g.alloc_register("r_par")
        pollsem = st_top.enter_context(nc.semaphore("pollsem"))
        g.reg_mov(r_pw, 0)

        trk.dma("q2", pidsb[:], pid_t[0:1, 0:1], writes=[("pid",)])
        trk.pre("p", reads=[("pid",)])
        g.reg_load(r_pid, pidsb[0:1, 0:1])
        g.reg_alu(r_par, r_pid, 1, OP.bitwise_and)

        trk.dma("q0", cB1[:], b1c[:, :], writes=[("cB1",)])
        trk.dma("q0", cH128[:], h128c[:, :], writes=[("cH128",)])
        trk.dma("q0", cIDT[:], idtc[:, :], writes=[("cIDT",)])
        trk.ve(nc.vector.memset, onesf[:], 1.0, writes=[("ones",)])

        def gwm():
            g.reg_add(r_pw, r_pw, 16)
            g.wait_ge(pollsem, r_pw)

        def set_flag(widx):
            trk.ve(nc.vector.memset, ftile[0:1, widx:widx + 1], FLAGV + widx,
                   writes=[("ftile", widx)])
            trk.pre("p", reads=[("ftile", widx)])
            with g.If_eq(r_par, 0):
                g.dma_start(out=shFlg[0, widx:widx + 1], in_=ftile[0:1, widx:widx + 1]).then_inc(pollsem, 16)
            with g.Else():
                g.dma_start(out=shFlg[1, widx:widx + 1], in_=ftile[0:1, widx:widx + 1]).then_inc(pollsem, 16)
            gwm()

        def poll_flag(widx, key):
            g.reg_mov(r_loop, 1)
            with g.While(r_loop):
                with g.If_eq(r_par, 0):
                    g.dma_start(out=pfl[0:1, widx:widx + 1], in_=shFlg[1, widx:widx + 1]).then_inc(pollsem, 16)
                with g.Else():
                    g.dma_start(out=pfl[0:1, widx:widx + 1], in_=shFlg[0, widx:widx + 1]).then_inc(pollsem, 16)
                g.reg_add(r_pw, r_pw, 16)
                g.wait_ge(pollsem, r_pw)
                g.reg_load(r_val, pfl[0:1, widx:widx + 1])
                g.reg_alu(r_loop, r_val, FLAGV + widx, OP.bitwise_xor)
            inst = g.engine_nop()
            trk.post("p", inst, writes=[key])

        def pair_write(even_ap, odd_ap, src_ap, reads):
            trk.pre("p", reads=list(reads))
            with g.If_eq(r_par, 0):
                g.dma_start(out=even_ap, in_=src_ap).then_inc(pollsem, 16)
            with g.Else():
                g.dma_start(out=odd_ap, in_=src_ap).then_inc(pollsem, 16)
            gwm()
            inst = g.engine_nop()
            trk.post("p", inst, reads=list(reads))

        def pair_read_peer(dst_ap, even_ap, odd_ap, writes):
            trk.pre("p", writes=list(writes))
            with g.If_eq(r_par, 0):
                g.dma_start(out=dst_ap, in_=odd_ap).then_inc(pollsem, 16)
            with g.Else():
                g.dma_start(out=dst_ap, in_=even_ap).then_inc(pollsem, 16)
            gwm()
            inst = g.engine_nop()
            trk.post("p", inst, writes=list(writes))

        def half_write_tiles(pairs, reads):
            trk.pre("p", reads=list(reads))
            with g.If_eq(r_par, 0):
                for de, do, sr in pairs:
                    g.dma_start(out=de, in_=sr).then_inc(pollsem, 16)
            with g.Else():
                for de, do, sr in pairs:
                    g.dma_start(out=do, in_=sr).then_inc(pollsem, 16)
            g.reg_add(r_pw, r_pw, 16 * len(pairs))
            g.wait_ge(pollsem, r_pw)
            inst = g.engine_nop()
            trk.post("p", inst, reads=list(reads))

        def half_write(dst_even_ap, dst_odd_ap, src_ap, reads):
            trk.pre("p", reads=list(reads))
            with g.If_eq(r_par, 0):
                g.dma_start(out=dst_even_ap, in_=src_ap).then_inc(pollsem, 16)
            with g.Else():
                g.dma_start(out=dst_odd_ap, in_=src_ap).then_inc(pollsem, 16)
            gwm()
            inst = g.engine_nop()
            trk.post("p", inst, reads=list(reads))

        stPP = ExitStack()
        raws = stPP.enter_context(nc.sbuf_tensor(un("pp_raw"), [128, 2 * 2048], F32))
        qf = stPP.enter_context(nc.sbuf_tensor(un("pp_qf"), [128, 2048], F32))
        qout = stPP.enter_context(nc.sbuf_tensor(un("pp_q"), [128, 2 * 2048], BF16))
        rawD = stPP.enter_context(nc.sbuf_tensor(un("pp_rawD"), [128, IFF], F32))
        tD = stPP.enter_context(nc.sbuf_tensor(un("pp_tD"), [128, IFF], F32))
        qD = stPP.enter_context(nc.sbuf_tensor(un("pp_qD"), [128, IFF], BF16))
        ppsum = stPP.enter_context(nc.psum_tensor(un("pp_ps"), [1, 8], F32))
        ppsT = [stPP.enter_context(nc.psum_tensor(un("pp_psT"), [128, 128], BF16)) for _ in range(2)]
        tT = [stPP.enter_context(nc.sbuf_tensor(un("pp_tT"), [128, 2048], BF16)) for _ in range(2)]

        def pp_tiles(widx, raw, n_rows, n_cols):
            out = []
            for i in range(n_rows // 128):
                for j in range(n_cols // 2048):
                    out.append((i * 128, j * 2048))
            return out

        def pp_scale(widx, raw, n_rows, n_cols, flg_scale):
            tiles = pp_tiles(widx, raw, n_rows, n_cols)
            trk.ve(nc.vector.memset, ppacc[:, widx:widx + 1], 0.0, writes=[("ppacc", widx)])
            for i, (r0, c0) in enumerate(tiles):
                buf = i % 2
                rt = raws[:, buf * 2048:(buf + 1) * 2048]
                trk.dma("q0", rt, raw[r0:r0 + 128, c0:c0 + 2048], writes=[("ppraw", buf)])
                trk.ve(nc.vector.tensor_reduce, ppacc[:, 3:4], rt, axis=AX, op=OP.add,
                       apply_absolute_value=True, reads=[("ppraw", buf)], writes=[("ppt",)])
                trk.ve(nc.vector.tensor_tensor, ppacc[:, widx:widx + 1], ppacc[:, widx:widx + 1],
                       ppacc[:, 3:4], op=OP.add,
                       reads=[("ppt",), ("ppacc", widx)], writes=[("ppacc", widx)])
            with trk.tg(reads=[("ppacc", widx), ("ones",)], writes=[("pps", widx)]) as t:
                t.add(nc.tensor.matmul(ppsum[0:1, widx:widx + 1], ppacc[:, widx:widx + 1],
                                       onesf[:, 0:1], start=True, stop=True))
            trk.se(nc.scalar.activation, pp1[0:1, widx:widx + 1], ppsum[0:1, widx:widx + 1], AF.Copy,
                   reads=[("pps", widx)], writes=[("pp1", widx)])
            pair_write(shScale[0, widx:widx + 1], shScale[1, widx:widx + 1],
                       pp1[0:1, widx:widx + 1], reads=[("pp1", widx)])
            set_flag(flg_scale)
            poll_flag(flg_scale, ("flg", flg_scale))
            pair_read_peer(pp1[0:1, 16 + widx:17 + widx], shScale[0, widx:widx + 1],
                           shScale[1, widx:widx + 1], writes=[("pp1p", widx)])
            n_el = float(2 * n_rows * n_cols)
            trk.ve(nc.vector.tensor_tensor, pp1[0:1, 8 + widx:9 + widx], pp1[0:1, widx:widx + 1],
                   pp1[0:1, 16 + widx:17 + widx], op=OP.add,
                   reads=[("pp1", widx), ("pp1p", widx)], writes=[("swsum", widx)])
            trk.ve(nc.vector.tensor_scalar, pp1[0:1, 8 + widx:9 + widx], pp1[0:1, 8 + widx:9 + widx],
                   1.0 / n_el, 1e-5, op0=OP.mult, op1=OP.max,
                   reads=[("swsum", widx)], writes=[("sw", widx)])
            trk.ve(nc.vector.reciprocal, pp1[0:1, 24 + widx:25 + widx], pp1[0:1, 8 + widx:9 + widx],
                   reads=[("sw", widx)], writes=[("swi", widx)])
            trk.dma("q2", scrd[0:1, 2 * widx:2 * widx + 1], pp1[0:1, 8 + widx:9 + widx],
                    reads=[("sw", widx)], writes=[("scr_sw", widx)])
            trk.dma("q2", scrd[0:1, 2 * widx + 1:2 * widx + 2], pp1[0:1, 24 + widx:25 + widx],
                    reads=[("swi", widx)], writes=[("scr_swi", widx)])
            trk.dma("q2", swcol[:, widx:widx + 1], bass.AP(scrd, 2 * widx, [[0, 128], [1, 1]]),
                    reads=[("scr_sw", widx)], writes=[("swb", widx)])
            trk.dma("q2", swcol[:, 4 + widx:5 + widx], bass.AP(scrd, 2 * widx + 1, [[0, 128], [1, 1]]),
                    reads=[("scr_swi", widx)], writes=[("swib", widx)])

        def pp_quant_gu(widx, raw, sh_out, n_rows, flg_q, shkey):
            tiles = pp_tiles(widx, raw, n_rows, D)
            for i, (r0, c0) in enumerate(tiles):
                buf = i % 2
                rt = raws[:, buf * 2048:(buf + 1) * 2048]
                qt = qout[:, buf * 2048:(buf + 1) * 2048]
                trk.dma("q0", rt, raw[r0:r0 + 128, c0:c0 + 2048], writes=[("ppraw", buf)])
                trk.se(nc.scalar.activation, qf[:], rt, AF.Copy,
                       scale=swcol[:, 4 + widx:5 + widx], bias=MAGIC,
                       reads=[("ppraw", buf), ("swib", widx)], writes=[("ppqf",)])
                trk.ve(nc.vector.tensor_scalar, qf[:], qf[:], -MAGIC, 1.0, op0=OP.add, op1=OP.min,
                       reads=[("ppqf",)], writes=[("ppqf",)])
                trk.ve(nc.vector.tensor_scalar, qt, qf[:], -1.0, None, op0=OP.max,
                       reads=[("ppqf",)], writes=[("ppq", buf)])
                for k in range(16):
                    pb = k % 2
                    with trk.tg(reads=[("ppq", buf), ("cIDT",)], writes=[("ppsT", pb)]) as t:
                        t.add(nc.tensor.transpose(ppsT[pb][:], qt[:, k * 128:(k + 1) * 128], cIDT[:]))
                    trk.ve(nc.vector.tensor_copy, tT[buf][:, k * 128:(k + 1) * 128], ppsT[pb][:],
                           reads=[("ppsT", pb)], writes=[("ppT", buf)])
                half_write_tiles(
                    [(sh_out[k * 128:(k + 1) * 128, r0:r0 + 128],
                      sh_out[k * 128:(k + 1) * 128, n_rows + r0:n_rows + r0 + 128],
                      tT[buf][:, k * 128:(k + 1) * 128]) for k in range(16)],
                    reads=[("ppT", buf)])
            set_flag(flg_q)
            poll_flag(flg_q, ("flg", flg_q))
            trk.lastw[shkey] = ("p", trk.cnt["p"])

        def pp_quant_wd(widx, raw, sh_out, n_rows, flg_q, shkey):
            for rb in range(n_rows // 128):
                r0 = rb * 128
                for j in range(4):
                    trk.dma("q0", rawD[:, j * 2048:(j + 1) * 2048],
                            raw[r0:r0 + 128, j * 2048:(j + 1) * 2048], writes=[("ppD", j)])
                trk.se(nc.scalar.activation, tD[:], rawD[:], AF.Copy,
                       scale=swcol[:, 4 + widx:5 + widx], bias=MAGIC,
                       reads=[("ppD", 0), ("ppD", 1), ("ppD", 2), ("ppD", 3), ("swib", widx)],
                       writes=[("pptD",)])
                trk.ve(nc.vector.tensor_scalar, tD[:], tD[:], -MAGIC, 1.0, op0=OP.add, op1=OP.min,
                       reads=[("pptD",)], writes=[("pptD",)])
                dst = bass.AP(qD, 0, [[IFF, 128], [1, 128], [128, 64]])
                src = bass.AP(tD, 0, [[IFF, 128], [64, 128], [1, 64]])
                trk.ve(nc.vector.tensor_scalar, dst, src, -1.0, None, op0=OP.max,
                       reads=[("pptD",)], writes=[("ppqD",)])
                for grp in range(4):
                    gb = grp % 2
                    for k in range(16):
                        kk = grp * 16 + k
                        pb = k % 2
                        with trk.tg(reads=[("ppqD",), ("cIDT",)], writes=[("ppsT", pb)]) as t:
                            t.add(nc.tensor.transpose(ppsT[pb][:], qD[:, kk * 128:(kk + 1) * 128],
                                                      cIDT[:]))
                        trk.ve(nc.vector.tensor_copy, tT[gb][:, k * 128:(k + 1) * 128], ppsT[pb][:],
                               reads=[("ppsT", pb)], writes=[("ppT", gb)])
                    half_write_tiles(
                        [(sh_out[(grp * 16 + k) * 128:(grp * 16 + k + 1) * 128, r0:r0 + 128],
                          sh_out[(grp * 16 + k) * 128:(grp * 16 + k + 1) * 128,
                                 n_rows + r0:n_rows + r0 + 128],
                          tT[gb][:, k * 128:(k + 1) * 128]) for k in range(16)],
                        reads=[("ppT", gb)])
            set_flag(flg_q)
            poll_flag(flg_q, ("flg", flg_q))
            trk.lastw[shkey] = ("p", trk.cnt["p"])

        def pp_cast_attn():
            jobs = [(wqkv_raw, shWqkv, 1024, D), (wz_raw, shWz, 512, D),
                    (wout_raw, shWout, 1024, 1024)]
            for ji, (raw, sh_outt, n_rows, n_cols) in enumerate(jobs):
                nrr = n_rows if n_cols == 2048 else n_rows // 2
                for i in range(max(1, (n_rows * n_cols) // (128 * 2048))):
                    buf = i % 2
                    rt = raws[:, buf * 2048:(buf + 1) * 2048]
                    qt = qout[:, buf * 2048:(buf + 1) * 2048]
                    if n_cols == 2048:
                        src = raw[i * 128:(i + 1) * 128, :]
                    else:
                        src = raw.ap().rearrange("(a b) c -> a (b c)", b=2)[i * 128:(i + 1) * 128, :]
                    trk.dma("q0", rt, src, writes=[("ppraw", buf)])
                    trk.ve(nc.vector.tensor_copy, qt, rt, reads=[("ppraw", buf)], writes=[("ppq", buf)])
                    if n_cols == 2048:
                        de = sh_outt[i * 128:(i + 1) * 128, :]
                        do = sh_outt[n_rows + i * 128:n_rows + (i + 1) * 128, :]
                    else:
                        shv = sh_outt.ap().rearrange("(a b) c -> a (b c)", b=2)
                        de = shv[i * 128:(i + 1) * 128, :]
                        do = shv[(n_rows + i * 128) // 2 * 2 // 2:0] if False else shv[n_rows // 2 + i * 128:n_rows // 2 + (i + 1) * 128, :]
                    half_write(de, do, qt, reads=[("ppq", buf)])
            trk.dma("q0", raws[:, 0:2048].tensor[0:32, 0:2048] if False else raws[0:32, 0:2048],
                    wba_raw[:, :], writes=[("ppraw", 0)])
            trk.ve(nc.vector.tensor_copy, qout[0:32, 0:2048], raws[0:32, 0:2048],
                   reads=[("ppraw", 0)], writes=[("ppq", 0)])
            trk.pre("p", reads=[("ppq", 0)])
            g.dma_start(out=shWba[:, :], in_=qout[0:32, 0:2048]).then_inc(pollsem, 16)
            gwm()
            inst = g.engine_nop()
            trk.post("p", inst, reads=[("ppq", 0)])
            set_flag(6)
            poll_flag(6, ("flg", 6))
            for kk in (("shWqkvb",), ("shWzb",), ("shWoutb",), ("shWbab",)):
                trk.lastw[kk] = ("p", trk.cnt["p"])

        pp_scale(0, wg_raw, IFF // 2, D, 0)
        pp_scale(1, wu_raw, IFF // 2, D, 2)
        pp_scale(2, wd_raw, D // 2, IFF, 4)
        pp_quant_gu(0, wg_raw, shWgT, IFF // 2, 1, ("shWgq",))
        pp_quant_gu(1, wu_raw, shWuT, IFF // 2, 3, ("shWuq",))
        pp_quant_wd(2, wd_raw, shWdT, D // 2, 5, ("shWdq",))

        def stage_A(xsrc, src_key):
            stA = ExitStack()
            with stA:
                xt = stA.enter_context(nc.sbuf_tensor(un("A_xt"), [128, 2 * D], F32))
                sq = stA.enter_context(nc.sbuf_tensor(un("A_sq"), [128, D], F32))
                f1 = stA.enter_context(nc.sbuf_tensor(un("A_f1"), [128, D], F32))
                xn = stA.enter_context(nc.sbuf_tensor(un("A_xn"), [128, D], BF16))
                xnT = stA.enter_context(nc.sbuf_tensor(un("A_xnT"), [128, D], BF16))
                hh = stA.enter_context(nc.sbuf_tensor(un("A_h2048"), [128, 16 * D], BF16))
                lns = stA.enter_context(nc.sbuf_tensor(un("A_lns"), [128, 8], F32))
                psT = [stA.enter_context(nc.psum_tensor(un(f"A_psT{i}"), [128, 512], BF16)) for i in range(2)]
                psF = [stA.enter_context(nc.psum_tensor(un(f"A_psF{i}"), [128, 512], F32)) for i in range(4)]
                for k in range(16):
                    trk.dma("q1", hh[:, k * D:(k + 1) * D], h2048c[k * 128:(k + 1) * 128, :],
                            writes=[("hh", k)])
                rd_hh = [("hh", k) for k in range(16)]
                for tt in range(NTT):
                    xb = tt % 2
                    xv = xt[:, xb * D:(xb + 1) * D]
                    trk.dma("q0", xv, xsrc[tt * 128:(tt + 1) * 128, :],
                            reads=([(src_key, tt)] if src_key else []), writes=[("A_xt", xb)])
                    trk.ve(nc.vector.tensor_reduce, lns[:, 0:1], xv, axis=AX, op=OP.add,
                           reads=[("A_xt", xb)], writes=[("ln0",)])
                    trk.se(nc.scalar.activation, sq[:], xv, AF.Square,
                           reads=[("A_xt", xb)], writes=[("A_sq",)])
                    trk.ve(nc.vector.tensor_reduce, lns[:, 1:2], sq[:], axis=AX, op=OP.add,
                           reads=[("A_sq",)], writes=[("ln1",)])
                    trk.se(nc.scalar.activation, lns[:, 2:3], lns[:, 0:1], AF.Copy, scale=-1.0 / D,
                           reads=[("ln0",)], writes=[("negm",)])
                    trk.ve(nc.vector.tensor_scalar, lns[:, 3:4], lns[:, 1:2], 1.0 / D, None, op0=OP.mult,
                           reads=[("ln1",)], writes=[("ex2",)])
                    trk.ve(nc.vector.tensor_tensor, lns[:, 4:5], lns[:, 2:3], lns[:, 2:3], op=OP.mult,
                           reads=[("negm",)], writes=[("m2",)])
                    trk.ve(nc.vector.tensor_tensor, lns[:, 5:6], lns[:, 3:4], lns[:, 4:5], op=OP.subtract,
                           reads=[("ex2",), ("m2",)], writes=[("var",)])
                    trk.ve(nc.vector.tensor_scalar, lns[:, 5:6], lns[:, 5:6], 1e-5, None, op0=OP.add,
                           reads=[("var",)], writes=[("var",)])
                    trk.ve(nc.vector.reciprocal, lns[:, 6:7], lns[:, 5:6],
                           reads=[("var",)], writes=[("rv",)])
                    trk.se(nc.scalar.activation, lns[:, 7:8], lns[:, 6:7], AF.Sqrt,
                           reads=[("rv",)], writes=[("rstd",)])
                    trk.ve(nc.vector.tensor_scalar, xn[:], xv, lns[:, 2:3], lns[:, 7:8],
                           op0=OP.add, op1=OP.mult,
                           reads=[("A_xt", xb), ("negm",), ("rstd",)],
                           writes=[("A_xn",), ("A_i1",)])
                    for kb in range(4):
                        pb = kb % 2
                        with trk.tg(reads=[("A_xn",), ("cIDT",)], writes=[("A_psT", pb)]) as t:
                            for j in range(4):
                                k = kb * 4 + j
                                t.add(nc.tensor.transpose(psT[pb][:, j * 128:(j + 1) * 128],
                                                          xn[:, k * 128:(k + 1) * 128], cIDT[:]))
                        trk.ve(nc.vector.tensor_copy, xnT[:, kb * 512:(kb + 1) * 512], psT[pb][:],
                               reads=[("A_psT", pb)], writes=[("A_xnT", kb)])
                    rd_xnT = [("A_xnT", kb) for kb in range(4)]
                    for ncn in range(4):
                        with trk.tg(reads=rd_xnT + rd_hh, writes=[("A_psF", ncn)]) as t:
                            for k in range(16):
                                t.add(nc.tensor.matmul(psF[ncn][:],
                                                       xnT[:, k * 128:(k + 1) * 128],
                                                       hh[:, k * D + ncn * 512:k * D + (ncn + 1) * 512],
                                                       start=(k == 0), stop=(k == 15)))
                        trk.se(nc.scalar.activation, f1[:, ncn * 512:(ncn + 1) * 512], psF[ncn][:], AF.Copy,
                               reads=[("A_psF", ncn)], writes=[("A_f1", ncn)])
                    if _DBG and tt == 6:
                        trk.dma("q2", d_hh[:, 0:D], hh[:, 0:D], reads=[("hh", 0)])
                        trk.dma("q2", d_hh[:, D:2 * D], hh[:, 14 * D:15 * D], reads=[("hh", 14)])
                        trk.dma("q2", d_hh[:, 2 * D:3 * D], hh[:, 15 * D:16 * D], reads=[("hh", 15)])
                        trk.dma("q2", d_xn[:, :], xn[:], reads=[("A_xn",)])
                        trk.dma("q2", d_xnT[:, :], xnT[:], reads=[("A_xnT", kb2) for kb2 in range(4)])
                        trk.dma("q2", d_f1[:, :], f1[:], reads=[("A_f1", i2) for i2 in range(4)])
                    rd_f1 = [("A_f1", i) for i in range(4)]
                    trk.ve(nc.vector.tensor_reduce, lns[:, 0:1], f1[:], axis=AX, op=OP.max,
                           apply_absolute_value=True, reads=rd_f1, writes=[("mx",)])
                    trk.ve(nc.vector.tensor_scalar, lns[:, 1:2], lns[:, 0:1], C2048, 1e-5,
                           op0=OP.mult, op1=OP.max, reads=[("mx",)], writes=[("mxc",)])
                    trk.ve(nc.vector.reciprocal, lns[:, 2:3], lns[:, 1:2],
                           reads=[("mxc",)], writes=[("rec",)])
                    trk.ve(nc.vector.tensor_scalar, lns[:, 3:4], lns[:, 2:3], 127.0 * C2048, None,
                           op0=OP.mult, reads=[("rec",)], writes=[("seff",)])
                    trk.ve(nc.vector.tensor_scalar, inv1[:, tt:tt + 1], lns[:, 1:2], 1.0 / 127.0, None,
                           op0=OP.mult, reads=[("mxc",)], writes=[("inv1", tt)])
                    trk.se(nc.scalar.activation, f1[:], f1[:], AF.Copy,
                           scale=lns[:, 3:4], bias=MAGIC,
                           reads=rd_f1 + [("seff",)], writes=[("A_t1",)] + rd_f1)
                    trk.se(nc.scalar.activation, xn[:], f1[:], AF.Copy, bias=-MAGIC,
                           reads=[("A_t1",)], writes=[("A_i1",), ("A_xn",)])
                    for kb in range(4):
                        pb = kb % 2
                        with trk.tg(reads=[("A_i1",), ("cIDT",)], writes=[("A_psT", pb)]) as t:
                            for j in range(4):
                                k = kb * 4 + j
                                t.add(nc.tensor.transpose(psT[pb][:, j * 128:(j + 1) * 128],
                                                          xn[:, k * 128:(k + 1) * 128], cIDT[:]))
                        src = bass.AP(psT[pb], 0, [[512, 128], [128, 4], [1, 128]])
                        dst = bass.AP(i1Tbox["t"], (kb * 4) * 1024 + tt * 128,
                                      [[16 * 1024, 128], [1024, 4], [1, 128]])
                        trk.ve(nc.vector.tensor_copy, dst, src,
                               reads=[("A_psT", pb)],
                               writes=[("i1T", kb * 4 + j2) for j2 in range(4)])
            trk.join()

        def stage_BCD(th, xsrc, xdst, out_key, src_key):
            stH = ExitStack()
            with stH:
                i2T = stH.enter_context(nc.sbuf_tensor(un("C_i2T"), [128, 64 * 512], I8))
                stHh = ExitStack()
                h = [stHh.enter_context(nc.sbuf_tensor(un(f"B_h{m}"), [128, IFF], BF16)) for m in range(4)]
                stB = ExitStack()
                with stB:
                    wgS = [stB.enter_context(nc.sbuf_tensor(un(f"B_wg{i}"), [128, 16 * 512], BF16)) for i in range(2)]
                    wuS = [stB.enter_context(nc.sbuf_tensor(un(f"B_wu{i}"), [128, 16 * 512], BF16)) for i in range(2)]
                    gsc = stB.enter_context(nc.sbuf_tensor(un("B_gs"), [128, 512], F32))
                    usc = stB.enter_context(nc.sbuf_tensor(un("B_us"), [128, 512], F32))
                    psG = [stB.enter_context(nc.psum_tensor(un(f"B_psG{i}"), [128, 512], F32)) for i in range(2)]
                    psU = [stB.enter_context(nc.psum_tensor(un(f"B_psU{i}"), [128, 512], F32)) for i in range(2)]
                    rd_i1 = [("i1T", k) for k in range(16)]
                    for gg in range(16):
                        sbi = gg % 2
                        for k in range(16):
                            trk.dma("q0", wgS[sbi][:, k * 512:(k + 1) * 512],
                                    shWgT[k * 128:(k + 1) * 128, gg * 512:(gg + 1) * 512],
                                    reads=[("shWgq",)], writes=[("wgS", sbi, k)])
                            trk.dma("q0", wuS[sbi][:, k * 512:(k + 1) * 512],
                                    shWuT[k * 128:(k + 1) * 128, gg * 512:(gg + 1) * 512],
                                    reads=[("shWuq",)], writes=[("wuS", sbi, k)])
                        for m in range(4):
                            tt = th * 4 + m
                            pb = m % 2
                            rdg = [("wgS", sbi, k) for k in range(16)] + rd_i1
                            with trk.tg(reads=rdg, writes=[("psG", pb)]) as t:
                                for k in range(16):
                                    t.add(nc.tensor.matmul(psG[pb][:],
                                                           i1Tbox["t"][:, k * 1024 + tt * 128:k * 1024 + (tt + 1) * 128],
                                                           wgS[sbi][:, k * 512:(k + 1) * 512],
                                                           start=(k == 0), stop=(k == 15)))
                            rdu = [("wuS", sbi, k) for k in range(16)] + rd_i1
                            with trk.tg(reads=rdu, writes=[("psU", pb)]) as t:
                                for k in range(16):
                                    t.add(nc.tensor.matmul(psU[pb][:],
                                                           i1Tbox["t"][:, k * 1024 + tt * 128:k * 1024 + (tt + 1) * 128],
                                                           wuS[sbi][:, k * 512:(k + 1) * 512],
                                                           start=(k == 0), stop=(k == 15)))
                            if _DBG and th == 0 and gg == 0 and m == 0:
                                trk.dma("q2", d_wgs[:, :], wgS[0][:, 0:2048],
                                        reads=[("wgS", 0, kx) for kx in range(4)])
                                trk.ve(nc.vector.tensor_copy, gsc[:], psG[pb][:],
                                       reads=[("psG", pb)], writes=[("gs",)])
                                trk.dma("q2", d_pg[:, :], gsc[:], reads=[("gs",)])
                                trk.ve(nc.vector.tensor_copy, usc[:], psU[pb][:],
                                       reads=[("psU", pb)], writes=[("us",)])
                                trk.dma("q2", d_pu[:, :], usc[:], reads=[("us",)])
                                trk.dma("q2", d_cg[:, :], cgc[:, 0:8], reads=[("cg", tt)])
                            trk.se(nc.scalar.activation, gsc[:], psG[pb][:], AF.Silu,
                                   scale=cgc[:, tt:tt + 1],
                                   reads=[("psG", pb), ("cg", tt)], writes=[("gs",)])
                            trk.ve(nc.vector.tensor_scalar, usc[:], psU[pb][:], cuc[:, tt:tt + 1], None,
                                   op0=OP.mult, reads=[("psU", pb), ("cu", tt)], writes=[("us",)])
                            trk.ve(nc.vector.tensor_tensor, h[m][:, gg * 512:(gg + 1) * 512],
                                   gsc[:], usc[:], op=OP.mult,
                                   reads=[("gs",), ("us",)], writes=[("h", m, gg)])
                if _DBG and th == 0:
                    trk.dma("q2", d_h0[:, :], h[0][:, 0:1024], reads=[("h", 0, 0), ("h", 0, 1)])
                trk.join()
                stC = ExitStack()
                with stC:
                    h2b = [stC.enter_context(nc.sbuf_tensor(un(f"C_h2{i}"), [128, IFF], BF16)) for i in range(2)]
                    t2s = stC.enter_context(nc.sbuf_tensor(un("C_t2s"), [128, 2048], F32))
                    sA = [stC.enter_context(nc.sbuf_tensor(un(f"C_sA{i}"), [128, 128], BF16)) for i in range(2)]
                    sB = [stC.enter_context(nc.sbuf_tensor(un(f"C_sB{i}"), [128, 128], BF16)) for i in range(2)]
                    lns = stC.enter_context(nc.sbuf_tensor(un("C_lns"), [128, 8], F32))
                    ps1 = [stC.enter_context(nc.psum_tensor(un(f"C_ps1{i}"), [128, 128], BF16)) for i in range(2)]
                    ps2 = [stC.enter_context(nc.psum_tensor(un(f"C_ps2{i}"), [128, 128], F32)) for i in range(2)]
                    ps3 = [stC.enter_context(nc.psum_tensor(un(f"C_ps3{i}"), [128, 256], BF16)) for i in range(2)]

                    def c_front(m):
                        h2 = h2b[m % 2]
                        for q in range(64):
                            pb = q % 2
                            with trk.tg(reads=[("h", m, q // 4), ("cIDT",)], writes=[("ps1", pb)]) as t:
                                t.add(nc.tensor.transpose(ps1[pb][:], h[m][:, q * 128:(q + 1) * 128], cIDT[:]))
                            trk.ve(nc.vector.tensor_copy, sA[pb][:], ps1[pb][:],
                                   reads=[("ps1", pb)], writes=[("sA", pb)])
                            with trk.tg(reads=[("sA", pb), ("cB1",)], writes=[("ps2", pb)]) as t:
                                t.add(nc.tensor.matmul(ps2[pb][:], cB1[:], sA[pb][:], start=True, stop=True))
                            trk.se(nc.scalar.activation, sB[pb][:], ps2[pb][:], AF.Copy,
                                   reads=[("ps2", pb)], writes=[("sB", pb)])
                            with trk.tg(reads=[("sB", pb), ("cIDT",)], writes=[("ps3", pb)]) as t:
                                t.add(nc.tensor.transpose(ps3[pb][:, 0:128], sB[pb][:], cIDT[:]))
                            trk.ve(nc.vector.tensor_copy, h[m][:, q * 128:(q + 1) * 128], ps3[pb][:, 0:128],
                                   reads=[("ps3", pb)], writes=[("h", m, q // 4)])
                        rd_all_h = [("h", m, q) for q in range(16)]
                        for jp in range(64):
                            pb = jp % 2
                            src = bass.AP(h[m], jp, [[IFF, 128], [64, 128]])
                            with trk.tg(reads=rd_all_h + [("cIDT",)], writes=[("ps1", pb)]) as t:
                                t.add(nc.tensor.transpose(ps1[pb][:], src, cIDT[:]))
                            trk.ve(nc.vector.tensor_copy, sA[pb][:], ps1[pb][:],
                                   reads=[("ps1", pb)], writes=[("sA", pb)])
                            with trk.tg(reads=[("sA", pb), ("cH128",)], writes=[("ps2", pb)]) as t:
                                t.add(nc.tensor.matmul(ps2[pb][:], sA[pb][:], cH128[:], start=True, stop=True))
                            trk.se(nc.scalar.activation, h2[:, jp * 128:(jp + 1) * 128], ps2[pb][:], AF.Copy,
                                   reads=[("ps2", pb)], writes=[("h2m", m)])

                    def c_back(m):
                        tt = th * 4 + m
                        h2 = h2b[m % 2]
                        lb = (m % 2) * 4
                        trk.ve(nc.vector.tensor_reduce, lns[:, lb:lb + 1], h2[:], axis=AX, op=OP.max,
                               apply_absolute_value=True, reads=[("h2m", m)], writes=[("mx2", m)])
                        trk.ve(nc.vector.tensor_scalar, lns[:, lb + 1:lb + 2], lns[:, lb:lb + 1], C8192, 1e-5,
                               op0=OP.mult, op1=OP.max, reads=[("mx2", m)], writes=[("mxc2", m)])
                        trk.ve(nc.vector.reciprocal, lns[:, lb + 2:lb + 3], lns[:, lb + 1:lb + 2],
                               reads=[("mxc2", m)], writes=[("rec2", m)])
                        trk.ve(nc.vector.tensor_scalar, lns[:, lb + 3:lb + 4], lns[:, lb + 2:lb + 3],
                               127.0 * C8192, None,
                               op0=OP.mult, reads=[("rec2", m)], writes=[("seff2", m)])
                        trk.ve(nc.vector.tensor_scalar, inv2[:, tt:tt + 1], lns[:, lb + 1:lb + 2], 1.0 / 127.0,
                               None, op0=OP.mult, reads=[("mxc2", m)], writes=[("inv2", tt)])
                        for cq in range(4):
                            trk.se(nc.scalar.activation, t2s[:], h2[:, cq * 2048:(cq + 1) * 2048], AF.Copy,
                                   scale=lns[:, lb + 3:lb + 4], bias=MAGIC,
                                   reads=[("h2m", m), ("seff2", m)], writes=[("t2s",)])
                            trk.se(nc.scalar.activation, h2[:, cq * 2048:(cq + 1) * 2048], t2s[:], AF.Copy,
                                   bias=-MAGIC, reads=[("t2s",)], writes=[("h2m", m)])
                        for k2 in range(32):
                            pb = k2 % 2
                            with trk.tg(reads=[("h2m", m), ("cIDT",)], writes=[("ps3", pb)]) as t:
                                for j2 in range(2):
                                    k = k2 * 2 + j2
                                    t.add(nc.tensor.transpose(ps3[pb][:, j2 * 128:(j2 + 1) * 128],
                                                              h2[:, k * 128:(k + 1) * 128], cIDT[:]))
                            trk.ve(nc.vector.tensor_copy,
                                   bass.AP(i2T, (k2 * 2) * 512 + m * 128, [[64 * 512, 128], [512, 2], [1, 128]]),
                                   ps3[pb][:],
                                   reads=[("ps3", pb)], writes=[("i2T", k2 * 2, m), ("i2T", k2 * 2 + 1, m)])

                    for m in range(4):
                        c_front(m)
                        if m > 0:
                            c_back(m - 1)
                    c_back(3)
                if _DBG and th == 0:
                    trk.dma("q2", d_h2[:, :], h2b[1][:, 0:1024], reads=[("h2m", 3)])
                    trk.dma("q2", d_i2T[:, :], i2T[:, 0:512], reads=[("i2T", 0, mm2) for mm2 in range(4)])
                trk.join()
                stHh.close()
                stD = ExitStack()
                with stD:
                    wdS = [stD.enter_context(nc.sbuf_tensor(un(f"D_wd{i}"), [128, 16 * 1024], BF16)) for i in range(2)]
                    cch = [stD.enter_context(nc.sbuf_tensor(un(f"D_cc{i}"), [128, 16 * 128], BF16)) for i in range(2)]
                    xrs = [stD.enter_context(nc.sbuf_tensor(un(f"D_xr{i}"), [128, D], F32)) for i in range(4)]
                    xot = [stD.enter_context(nc.sbuf_tensor(un(f"D_xo{i}"), [128, D], F32)) for i in range(4)]
                    psD = [stD.enter_context(nc.psum_tensor(un(f"D_ps{i}"), [128, 1024], F32)) for i in range(4)]
                    for m in range(4):
                        tt = th * 4 + m
                        trk.ve(nc.vector.tensor_tensor, cdc[:, tt:tt + 1], inv2[:, tt:tt + 1],
                               swcol[:, 2:3], op=OP.mult,
                               reads=[("inv2", tt), ("swb", 2)], writes=[("cd0", tt)])
                        trk.ve(nc.vector.tensor_scalar, cdc[:, tt:tt + 1], cdc[:, tt:tt + 1], 0.5, None,
                               op0=OP.mult, reads=[("cd0", tt)], writes=[("cd", tt)])
                        trk.dma("q1", xrs[m][:], xsrc[tt * 128:(tt + 1) * 128, :],
                                reads=([(src_key, tt)] if src_key else []), writes=[("xr", m)])
                    for ogh in range(2):
                        for kq in range(4):
                            sbi = kq % 2
                            for k in range(16):
                                kk = kq * 16 + k
                                trk.dma("q0",
                                        wdS[sbi][:, k * 1024:(k + 1) * 1024],
                                        shWdT[kk * 128:(kk + 1) * 128, ogh * 1024:(ogh + 1) * 1024],
                                        reads=[("shWdq",)], writes=[("wdS", sbi, k)])
                            for m in range(4):
                                cb = cch[m % 2]
                                src = bass.AP(i2T, (kq * 16) * 512 + m * 128,
                                              [[64 * 512, 128], [512, 16], [1, 128]])
                                dst = bass.AP(cb, 0, [[16 * 128, 128], [128, 16], [1, 128]])
                                trk.ve(nc.vector.tensor_copy, dst, src,
                                       reads=[("i2T", kq * 16 + kx, m) for kx in range(16)],
                                       writes=[("cch", m % 2)])
                                rd = [("wdS", sbi, k) for k in range(16)] + [("cch", m % 2)]
                                with trk.tg(reads=rd, writes=[("psD", m)]) as t:
                                    for k in range(16):
                                        st0 = (kq == 0 and k == 0)
                                        sp0 = (kq == 3 and k == 15)
                                        t.add(nc.tensor.matmul(psD[m][:, 0:512],
                                                               cb[:, k * 128:(k + 1) * 128],
                                                               wdS[sbi][:, k * 1024:k * 1024 + 512],
                                                               start=st0, stop=sp0))
                                        t.add(nc.tensor.matmul(psD[m][:, 512:1024],
                                                               cb[:, k * 128:(k + 1) * 128],
                                                               wdS[sbi][:, k * 1024 + 512:(k + 1) * 1024],
                                                               start=st0, stop=sp0))
                        for m in range(4):
                            tt = th * 4 + m
                            trk.se(nc.scalar.activation,
                                   xot[m][:, ogh * 1024:(ogh + 1) * 1024],
                                   psD[m][:], AF.Copy, scale=cdc[:, tt:tt + 1],
                                   reads=[("psD", m), ("cd", tt)], writes=[("xot", m, ogh)])
                        for m in range(4):
                            trk.ve(nc.vector.tensor_tensor,
                                   xot[m][:, ogh * 1024:(ogh + 1) * 1024],
                                   xot[m][:, ogh * 1024:(ogh + 1) * 1024],
                                   xrs[m][:, ogh * 1024:(ogh + 1) * 1024], op=OP.add,
                                   reads=[("xot", m, ogh), ("xr", m)],
                                   writes=[("xot", m, ogh)])
                    for m in range(4):
                        tt = th * 4 + m
                        trk.dma("q2", xdst[tt * 128:(tt + 1) * 128, :], xot[m][:],
                                reads=[("xot", m, 0), ("xot", m, 1)], writes=[(out_key, tt)])
                trk.join()

        def ffn(xsrc, xdst, out_key, src_key):
            stF0 = ExitStack()
            i1Tbox["t"] = stF0.enter_context(nc.sbuf_tensor(un("i1T"), [128, 16 * 1024], BF16))
            stage_A(xsrc, src_key)
            for tt in range(NTT):
                trk.ve(nc.vector.tensor_tensor, cgc[:, tt:tt + 1], inv1[:, tt:tt + 1],
                       swcol[:, 0:1], op=OP.mult,
                       reads=[("inv1", tt), ("swb", 0)], writes=[("cg", tt)])
                trk.ve(nc.vector.tensor_tensor, cuc[:, tt:tt + 1], inv1[:, tt:tt + 1],
                       swcol[:, 1:2], op=OP.mult,
                       reads=[("inv1", tt), ("swb", 1)], writes=[("cu", tt)])
            if _DBG and out_key == "XO" or _DBG and out_key == "X1":
                trk.dma("q2", d_i1T[:, :], i1T[:, 0:1024], reads=[("i1T", 0)])
                trk.dma("q2", d_inv1[:, :], inv1[:, 0:8], reads=[("inv1", tt2) for tt2 in range(8)])
            for th in range(2):
                stage_BCD(th, xsrc, xdst, out_key, src_key)
            trk.join()
            stF0.close()
            trk.join()



        def attention(xsrc, src_key, xdst, out_key):
            stAt = ExitStack()
            with stAt:
                A_ = lambda name, shape, dt=F32: stAt.enter_context(nc.sbuf_tensor(un(name), shape, dt))
                q_f64 = A_("at_qf64", [64, 8 * 1024], BF16)
                k_f64 = A_("at_kf64", [64, 8 * 1024], BF16)
                v_cm = A_("at_vcm", [64, 16 * 1024], BF16)
                k_cm = A_("at_kcm", [64, 16 * 512], BF16)
                g_fm = A_("at_g", [16, 1024], F32)
                b_fm = A_("at_b", [16, 1024], F32)
                gT2 = A_("at_gT2", [64, 256], F32)
                bT2 = A_("at_bT2", [64, 256], F32)
                cum2 = A_("at_cum2", [64, 256], F32)
                crev2 = A_("at_crev2", [64, 256], F32)
                gam2 = A_("at_gam2", [64, 256], F32)
                bg2 = A_("at_bg2", [64, 256], F32)
                ecc2 = A_("at_ecc2", [64, 256], F32)
                cumT2 = A_("at_cumT2", [128, 128], F32)
                mUI2 = A_("at_mUI", [64, 64], F32)
                mSL2 = A_("at_mSL", [64, 64], F32)
                bd2 = A_("at_bd2", [128, 2], F32)
                b2up = A_("at_b2up", [2, 128], F32)
                onesr = A_("at_ones", [1, 128], F32)
                l64u = A_("at_l64u", [64, 64], F32)
                l64s = A_("at_l64s", [64, 64], F32)
                idf = A_("at_idf", [128, 128], F32)
                dtb_sb = A_("at_dtb", [16, 1], F32)
                alog_sb = A_("at_alog", [16, 1], F32)
                negea = A_("at_negea", [16, 1], F32)
                nw_sb = A_("at_nw", [128, 1], F32)
                cwsb = A_("at_cw", [128, 64], F32)
                halo_sb = A_("at_halo", [128, 48], F32)
                halo_st = A_("at_halost", [128, 48], F32)
                cfl_sb = A_("at_cfl", [1, 1024], F32)
                eccr = A_("at_eccr", [1, 256], F32)

                scol = {}
                cnt0, cnt1 = 0, 0
                for h in range(16):
                    if (h // 2) % 2 == 0:
                        scol[h] = cnt0; cnt0 += 1
                    else:
                        scol[h] = cnt1; cnt1 += 1

                def bhof(h):
                    return ((h // 2) % 2) * 64

                attv = attc.ap().rearrange("a (r c) -> a r c", r=64)
                trk.dma("q1", mUI2[0:64, :], attv[3], writes=[("mui",)])
                trk.dma("q1", mSL2[0:64, :], attv[0], writes=[("msl",)])
                trk.dma("q1", bd2[:, :], bass.AP(attc, 4 * 4096, [[2, 128], [1, 2]]), writes=[("bd2",)])
                trk.dma("q1", b2up[:, :], bass.AP(attc, 5 * 4096, [[128, 2], [1, 128]]), writes=[("b2up",)])
                trk.dma("q1", l64u[:, :], attv[3], writes=[("l64u",)])
                trk.dma("q1", l64s[:, :], attv[0], writes=[("l64s",)])
                trk.ve(nc.vector.memset, onesr[:], 1.0, writes=[("onesr",)])
                trk.se(nc.scalar.activation, idf[:], cIDT[:], AF.Copy, reads=[("cIDT",)], writes=[("idf",)])
                trk.dma("q1", dtb_sb[:, :], bass.AP(smalls, 0, [[1, 16], [1, 1]]), writes=[("dtb",)])
                trk.dma("q1", alog_sb[:, :], bass.AP(smalls, 64, [[1, 16], [1, 1]]), writes=[("alog",)])
                trk.dma("q1", nw_sb[0:64, :], bass.AP(smalls, 128, [[1, 64], [1, 1]]), writes=[("nw",)])
                trk.dma("q1", nw_sb[64:128, :], bass.AP(smalls, 128, [[1, 64], [1, 1]]), writes=[("nw",)])
                for gi in range(16):
                    trk.dma("q1", cwsb[:, gi * 4:(gi + 1) * 4], convw[gi * 128:(gi + 1) * 128, :],
                            writes=[("cw", gi)])
                trk.pre("p", writes=[("halo_sb",)])
                g.memset(halo_sb[:], 0.0)
                with g.If_eq(r_par, 1):
                    g.reg_mov(r_loop, 1)
                    with g.While(r_loop):
                        g.dma_start(out=pfl[0:1, 7:8], in_=shFlg[0, 7:8]).then_inc(pollsem, 16)
                        g.reg_add(r_pw, r_pw, 16)
                        g.wait_ge(pollsem, r_pw)
                        g.reg_load(r_val, pfl[0:1, 7:8])
                        g.reg_alu(r_loop, r_val, FLAGV + 7, OP.bitwise_xor)
                    g.dma_start(out=halo_sb[:, :], in_=shHalo[:, :]).then_inc(pollsem, 16)
                    g.reg_add(r_pw, r_pw, 16)
                    g.wait_ge(pollsem, r_pw)
                inst = g.engine_nop()
                trk.post("p", inst, writes=[("halo_sb",)])

                stLN = ExitStack()
                ln2xT = stLN.enter_context(nc.sbuf_tensor(un("at_lnT"), [128, 16 * 1024], BF16))
                with ExitStack() as stL:
                    xt = stL.enter_context(nc.sbuf_tensor(un("at_xt"), [128, 2 * D], F32))
                    sq = stL.enter_context(nc.sbuf_tensor(un("at_sq"), [128, D], F32))
                    xn = stL.enter_context(nc.sbuf_tensor(un("at_xn"), [128, D], BF16))
                    lns = stL.enter_context(nc.sbuf_tensor(un("at_lns"), [128, 8], F32))
                    psT = [stL.enter_context(nc.psum_tensor(un("at_psT"), [128, 512], BF16)) for _ in range(2)]
                    for tt in range(NTT):
                        xb = tt % 2
                        xv = xt[:, xb * D:(xb + 1) * D]
                        trk.dma("q0", xv, xsrc[tt * 128:(tt + 1) * 128, :],
                                reads=([(src_key, tt)] if src_key else []), writes=[("at_xt", xb)])
                        trk.ve(nc.vector.tensor_reduce, lns[:, 0:1], xv, axis=AX, op=OP.add,
                               reads=[("at_xt", xb)], writes=[("aln0",)])
                        trk.se(nc.scalar.activation, sq[:], xv, AF.Square,
                               reads=[("at_xt", xb)], writes=[("at_sq",)])
                        trk.ve(nc.vector.tensor_reduce, lns[:, 1:2], sq[:], axis=AX, op=OP.add,
                               reads=[("at_sq",)], writes=[("aln1",)])
                        trk.se(nc.scalar.activation, lns[:, 2:3], lns[:, 0:1], AF.Copy, scale=-1.0 / D,
                               reads=[("aln0",)], writes=[("anegm",)])
                        trk.ve(nc.vector.tensor_scalar, lns[:, 3:4], lns[:, 1:2], 1.0 / D, None, op0=OP.mult,
                               reads=[("aln1",)], writes=[("aex2",)])
                        trk.ve(nc.vector.tensor_tensor, lns[:, 4:5], lns[:, 2:3], lns[:, 2:3], op=OP.mult,
                               reads=[("anegm",)], writes=[("am2",)])
                        trk.ve(nc.vector.tensor_tensor, lns[:, 5:6], lns[:, 3:4], lns[:, 4:5], op=OP.subtract,
                               reads=[("aex2",), ("am2",)], writes=[("avar",)])
                        trk.ve(nc.vector.tensor_scalar, lns[:, 5:6], lns[:, 5:6], 1e-5, None, op0=OP.add,
                               reads=[("avar",)], writes=[("avar",)])
                        trk.ve(nc.vector.reciprocal, lns[:, 6:7], lns[:, 5:6],
                               reads=[("avar",)], writes=[("arv",)])
                        trk.se(nc.scalar.activation, lns[:, 7:8], lns[:, 6:7], AF.Sqrt,
                               reads=[("arv",)], writes=[("arstd",)])
                        trk.ve(nc.vector.tensor_scalar, xn[:], xv, lns[:, 2:3], lns[:, 7:8],
                               op0=OP.add, op1=OP.mult,
                               reads=[("at_xt", xb), ("anegm",), ("arstd",)], writes=[("at_xn",)])
                        for kb in range(4):
                            pb = kb % 2
                            with trk.tg(reads=[("at_xn",), ("cIDT",)], writes=[("at_psT", pb)]) as t:
                                for j in range(4):
                                    k = kb * 4 + j
                                    t.add(nc.tensor.transpose(psT[pb][:, j * 128:(j + 1) * 128],
                                                              xn[:, k * 128:(k + 1) * 128], cIDT[:]))
                            src = bass.AP(psT[pb], 0, [[512, 128], [128, 4], [1, 128]])
                            dst = bass.AP(ln2xT, (kb * 4) * 1024 + tt * 128,
                                          [[16 * 1024, 128], [1024, 4], [1, 128]])
                            trk.ve(nc.vector.tensor_copy, dst, src,
                                   reads=[("at_psT", pb)],
                                   writes=[("lnT", kb * 4 + j2) for j2 in range(4)])
                trk.join()
                if _ACUT == 1:
                    return

                rd_lnT = [("lnT", k) for k in range(16)]
                with ExitStack() as stP:
                    qk_fm = stP.enter_context(nc.sbuf_tensor(un("at_qk"), [128, 8 * 1024], BF16))
                    wsl = [stP.enter_context(nc.sbuf_tensor(un("at_wsl"), [128, 16 * 256], BF16)) for _ in range(2)]
                    cin = stP.enter_context(nc.sbuf_tensor(un("at_cin"), [128, 1028], F32))
                    cvo = stP.enter_context(nc.sbuf_tensor(un("at_cvo"), [128, 1024], F32))
                    sil = stP.enter_context(nc.sbuf_tensor(un("at_sil"), [128, 1024], F32))
                    sq2 = stP.enter_context(nc.sbuf_tensor(un("at_sq2"), [128, 1024], F32))
                    nr2 = stP.enter_context(nc.sbuf_tensor(un("at_nr2"), [2, 1024], F32))
                    vt_s = stP.enter_context(nc.sbuf_tensor(un("at_vts"), [128, 1024], BF16))
                    zt_s = stP.enter_context(nc.sbuf_tensor(un("at_zts"), [128, 1024], BF16))
                    psP = [stP.enter_context(nc.psum_tensor(un("at_psP"), [128, 1024], F32)) for _ in range(2)]
                    psN = stP.enter_context(nc.psum_tensor(un("at_psN"), [2, 512], F32))
                    psB = stP.enter_context(nc.psum_tensor(un("at_psB"), [128, 512], F32))
                    psV = stP.enter_context(nc.psum_tensor(un("at_psV"), [128, 512], BF16))
                    groups = [("qkv", shWqkv, gi, ("shWqkvb",)) for gi in range(8)] + \
                             [("z", shWz, gi, ("shWzb",)) for gi in range(4)]
                    for grpidx, (kind, wsh, gi, wkey) in enumerate(groups):
                        rot = grpidx % 2
                        for k in range(16):
                            trk.dma("q0", wsl[rot][:, k * 256:(k + 1) * 256],
                                    wsh[gi * 256:(gi + 1) * 256, k * 128:(k + 1) * 128],
                                    reads=[wkey], writes=[("wsl", rot, k)], transpose=True)
                        for ct in range(2):
                            gti0 = gi * 2 + ct
                            pp = psP[(grpidx * 2 + ct) % 2]
                            ppk = ("psP", (grpidx * 2 + ct) % 2)
                            rdw = [("wsl", rot, k) for k in range(16)]
                            with trk.tg(reads=rdw + rd_lnT, writes=[ppk]) as t:
                                for k in range(16):
                                    for th2 in range(2):
                                        t.add(nc.tensor.matmul(
                                            pp[:, th2 * 512:(th2 + 1) * 512],
                                            wsl[rot][:, k * 256 + ct * 128:k * 256 + (ct + 1) * 128],
                                            ln2xT[:, k * 1024 + th2 * 512:k * 1024 + (th2 + 1) * 512],
                                            start=(k == 0), stop=(k == 15)))
                            if kind == "z":
                                zi = gi * 2 + ct
                                trk.se(nc.scalar.activation, zt_s[:], pp[:], AF.Silu,
                                       reads=[ppk], writes=[("zt",)])
                                trk.dma("q2", zsp[zi * 128:(zi + 1) * 128, :], zt_s[:],
                                        reads=[("zt",)], writes=[("zsp", zi)])
                                continue
                            gti = gti0
                            trk.se(nc.scalar.activation, cin[:, 3:1027], pp[:], AF.Copy,
                                   reads=[ppk], writes=[("cin",)])
                            trk.ve(nc.vector.tensor_copy, halo_st[:, gti * 3:(gti + 1) * 3],
                                   cin[:, 1024:1027], reads=[("cin",)], writes=[("hst", gti)])
                            trk.ve(nc.vector.tensor_copy, cin[:, 0:3], halo_sb[:, gti * 3:(gti + 1) * 3],
                                   reads=[("halo_sb",)], writes=[("cin",)])
                            trk.ve(nc.vector.tensor_scalar, cvo[:], cin[:, 0:1024],
                                   cwsb[:, gti * 4:gti * 4 + 1], None, op0=OP.mult,
                                   reads=[("cin",), ("cw", gti)], writes=[("cvo",)])
                            for tap in range(1, 4):
                                trk.ve(nc.vector.scalar_tensor_tensor, cvo[:], cin[:, tap:1024 + tap],
                                       cwsb[:, gti * 4 + tap:gti * 4 + tap + 1], cvo[:],
                                       op0=OP.mult, op1=OP.add,
                                       reads=[("cin",), ("cw", gti), ("cvo",)], writes=[("cvo",)])
                            trk.se(nc.scalar.activation, sil[:], cvo[:], AF.Silu,
                                   reads=[("cvo",)], writes=[("sil",)])
                            if gti < 8:
                                trk.se(nc.scalar.activation, sq2[:], sil[:], AF.Square,
                                       reads=[("sil",)], writes=[("sq2",)])
                                for th2 in range(2):
                                    with trk.tg(reads=[("sq2",), ("bd2",)], writes=[("psN",)]) as t:
                                        t.add(nc.tensor.matmul(psN[:, :], bd2[:],
                                                               sq2[:, th2 * 512:(th2 + 1) * 512],
                                                               start=True, stop=True))
                                    trk.ve(nc.vector.tensor_scalar, nr2[:, 0:512], psN[:], 1e-6, None,
                                           op0=OP.add, reads=[("psN",)], writes=[("nr2",)])
                                    trk.ve(nc.vector.reciprocal, nr2[:, 0:512], nr2[:, 0:512],
                                           reads=[("nr2",)], writes=[("nr2",)])
                                    scl = 0.015625 if gti < 4 else 1.0
                                    trk.se(nc.scalar.activation, nr2[:, 0:512], nr2[:, 0:512], AF.Sqrt,
                                           scale=scl, reads=[("nr2",)], writes=[("nr2",)])
                                    with trk.tg(reads=[("nr2",), ("b2up",)], writes=[("psB",)]) as t:
                                        t.add(nc.tensor.matmul(psB[:, :], b2up[:], nr2[:, 0:512],
                                                               start=True, stop=True))
                                    trk.ve(nc.vector.tensor_tensor,
                                           qk_fm[:, gti * 1024 + th2 * 512:gti * 1024 + (th2 + 1) * 512],
                                           sil[:, th2 * 512:(th2 + 1) * 512], psB[:], op=OP.mult,
                                           reads=[("sil",), ("psB",)], writes=[("qk", gti)])
                                dstf = q_f64 if gti < 4 else k_f64
                                dk0 = ("qf64",) if gti < 4 else ("kf64",)
                                hb2 = (gti % 4) * 2
                                trk.ve(nc.vector.tensor_copy, dstf[:, hb2 * 1024:(hb2 + 1) * 1024],
                                       qk_fm[0:64, gti * 1024:(gti + 1) * 1024],
                                       reads=[("qk", gti)], writes=[dk0])
                                for th2 in range(2):
                                    with trk.tg(reads=[("qk", gti), ("cIDT",)], writes=[("psB",)]) as t:
                                        t.add(nc.tensor.matmul(
                                            psB[0:64, :], cIDT[64:128, 64:128],
                                            bass.AP(qk_fm, 64 * 8192 + gti * 1024 + th2 * 512,
                                                    [[8192, 64], [1, 512]]),
                                            start=True, stop=True))
                                    trk.ve(nc.vector.tensor_copy,
                                           dstf[:, (hb2 + 1) * 1024 + th2 * 512:(hb2 + 1) * 1024 + (th2 + 1) * 512],
                                           psB[0:64, :], reads=[("psB",)], writes=[dk0])
                                if gti >= 4:
                                    kt = gti - 4
                                    for cc in range(16):
                                        with trk.tg(reads=[("qk", gti), ("cIDT",)], writes=[("psV",)]) as t:
                                            t.add(nc.tensor.transpose(
                                                psV[0:64, 0:128],
                                                qk_fm[:, gti * 1024 + cc * 64:gti * 1024 + (cc + 1) * 64],
                                                cIDT[:]))
                                        trk.ve(nc.vector.tensor_copy,
                                               k_cm[:, cc * 512 + kt * 128:cc * 512 + (kt + 1) * 128],
                                               psV[0:64, 0:128],
                                               reads=[("psV",)], writes=[("kcm", cc, kt)])
                            else:
                                vi = gti - 8
                                trk.ve(nc.vector.tensor_copy, vt_s[:], sil[:],
                                       reads=[("sil",)], writes=[("vts",)])
                                for cc in range(16):
                                    with trk.tg(reads=[("vts",), ("cIDT",)], writes=[("psV",)]) as t:
                                        t.add(nc.tensor.transpose(psV[0:64, 0:128],
                                                                  vt_s[:, cc * 64:(cc + 1) * 64], cIDT[:]))
                                    trk.ve(nc.vector.tensor_copy,
                                           v_cm[:, cc * 1024 + vi * 128:cc * 1024 + (vi + 1) * 128],
                                           psV[0:64, 0:128],
                                           reads=[("psV",)], writes=[("vcm", cc, vi)])
                    trk.pre("p", reads=[("hst", gtj) for gtj in range(16)])
                    with g.If_eq(r_par, 0):
                        g.dma_start(out=shHalo[:, :], in_=halo_st[:, :]).then_inc(pollsem, 16)
                        g.reg_add(r_pw, r_pw, 16)
                        g.wait_ge(pollsem, r_pw)
                        g.memset(ftile[0:1, 7:8], FLAGV + 7)
                        g.dma_start(out=shFlg[0, 7:8], in_=ftile[0:1, 7:8]).then_inc(pollsem, 16)
                        g.reg_add(r_pw, r_pw, 16)
                        g.wait_ge(pollsem, r_pw)
                    inst = g.engine_nop()
                    trk.post("p", inst, reads=[("hst", gtj) for gtj in range(16)])
                    for k in range(16):
                        trk.dma("q0", wsl[0][:, k * 256:k * 256 + 32],
                                shWba[0:32, k * 128:(k + 1) * 128],
                                reads=[("shWbab",)], writes=[("wsl", 0, k)], transpose=True)
                    with trk.tg(reads=[("wsl", 0, k) for k in range(16)] + rd_lnT, writes=[("psP", 0)]) as t:
                        for k in range(16):
                            for th2 in range(2):
                                t.add(nc.tensor.matmul(psP[0][0:16, th2 * 512:(th2 + 1) * 512],
                                                       wsl[0][:, k * 256:k * 256 + 16],
                                                       ln2xT[:, k * 1024 + th2 * 512:k * 1024 + (th2 + 1) * 512],
                                                       start=(k == 0), stop=(k == 15)))
                    with trk.tg(reads=[("wsl", 0, k) for k in range(16)] + rd_lnT, writes=[("psP", 1)]) as t:
                        for k in range(16):
                            for th2 in range(2):
                                t.add(nc.tensor.matmul(psP[1][0:16, th2 * 512:(th2 + 1) * 512],
                                                       wsl[0][:, k * 256 + 16:k * 256 + 32],
                                                       ln2xT[:, k * 1024 + th2 * 512:k * 1024 + (th2 + 1) * 512],
                                                       start=(k == 0), stop=(k == 15)))
                    trk.se(nc.scalar.activation, b_fm[:], psP[0][0:16, :], AF.Sigmoid,
                           reads=[("psP", 0)], writes=[("bfm",)])
                    trk.ve(nc.vector.tensor_scalar, g_fm[:], psP[1][0:16, :], dtb_sb[:, 0:1], None,
                           op0=OP.add, reads=[("psP", 1), ("dtb",)], writes=[("ga",)])
                    trk.se(nc.scalar.activation, sq2[0:16, :], g_fm[:], AF.Abs,
                           reads=[("ga",)], writes=[("gau",)])
                    trk.se(nc.scalar.activation, sq2[0:16, :], sq2[0:16, :], AF.Exp, scale=-1.0,
                           reads=[("gau",)], writes=[("gau",)])
                    trk.se(nc.scalar.activation, sq2[0:16, :], sq2[0:16, :], AF.Ln, bias=1.0,
                           reads=[("gau",)], writes=[("gau",)])
                    trk.ve(nc.vector.tensor_scalar, sil[0:16, :], g_fm[:], 0.0, None, op0=OP.max,
                           reads=[("ga",)], writes=[("gr",)])
                    trk.ve(nc.vector.tensor_tensor, g_fm[:], sil[0:16, :], sq2[0:16, :], op=OP.add,
                           reads=[("gr",), ("gau",)], writes=[("gfm",)])
                    trk.se(nc.scalar.activation, negea[:], alog_sb[:], AF.Exp,
                           reads=[("alog",)], writes=[("nea",)])
                    trk.ve(nc.vector.tensor_scalar, negea[:], negea[:], -1.0, None, op0=OP.mult,
                           reads=[("nea",)], writes=[("nea",)])
                    trk.ve(nc.vector.tensor_scalar, g_fm[:], g_fm[:], negea[:, 0:1], None, op0=OP.mult,
                           reads=[("gsp",), ("nea",)], writes=[("gfm",)])
                trk.join()
                stLN.close()
                trk.join()
                if _DBGA:
                    trk.dma("q2", d_lnT[:, :], ln2xT[:, 0:1024], reads=[("lnT", 0)])
                    trk.dma("q2", d_q[:, :], q_f64[0:64, :], reads=[("qf64",)])
                    trk.dma("q2", d_k[:, :], k_f64[0:64, :], reads=[("kf64",)])
                    trk.dma("q2", d_vcm[:, :], v_cm[0:64, :],
                            reads=[("vcm", cc2, vi2) for cc2 in range(16) for vi2 in range(8)])
                    trk.dma("q2", d_kcm[:, :], k_cm[0:64, :],
                            reads=[("kcm", cc2, kt2) for cc2 in range(16) for kt2 in range(4)])
                    trk.dma("q2", d_gb[:, 0:1024], g_fm[:, :], reads=[("gfm",)])
                    trk.dma("q2", d_gb[:, 1024:2048], b_fm[:, :], reads=[("bfm",)])
                    trk.dma("q2", d_z[:, :], zsp[0:128, :], reads=[("zsp", 0)])
                if _ACUT == 2:
                    return

                with ExitStack() as stQ:
                    psC = stQ.enter_context(nc.psum_tensor(un("at_psC"), [128, 256], F32))
                    psC2 = stQ.enter_context(nc.psum_tensor(un("at_psC2"), [128, 256], F32))
                    for c in range(16):
                        with trk.tg(reads=[("gfm",), ("idf",)], writes=[("psC",)]) as t:
                            t.add(nc.tensor.transpose(psC[0:64, c * 16:(c + 1) * 16],
                                                      g_fm[:, c * 64:(c + 1) * 64], idf[0:16, 0:16]))
                        with trk.tg(reads=[("bfm",), ("idf",)], writes=[("psC2",)]) as t:
                            t.add(nc.tensor.transpose(psC2[0:64, c * 16:(c + 1) * 16],
                                                      b_fm[:, c * 64:(c + 1) * 64], idf[0:16, 0:16]))
                    trk.ve(nc.vector.tensor_copy, gT2[:], psC[0:64, :], reads=[("psC",)], writes=[("gT2",)])
                    trk.ve(nc.vector.tensor_copy, bT2[:], psC2[0:64, :], reads=[("psC2",)], writes=[("bT2",)])
                    with trk.tg(reads=[("gT2",), ("l64u",)], writes=[("psC",)]) as t:
                        t.add(nc.tensor.matmul(psC[0:64, :], l64u[:],
                                               gT2[0:64, :], start=True, stop=True))
                    with trk.tg(reads=[("gT2",), ("l64s",)], writes=[("psC2",)]) as t:
                        t.add(nc.tensor.matmul(psC2[0:64, :], l64s[:],
                                               gT2[0:64, :], start=True, stop=True))
                    trk.ve(nc.vector.tensor_copy, cum2[:], psC[0:64, :],
                           reads=[("psC",)], writes=[("cum2",)])
                    trk.se(nc.scalar.activation, crev2[:], psC2[0:64, :], AF.Exp,
                           reads=[("psC2",)], writes=[("crev2",)])
                    trk.se(nc.scalar.activation, gam2[:], cum2[:], AF.Exp,
                           reads=[("cum2",)], writes=[("gam2",)])
                    trk.ve(nc.vector.tensor_tensor, bg2[:], bT2[:], gam2[:], op=OP.mult,
                           reads=[("bT2",), ("gam2",)], writes=[("bg2",)])
                    trk.dma("q2", scrd2[0:1, 0:256], cum2[63:64, :], reads=[("cum2",)], writes=[("scr2",)])
                    trk.dma("q2", eccr[:, :], scrd2[0:1, 0:256], reads=[("scr2",)], writes=[("eccr",)])
                    with trk.tg(reads=[("eccr",), ("onesr",)], writes=[("psC",)]) as t:
                        t.add(nc.tensor.matmul(psC[:, :], onesr[:], eccr[:], start=True, stop=True))
                    trk.se(nc.scalar.activation, ecc2[:], psC[0:64, :], AF.Exp,
                           reads=[("psC",)], writes=[("ecc2",)])
                    for half in range(2):
                        with trk.tg(reads=[("cum2",), ("idf",)], writes=[("psC2",)]) as t:
                            t.add(nc.tensor.transpose(psC2[:, half * 64:(half + 1) * 64],
                                                      cum2[0:64, half * 128:(half + 1) * 128],
                                                      idf[0:64, 0:64]))
                    trk.ve(nc.vector.tensor_copy, cumT2[:], psC2[:, 0:128],
                           reads=[("psC2",)], writes=[("cumT2",)])
                trk.join()
                if _DBGA:
                    trk.dma("q2", d_cum[:, 0:256], cum2[0:64, :], reads=[("cum2",)])
                    trk.dma("q2", d_cum[:, 256:512], crev2[0:64, :], reads=[("crev2",)])
                    trk.dma("q2", d_cum[:, 512:768], gam2[0:64, :], reads=[("gam2",)])
                    trk.dma("q2", d_cum[:, 768:1024], bg2[0:64, :], reads=[("bg2",)])
                    trk.dma("q2", d_cum[:, 1024:1280], ecc2[0:64, :], reads=[("ecc2",)])
                    trk.dma("q2", d_cumT[:, :], cumT2[:, :], reads=[("cumT2",)])
                if _ACUT == 3:
                    return

                y_fm = A_("at_y", [128, 8 * 1024], BF16)
                S_run = A_("at_Srun", [64, 16 * 64], F32)
                S_bf = A_("at_Sbf", [64, 16 * 64], BF16)
                ssc_bf = A_("at_ssc", [64, 16 * 64], BF16)
                trk.ve(nc.vector.memset, S_run[:], 0.0, writes=[("Srun",)])
                trk.pre("p", writes=[("Srun",)])
                with g.If_eq(r_par, 1):
                    g.reg_mov(r_loop, 1)
                    with g.While(r_loop):
                        g.dma_start(out=pfl[0:1, 8:9], in_=shFlg[0, 8:9]).then_inc(pollsem, 16)
                        g.reg_add(r_pw, r_pw, 16)
                        g.wait_ge(pollsem, r_pw)
                        g.reg_load(r_val, pfl[0:1, 8:9])
                        g.reg_alu(r_loop, r_val, FLAGV + 8, OP.bitwise_xor)
                    g.dma_start(out=S_run[0:64, :], in_=shS16[0:64, :]).then_inc(pollsem, 16)
                    g.reg_add(r_pw, r_pw, 16)
                    g.wait_ge(pollsem, r_pw)
                inst = g.engine_nop()
                trk.post("p", inst, writes=[("Srun",)])
                trk.ve(nc.vector.tensor_copy, S_bf[:], S_run[:], reads=[("Srun",)], writes=[("Sbf",)])

                if _ACUT == 4:
                    return
                with ExitStack() as stK:
                    K_ = lambda name, shape, dt=F32: stK.enter_context(nc.sbuf_tensor(un(name), shape, dt))
                    rX = [K_("at_rX", [128, 16 * 128], BF16) for _ in range(4)]
                    rP = [K_("at_rP", [128, 16 * 64], BF16) for _ in range(4)]
                    rQg = [K_("at_rQg", [128, 16 * 64], BF16) for _ in range(4)]
                    rKd = [K_("at_rKd", [128, 16 * 64], BF16) for _ in range(4)]
                    sc1 = K_("at_sc1", [128, 1024], F32)
                    cpR = K_("at_cpR", [64, 1024], F32)
                    ssc_f = K_("at_sscf", [64, 1024], F32)
                    e1_sb = K_("at_e1", [128, 1024], F32)
                    e2_sb = K_("at_e2", [128, 1024], F32)
                    gam_sb = K_("at_gamrep", [128, 1024], F32)
                    mkk_sb = K_("at_mkk", [128, 512], BF16)
                    mkq_sb = K_("at_mkq", [128, 512], BF16)
                    negA = K_("at_negA", [128, 16 * 64], BF16)
                    negAT = K_("at_negAT", [128, 16 * 64], BF16)
                    nbuf = K_("at_nb", [128, 16 * 64], BF16)
                    ntbuf = K_("at_ntb", [128, 16 * 64], BF16)
                    tkgT = K_("at_tkgT", [128, 16 * 64], BF16)
                    u_sb = K_("at_u", [128, 16 * 64], BF16)
                    psR = stK.enter_context(nc.psum_tensor(un("at_psR"), [128, 1024], F32))
                    psM = stK.enter_context(nc.psum_tensor(un("at_psM"), [128, 512], F32))
                    psM2 = stK.enter_context(nc.psum_tensor(un("at_psM2"), [128, 512], F32))
                    psS1 = stK.enter_context(nc.psum_tensor(un("at_psS1"), [128, 512], F32))
                    psS2 = stK.enter_context(nc.psum_tensor(un("at_psS2"), [128, 512], F32))
                    psX = [stK.enter_context(nc.psum_tensor(un("at_psX"), [128, 512], F32)) for _ in range(2)]
                    psS1b = psS1.bitcast(BF16)
                    psS2b = psS2.bitcast(BF16)
                    for c in range(_CCUT):
                        p = 0
                        tb = c // 2
                        rb = c % 4
                        chb = c * 16
                        half = c // 8
                        if _P1MASK & 1:
                            trk.dma("q2", scrd2[0:1, 256:1280],
                                    bass.AP(cumT2, ((c % 8) * 16) * 128 + half * 64, [[128, 16], [1, 64]]),
                                    reads=[("cumT2",)], writes=[("cflr",)])
                            trk.dma("q2", cfl_sb[:, :], scrd2[0:1, 256:1280],
                                    reads=[("cflr",)], writes=[("cfl",)])
                        if _P1MASK & 2:
                            with trk.tg(reads=[("cfl",), ("onesr",)], writes=[("psR",)]) as t:
                                for nn2 in range(2):
                                    t.add(nc.tensor.matmul(psR[:, nn2 * 512:(nn2 + 1) * 512], onesr[:],
                                                           cfl_sb[:, nn2 * 512:(nn2 + 1) * 512],
                                                           start=True, stop=True))
                        if _P1MASK & 8:
                            trk.ve(nc.vector.tensor_copy, cpR[:],
                                   bass.AP(psR, 0, [[1024, 64], [1, 1024]]),
                                   reads=[("psR",)], writes=[("cpR",)])
                            trk.se(nc.scalar.activation, gam_sb[0:64, :], cpR[:], AF.Exp,
                                   reads=[("cpR",)], writes=[("gamr",)])
                        edp = e2_sb[p:p + 64, :]
                        mui = bass.AP(mUI2, p * 64, [[64, 64], [0, 16], [1, 64]])
                        msl = bass.AP(mSL2, p * 64, [[64, 64], [0, 16], [1, 64]])
                        if _P1MASK & 4:
                            in1 = bass.AP(cum2, p * 256 + chb, [[256, 64], [1, 16], [0, 64]])
                            trk.ve(nc.vector.tensor_tensor, e2_sb[p:p + 64, :],
                                   cpR[:], in1, op=OP.subtract,
                                   reads=[("cpR",), ("cum2",)], writes=[("e2",)])
                        if _P1MASK & 16:
                            trk.ve(nc.vector.tensor_tensor, sc1[p:p + 64, :], edp, mui, op=OP.mult,
                                   reads=[("e2",), ("mui",)], writes=[("sc1",)])
                            trk.se(nc.scalar.activation, e1_sb[p:p + 64, :], sc1[p:p + 64, :], AF.Exp,
                                   reads=[("sc1",)], writes=[("e1",)])
                        if _P1MASK & 32:
                            trk.ve(nc.vector.scalar_tensor_tensor, sc1[p:p + 64, :], edp, -1.0, msl,
                                   op0=OP.mult, op1=OP.mult,
                                   reads=[("e2",), ("msl",)], writes=[("sc1",)])
                            trk.se(nc.scalar.activation, e2_sb[p:p + 64, :], sc1[p:p + 64, :], AF.Exp,
                                   reads=[("sc1",)], writes=[("e2",)])
                        if _KCUT < 2:
                            continue
                        with trk.tg(reads=[("kf64",)], writes=[("psM",)]) as t:
                            for kh in range(8):
                                ksl = k_f64[:, kh * 1024 + c * 64:kh * 1024 + (c + 1) * 64]
                                t.add(nc.tensor.matmul(psM[0:64, kh * 64:(kh + 1) * 64], ksl, ksl,
                                                      start=True, stop=True))
                        with trk.tg(reads=[("kf64",), ("qf64",)], writes=[("psM2",)]) as t:
                            for kh in range(8):
                                ksl = k_f64[:, kh * 1024 + c * 64:kh * 1024 + (c + 1) * 64]
                                qsl = q_f64[:, kh * 1024 + c * 64:kh * 1024 + (c + 1) * 64]
                                t.add(nc.tensor.matmul(psM2[0:64, kh * 64:(kh + 1) * 64], ksl, qsl,
                                                      start=True, stop=True))
                        trk.ve(nc.vector.tensor_copy, mkk_sb[p:p + 64, :], psM[p:p + 64, :],
                               reads=[("psM",)], writes=[("mkk",)])
                        trk.ve(nc.vector.tensor_copy, mkq_sb[p:p + 64, :], psM2[p:p + 64, :],
                               reads=[("psM2",)], writes=[("mkq",)])
                        mkqb = bass.AP(mkq_sb, 0, [[512, 64], [64, 8], [1, 64]])
                        trk.ve(nc.vector.tensor_tensor, sc1[p:p + 64, :], e1_sb[p:p + 64, :], mui, op=OP.mult,
                               reads=[("e1",), ("mui",)], writes=[("sc1",)])
                        for hl in range(2):
                            trk.ve(nc.vector.tensor_tensor,
                                   bass.AP(rP[rb], hl * 64, [[1024, 64], [128, 8], [1, 64]]),
                                   bass.AP(sc1, hl * 64, [[1024, 64], [128, 8], [1, 64]]), mkqb,
                                   op=OP.mult, reads=[("sc1",), ("mkq",)], writes=[("rP", rb)])
                        btb = bass.AP(bT2, p * 256 + chb, [[256, 64], [1, 16], [0, 64]])
                        trk.ve(nc.vector.tensor_tensor, sc1[p:p + 64, :], e2_sb[p:p + 64, :], btb, op=OP.mult,
                               reads=[("e2",), ("bT2",)], writes=[("sc1",)])
                        trk.ve(nc.vector.tensor_tensor, sc1[p:p + 64, :], sc1[p:p + 64, :], msl, op=OP.mult,
                               reads=[("sc1",), ("msl",)], writes=[("sc1",)])
                        mkkb = bass.AP(mkk_sb, 0, [[512, 64], [64, 8], [1, 64]])
                        for hl in range(2):
                            trk.ve(nc.vector.scalar_tensor_tensor,
                                   bass.AP(negA, hl * 64, [[1024, 64], [128, 8], [1, 64]]),
                                   bass.AP(sc1, hl * 64, [[1024, 64], [128, 8], [1, 64]]),
                                   -1.0, mkkb, op0=OP.mult, op1=OP.mult,
                                   reads=[("sc1",), ("mkk",)], writes=[("negAk",)])
                        with trk.tg(reads=[("negAk",), ("cIDT",)], writes=[("psS1",)]) as t:
                            for h in range(16):
                                t.add(nc.tensor.transpose(psS1b[p:p + 64, h * 64:(h + 1) * 64],
                                                          negA[p:p + 64, h * 64:(h + 1) * 64],
                                                          cIDT[p:p + 64, p:p + 64]))
                        trk.ve(nc.vector.tensor_copy, negAT[p:p + 64, :], psS1b[p:p + 64, :],
                               reads=[("psS1",)], writes=[("negATk",)])
                        vsl = bass.AP(v_cm, c * 1024, [[16 * 1024, 64], [1, 1024]])
                        btc = bass.AP(bT2, p * 256 + chb, [[256, 64], [1, 16], [0, 64]])
                        trk.ve(nc.vector.tensor_tensor,
                               bass.AP(rX[rb], p * 2048, [[2048, 64], [128, 16], [1, 64]]),
                               vsl, btc, op=OP.mult,
                               reads=[("vcm", c, vi2) for vi2 in range(8)] + [("bT2",)],
                               writes=[("rX", rb)])
                        ksl2 = bass.AP(k_cm, c * 512, [[16 * 512, 64], [64, 8], [1, 64]])
                        for hl in range(2):
                            trk.ve(nc.vector.tensor_tensor,
                                   bass.AP(rX[rb], 64 + hl * 128, [[2048, 64], [256, 8], [1, 64]]),
                                   ksl2, bass.AP(bg2, chb + hl, [[256, 64], [2, 8], [0, 64]]),
                                   op=OP.mult,
                                   reads=[("kcm", c, kt2) for kt2 in range(4)] + [("bg2",)],
                                   writes=[("rX", rb)])
                            trk.ve(nc.vector.tensor_tensor,
                                   bass.AP(rKd[rb], hl * 64, [[1024, 64], [128, 8], [1, 64]]),
                                   ksl2, bass.AP(crev2, chb + hl, [[256, 64], [2, 8], [0, 64]]),
                                   op=OP.mult,
                                   reads=[("kcm", c, kt2) for kt2 in range(4)] + [("crev2",)],
                                   writes=[("rKd", rb)])
                            trk.ve(nc.vector.tensor_tensor,
                                   bass.AP(rQg[rb], hl * 64, [[1024, 64], [128, 8], [1, 64]]),
                                   bass.AP(q_f64, c * 64, [[8192, 64], [1024, 8], [1, 64]]),
                                   bass.AP(gam_sb, hl * 64, [[1024, 64], [128, 8], [1, 64]]),
                                   op=OP.mult,
                                   reads=[("qf64",), ("gamr",)], writes=[("rQg", rb)])
                        if _KCUT < 3:
                            continue
                        curN, curNT = negA, negAT
                        ckN, ckNT = ("negAk",), ("negATk",)
                        for lvl in range(6):
                            for wave in range(2):
                                for hh in range(8):
                                    h = wave * 8 + hh
                                    bank = hh // 4
                                    win = (hh % 4) * 128
                                    with trk.tg(reads=[ckN, ckNT, ("rX", rb)],
                                                writes=[("psX", bank)]) as t:
                                        t.add(nc.tensor.matmul(psX[bank][p:p + 64, win:win + 128],
                                                               curNT[p:p + 64, h * 64:(h + 1) * 64],
                                                               rX[rb][p:p + 64, h * 128:(h + 1) * 128],
                                                               start=True, stop=True))
                                for bank in range(2):
                                    trk.ve(nc.vector.tensor_tensor,
                                           rX[rb][p:p + 64, wave * 1024 + bank * 512:wave * 1024 + (bank + 1) * 512],
                                           psX[bank][p:p + 64, :],
                                           rX[rb][p:p + 64, wave * 1024 + bank * 512:wave * 1024 + (bank + 1) * 512],
                                           op=OP.add,
                                           reads=[("psX", bank), ("rX", rb)], writes=[("rX", rb)])
                            if lvl < 5:
                                if lvl % 2 == 0:
                                    nn, nt, knn, knt = nbuf, ntbuf, ("nb", 0), ("ntb", 0)
                                else:
                                    nn, nt, knn, knt = negA, negAT, ("negAk",), ("negATk",)
                                with trk.tg(reads=[ckN, ckNT], writes=[("psS1",)]) as t:
                                    for h in range(8):
                                        t.add(nc.tensor.matmul(psS1[p:p + 64, h * 64:(h + 1) * 64],
                                                               curNT[p:p + 64, h * 64:(h + 1) * 64],
                                                               curN[p:p + 64, h * 64:(h + 1) * 64],
                                                               start=True, stop=True))
                                with trk.tg(reads=[ckN, ckNT], writes=[("psS2",)]) as t:
                                    for h in range(8, 16):
                                        t.add(nc.tensor.matmul(psS2[p:p + 64, (h - 8) * 64:(h - 7) * 64],
                                                               curNT[p:p + 64, h * 64:(h + 1) * 64],
                                                               curN[p:p + 64, h * 64:(h + 1) * 64],
                                                               start=True, stop=True))
                                with trk.tg(reads=[ckN, ckNT], writes=[("psM",)]) as t:
                                    for h in range(8):
                                        t.add(nc.tensor.matmul(psM[p:p + 64, h * 64:(h + 1) * 64],
                                                               curN[p:p + 64, h * 64:(h + 1) * 64],
                                                               curNT[p:p + 64, h * 64:(h + 1) * 64],
                                                               start=True, stop=True))
                                with trk.tg(reads=[ckN, ckNT], writes=[("psM2",)]) as t:
                                    for h in range(8, 16):
                                        t.add(nc.tensor.matmul(psM2[p:p + 64, (h - 8) * 64:(h - 7) * 64],
                                                               curN[p:p + 64, h * 64:(h + 1) * 64],
                                                               curNT[p:p + 64, h * 64:(h + 1) * 64],
                                                               start=True, stop=True))
                                trk.ve(nc.vector.tensor_copy, nn[p:p + 64, 0:512], psS1[p:p + 64, :],
                                       reads=[("psS1",)], writes=[knn])
                                trk.ve(nc.vector.tensor_copy, nn[p:p + 64, 512:1024], psS2[p:p + 64, :],
                                       reads=[("psS2",)], writes=[knn])
                                trk.ve(nc.vector.tensor_copy, nt[p:p + 64, 0:512], psM[p:p + 64, :],
                                       reads=[("psM",)], writes=[knt])
                                trk.ve(nc.vector.tensor_copy, nt[p:p + 64, 512:1024], psM2[p:p + 64, :],
                                       reads=[("psM2",)], writes=[knt])
                                curN, curNT = nn, nt
                                ckN, ckNT = knn, knt
                        if _KCUT < 4:
                            continue
                        with trk.tg(reads=[("rX", rb), ("cIDT",)], writes=[("psS1",)]) as t:
                            for h in range(16):
                                t.add(nc.tensor.transpose(
                                    psS1b[0:64, h * 64:(h + 1) * 64],
                                    rX[rb][0:64, h * 128 + 64:(h + 1) * 128],
                                    cIDT[0:64, 0:64]))
                        trk.se(nc.scalar.activation, tkgT[0:64, :], psS1b[0:64, :], AF.Copy, scale=-1.0,
                               reads=[("psS1",)], writes=[("tkgTk",)])
                        for wave in range(2):
                            for hh in range(8):
                                h = wave * 8 + hh
                                bh = bhof(h)
                                bank = hh // 4
                                win = (hh % 4) * 64
                                with trk.tg(reads=[("tkgTk",), ("Sbf",)],
                                            writes=[("psX", bank)]) as t:
                                    t.add(nc.tensor.matmul(psX[bank][0:64, win:win + 64],
                                                           tkgT[0:64, h * 64:(h + 1) * 64],
                                                           S_bf[0:64, h * 64:(h + 1) * 64],
                                                           start=True, stop=True))
                            for bank in range(2):
                                h0 = wave * 8 + bank * 4
                                trk.ve(nc.vector.tensor_tensor,
                                       bass.AP(u_sb, (wave * 512 + bank * 256),
                                               [[1024, 64], [64, 4], [1, 64]]),
                                       bass.AP(psX[bank], 0, [[512, 64], [64, 4], [1, 64]]),
                                       bass.AP(rX[rb], h0 * 128, [[2048, 64], [128, 4], [1, 64]]),
                                       op=OP.add,
                                       reads=[("psX", bank), ("rX", rb)], writes=[("u",)])
                        with trk.tg(reads=[("Sbf",), ("rQg", rb), ("u",), ("rP", rb)], writes=[("psS1",)]) as t:
                            for h in range(0, 16, 2):
                                win = (h // 2) * 64
                                t.add(nc.tensor.matmul(psS1[0:64, win:win + 64],
                                                       S_bf[0:64, h * 64:(h + 1) * 64],
                                                       rQg[rb][0:64, h * 64:(h + 1) * 64],
                                                       start=True, stop=False))
                                t.add(nc.tensor.matmul(psS1[0:64, win:win + 64],
                                                       u_sb[0:64, h * 64:(h + 1) * 64],
                                                       rP[rb][0:64, h * 64:(h + 1) * 64],
                                                       start=False, stop=True))
                        with trk.tg(reads=[("Sbf",), ("rQg", rb), ("u",), ("rP", rb)], writes=[("psS2",)]) as t:
                            for h in range(1, 16, 2):
                                win = (h // 2) * 64
                                t.add(nc.tensor.matmul(psS2[64:128, win:win + 64],
                                                       S_bf[0:64, h * 64:(h + 1) * 64],
                                                       rQg[rb][0:64, h * 64:(h + 1) * 64],
                                                       start=True, stop=False))
                                t.add(nc.tensor.matmul(psS2[64:128, win:win + 64],
                                                       u_sb[0:64, h * 64:(h + 1) * 64],
                                                       rP[rb][0:64, h * 64:(h + 1) * 64],
                                                       start=False, stop=True))
                        src0 = bass.AP(psS1, 0, [[512, 64], [64, 8], [1, 64]])
                        dst0 = bass.AP(y_fm, c * 64, [[8192, 64], [1024, 8], [1, 64]])
                        trk.ve(nc.vector.tensor_copy, dst0, src0,
                               reads=[("psS1",)], writes=[("y", c, 0)])
                        src1 = bass.AP(psS2, 64 * 512, [[512, 64], [64, 8], [1, 64]])
                        dst1 = bass.AP(y_fm, 64 * 8192 + c * 64, [[8192, 64], [1024, 8], [1, 64]])
                        trk.ve(nc.vector.tensor_copy, dst1, src1,
                               reads=[("psS2",)], writes=[("y", c, 1)])
                        if _KCUT < 5:
                            continue
                        trk.ve(nc.vector.tensor_tensor, ssc_f[0:64, :], S_run[0:64, :],
                               bass.AP(ecc2, chb, [[256, 64], [1, 16], [0, 64]]), op=OP.mult,
                               reads=[("Srun",), ("ecc2",)], writes=[("ssc",)])
                        with trk.tg(reads=[("rKd", rb), ("u",)], writes=[("psM",)]) as t:
                            for h in range(8):
                                win = h * 64
                                t.add(nc.tensor.matmul(psM[0:64, win:win + 64],
                                                       rKd[rb][0:64, h * 64:(h + 1) * 64],
                                                       u_sb[0:64, h * 64:(h + 1) * 64],
                                                       start=True, stop=True))
                        with trk.tg(reads=[("rKd", rb), ("u",)], writes=[("psM2",)]) as t:
                            for h in range(8, 16):
                                win = h * 64
                                t.add(nc.tensor.matmul(psM2[0:64, win - 512:win - 512 + 64],
                                                       rKd[rb][0:64, h * 64:(h + 1) * 64],
                                                       u_sb[0:64, h * 64:(h + 1) * 64],
                                                       start=True, stop=True))
                        trk.ve(nc.vector.tensor_tensor, S_run[0:64, 0:512], psM[0:64, :],
                               ssc_f[0:64, 0:512], op=OP.add,
                               reads=[("psM",), ("ssc",)], writes=[("Srun",)])
                        trk.ve(nc.vector.tensor_tensor, S_run[0:64, 512:1024], psM2[0:64, :],
                               ssc_f[0:64, 512:1024], op=OP.add,
                               reads=[("psM2",), ("ssc",)], writes=[("Srun",)])
                        trk.ve(nc.vector.tensor_copy, S_bf[0:64, :], S_run[0:64, :],
                               reads=[("Srun",)], writes=[("Sbf",)])
                        if _DBGA:
                            trk.dma("q2", d_cU[:, c * 1024:(c + 1) * 1024], u_sb[0:64, :], reads=[("u",)])
                            trk.dma("q2", d_cS2[:, c * 1024:(c + 1) * 1024], S_run[0:64, :], reads=[("Srun",)])
                            trk.dma("q2", d_cE1[:, c * 1024:(c + 1) * 1024], e1_sb[0:64, :], reads=[("e1",)])
                            trk.dma("q2", d_cE2[:, c * 1024:(c + 1) * 1024], e2_sb[0:64, :], reads=[("e2",)])
                            trk.dma("q2", d_cP[:, c * 1024:(c + 1) * 1024], rP[rb][0:64, :], reads=[("rP", rb)])
                            trk.dma("q2", d_cQ[:, c * 1024:(c + 1) * 1024], rQg[rb][0:64, :], reads=[("rQg", rb)])
                        if _DBGA and c in (0, 1):
                            d_cx = d_c0 if c == 0 else d_c1
                            trk.dma("q2", d_cx[:, 0:1024], u_sb[:, :], reads=[("u",)])
                            trk.dma("q2", d_cx[:, 1024:2048], tkgT[:, :], reads=[("tkgTk",)])
                            trk.dma("q2", d_cx[:, 2048:3072], negA[:, :], reads=[("negAk",)])
                            trk.dma("q2", d_cx[:, 3072:4096], negAT[:, :], reads=[("negATk",)])
                            trk.dma("q2", d_cx[:, 4096:5120], rKd[rb][:, :], reads=[("rKd", rb)])
                            trk.dma("q2", d_cx[:, 5120:6144], rQg[rb][:, :], reads=[("rQg", rb)])
                            trk.dma("q2", d_cx[:, 6144:7168], rP[rb][:, :], reads=[("rP", rb)])
                            trk.dma("q2", d_cx[0:64, 7168:8192], S_run[0:64, :], reads=[("Srun",)])
                    trk.pre("p", reads=[("Srun",)])
                    with g.If_eq(r_par, 0):
                        g.dma_start(out=shS16[0:64, :], in_=S_run[0:64, :]).then_inc(pollsem, 16)
                        g.reg_add(r_pw, r_pw, 16)
                        g.wait_ge(pollsem, r_pw)
                        g.memset(ftile[0:1, 8:9], FLAGV + 8)
                        g.dma_start(out=shFlg[0, 8:9], in_=ftile[0:1, 8:9]).then_inc(pollsem, 16)
                        g.reg_add(r_pw, r_pw, 16)
                        g.wait_ge(pollsem, r_pw)
                    inst = g.engine_nop()
                    trk.post("p", inst, reads=[("Srun",)])
                trk.join()
                if _DBGA:
                    trk.dma("q2", d_y[:, :], y_fm[:, :],
                            reads=[("y", c2, ph2) for c2 in range(16) for ph2 in range(2)])
                    trk.dma("q2", d_S[:, :], S_run[0:64, :], reads=[("Srun",)])
                if _ACUT == 5:
                    return

                with ExitStack() as stF:
                    wos = stF.enter_context(nc.sbuf_tensor(un("at_wos"), [128, 8 * 2048], BF16))
                    yq = stF.enter_context(nc.sbuf_tensor(un("at_yq"), [128, 8 * 1024], BF16))
                    sqf = stF.enter_context(nc.sbuf_tensor(un("at_sqf"), [128, 1024], F32))
                    nrf = stF.enter_context(nc.sbuf_tensor(un("at_nrf"), [2, 512], F32))
                    t1f = stF.enter_context(nc.sbuf_tensor(un("at_t1f"), [128, 512], F32))
                    zr = stF.enter_context(nc.sbuf_tensor(un("at_zr"), [128, 1024], BF16))
                    xrf = stF.enter_context(nc.sbuf_tensor(un("at_xrf"), [128, D], F32))
                    xof = stF.enter_context(nc.sbuf_tensor(un("at_xof"), [128, D], F32))
                    psF2 = stF.enter_context(nc.psum_tensor(un("at_psF2"), [2, 512], F32))
                    psB2 = stF.enter_context(nc.psum_tensor(un("at_psB2"), [128, 512], F32))
                    psW = [stF.enter_context(nc.psum_tensor(un("at_psW"), [128, 1024], F32)) for _ in range(2)]
                    rd_y = [("y", c2, ph2) for c2 in range(16) for ph2 in range(2)]
                    for yt in range(8):
                        ysl = y_fm[:, yt * 1024:(yt + 1) * 1024]
                        trk.se(nc.scalar.activation, sqf[:], ysl, AF.Square,
                               reads=rd_y, writes=[("sqf",)])
                        trk.dma("q1", zr[:, :], zsp[yt * 128:(yt + 1) * 128, :],
                                reads=[("zsp", yt)], writes=[("zr",)])
                        for th2 in range(2):
                            with trk.tg(reads=[("sqf",), ("bd2",)], writes=[("psF2",)]) as t:
                                t.add(nc.tensor.matmul(psF2[:, :], bd2[:],
                                                       sqf[:, th2 * 512:(th2 + 1) * 512],
                                                       start=True, stop=True))
                            trk.ve(nc.vector.tensor_scalar, nrf[:, :], psF2[:, :], 1.0 / 64.0, 1e-6,
                                   op0=OP.mult, op1=OP.add, reads=[("psF2",)], writes=[("nrf",)])
                            trk.ve(nc.vector.reciprocal, nrf[:, :], nrf[:, :],
                                   reads=[("nrf",)], writes=[("nrf",)])
                            trk.se(nc.scalar.activation, nrf[:, :], nrf[:, :], AF.Sqrt,
                                   reads=[("nrf",)], writes=[("nrf",)])
                            with trk.tg(reads=[("nrf",), ("b2up",)], writes=[("psB2",)]) as t:
                                t.add(nc.tensor.matmul(psB2[:, :], b2up[:], nrf[:, :],
                                                       start=True, stop=True))
                            trk.ve(nc.vector.tensor_tensor, t1f[:, :],
                                   ysl[:, th2 * 512:(th2 + 1) * 512] if False else y_fm[:, yt * 1024 + th2 * 512:yt * 1024 + (th2 + 1) * 512],
                                   psB2[:, :], op=OP.mult,
                                   reads=rd_y + [("psB2",)], writes=[("t1f",)])
                            trk.ve(nc.vector.tensor_scalar, t1f[:, :], t1f[:, :], nw_sb[:, 0:1], None,
                                   op0=OP.mult, reads=[("t1f",), ("nw",)], writes=[("t1f",)])
                            trk.ve(nc.vector.tensor_tensor,
                                   yq[:, yt * 1024 + th2 * 512:yt * 1024 + (th2 + 1) * 512],
                                   t1f[:, :], zr[:, th2 * 512:(th2 + 1) * 512],
                                   op=OP.mult, reads=[("t1f",), ("zr",)], writes=[("yqk", yt)])
                    for k in range(8):
                        trk.dma("q0", wos[:, k * 2048:(k + 1) * 2048],
                                shWout[0:2048, k * 128:(k + 1) * 128],
                                reads=[("shWoutb",)], writes=[("wos", k)], transpose=True)
                    for m in range(8):
                        pw = psW[m % 2]
                        pwk = ("psW", m % 2)
                        rdw = [("wos", k2) for k2 in range(8)] + [("yqk", y2) for y2 in range(8)]
                        trk.dma("q1", xrf[:, :], xsrc[m * 128:(m + 1) * 128, :],
                                reads=([(src_key, m)] if src_key else []), writes=[("xrf",)])
                        for half in range(2):
                            with trk.tg(reads=rdw, writes=[pwk]) as t:
                                for k in range(8):
                                    for nn2 in range(2):
                                        t.add(nc.tensor.matmul(
                                            pw[:, nn2 * 512:(nn2 + 1) * 512],
                                            yq[:, k * 1024 + m * 128:k * 1024 + (m + 1) * 128],
                                            wos[:, k * 2048 + half * 1024 + nn2 * 512:k * 2048 + half * 1024 + (nn2 + 1) * 512],
                                            start=(k == 0), stop=(k == 7)))
                            trk.ve(nc.vector.tensor_tensor, xof[:, half * 1024:(half + 1) * 1024],
                                   pw[:], xrf[:, half * 1024:(half + 1) * 1024], op=OP.add,
                                   reads=[pwk, ("xrf",)], writes=[("xofk", half)])
                        trk.dma("q2", xdst[m * 128:(m + 1) * 128, :], xof[:],
                                reads=[("xofk", 0), ("xofk", 1)], writes=[(out_key, m)])
                trk.join()

        if _DBG:
            trk.dma("q2", d_swcol[:, :], swcol[:, 0:8], reads=[("swb", 0), ("swb", 1), ("swb", 2)])
            trk.dma("q2", d_wg[:, :], shWg[0:128, :], reads=[("shWgq",)])
            trk.dma("q2", d_wd[:, :], shWd[0:128, 0:1024], reads=[("shWdq",)])
        trk.join()
        stPP.close()
        trk.join()
        if phases == "ffn1":
            ffn(xs, xo, "XO", None)
        elif phases == "cast":
            ffn(xs, xo, "XO", None)
            pp_cast_attn()
        elif phases == "attn":
            ffn(xs, xsp1, "X1", None)
            pp_cast_attn()
            attention(xsp1, "X1", xo, "XO")
        elif phases == "attnonly":
            pp_cast_attn()
            attention(xsp1, None, xo, "XO")
        elif phases == "noattn":
            ffn(xs, xsp1, "X1", None)
            ffn(xsp1, xo, "XO", "X1")
        else:
            ffn(xs, xsp1, "X1", None)
            pp_cast_attn()
            attention(xsp1, "X1", xsp2, "X2")
            ffn(xsp2, xo, "XO", "X2")
        trk.wait_all_on("p")

    return nc


def kernel(L0_attn_x, L0_attn_W_qkv, L0_attn_W_z, L0_attn_W_b, L0_attn_W_a,
           L0_attn_W_out, L0_attn_conv_w, L0_attn_dt_bias, L0_attn_A_log,
           L0_attn_norm_weight, L0_ffn_w_gate, L0_ffn_w_up, L0_ffn_w_down):
    x = np.ascontiguousarray(np.asarray(L0_attn_x, dtype=np.float32)).reshape(8192, D)
    wg = np.ascontiguousarray(np.asarray(L0_ffn_w_gate, dtype=np.float32))
    wu = np.ascontiguousarray(np.asarray(L0_ffn_w_up, dtype=np.float32))
    wd = np.ascontiguousarray(np.asarray(L0_ffn_w_down, dtype=np.float32))
    wqkv = np.ascontiguousarray(np.asarray(L0_attn_W_qkv, dtype=np.float32))
    wz = np.ascontiguousarray(np.asarray(L0_attn_W_z, dtype=np.float32))
    wb = np.asarray(L0_attn_W_b, dtype=np.float32)
    wa = np.asarray(L0_attn_W_a, dtype=np.float32)
    wout = np.ascontiguousarray(np.asarray(L0_attn_W_out, dtype=np.float32))
    cw = np.ascontiguousarray(np.asarray(L0_attn_conv_w, dtype=np.float32).reshape(D, 4))
    sm = np.zeros((3, 64), np.float32)
    sm[0, :16] = np.asarray(L0_attn_dt_bias, dtype=np.float32)
    sm[1, :16] = np.asarray(L0_attn_A_log, dtype=np.float32)
    sm[2, :64] = np.asarray(L0_attn_norm_weight, dtype=np.float32)
    wba = np.ascontiguousarray(np.concatenate([wb, wa], axis=0))

    H2048 = _hadamard(2048).astype(ml_dtypes.bfloat16)
    B1 = np.kron(np.eye(2), _hadamard(64)).astype(ml_dtypes.bfloat16)
    H128 = _hadamard(128).astype(ml_dtypes.bfloat16)
    IDT = np.eye(128).astype(ml_dtypes.bfloat16)
    attcn = np.zeros((6, 64 * 64), np.float32)
    attcn[0] = np.tril(np.ones((64, 64)), -1).reshape(-1)
    attcn[1] = np.tril(np.ones((64, 64)), 0).reshape(-1)
    attcn[2] = np.triu(np.ones((64, 64)), 1).reshape(-1)
    attcn[3] = np.triu(np.ones((64, 64)), 0).reshape(-1)
    bd2c = np.zeros((128, 2), np.float32)
    bd2c[0:64, 0] = 1.0
    bd2c[64:128, 1] = 1.0
    attcn[4, 0:256] = bd2c.reshape(-1)
    attcn[5, 0:256] = bd2c.T.reshape(-1)

    in_maps = []
    for c in range(8):
        par = c % 2
        m = dict(
            xs=x[c * TPC:(c + 1) * TPC],
            wg_raw=wg[par * (IFF // 2):(par + 1) * (IFF // 2)],
            wu_raw=wu[par * (IFF // 2):(par + 1) * (IFF // 2)],
            wd_raw=wd[par * (D // 2):(par + 1) * (D // 2)],
            wqkv_raw=wqkv[par * 1024:(par + 1) * 1024],
            wz_raw=wz[par * 512:(par + 1) * 512],
            wout_raw=wout[par * 1024:(par + 1) * 1024],
            wba_raw=wba,
            convw=cw,
            smalls=sm,
            h2048c=H2048, b1c=B1, h128c=H128, idtc=IDT, attc=attcn,
        )
        in_maps.append(m)

    nc = build_program(_PHASES)
    res = run_bass_kernel_spmd(nc, in_maps, core_ids=list(range(8)))
    global _LAST_RES
    _LAST_RES = res
    out = np.concatenate([np.asarray(r["xo"]) for r in res.results], axis=0)
    return out.reshape(4, 2048, D).astype(np.float32)


# version 14
# speedup vs baseline: 1.0443x; 1.0023x over previous
import numpy as np
import ml_dtypes
import concourse.bass as bass
import concourse.mybir as mybir
from concourse.bass_utils import run_bass_kernel_spmd
from contextlib import ExitStack

F32 = mybir.dt.float32
BF16 = mybir.dt.bfloat16
I8 = mybir.dt.int8
I32 = mybir.dt.int32
AF = mybir.ActivationFunctionType
OP = mybir.AluOpType
AX = mybir.AxisListType.X

MAGIC = float(np.float32(1.5 * 2 ** 23))
FLAGV = 0x5A5A0000

D = 2048
IFF = 8192
TPC = 1024
NTT = 8
NH, DK, DV, NKH = 16, 64, 64, 8
CCH, NCHK = 64, 16
C2048 = 1.0 / float(np.sqrt(2048.0))
C8192 = 1.0 / float(np.sqrt(8192.0))

_PHASES = "all"
_ACUT = None
_CCUT = 16
_KCUT = 5
_P1MASK = 63
_DBG = False
_DBGA = False


def _hadamard(n):
    i = np.arange(n)
    a = i[:, None] & i[None, :]
    pc = np.zeros((n, n), dtype=np.int64)
    while a.any():
        pc += a & 1
        a >>= 1
    return np.where(pc % 2 == 0, 1.0, -1.0).astype(np.float32)


class Trk:
    COMP = ("v", "s", "t", "p")

    def __init__(self, nc, st):
        self.nc = nc
        self.eng = {"v": nc.vector, "s": nc.scalar, "t": nc.tensor, "p": nc.gpsimd,
                    "q0": nc.sync, "q1": nc.scalar, "q2": nc.gpsimd}
        self.sem = {}
        for d in ("v", "s", "t", "p", "q0", "q1", "q2"):
            self.sem[d] = st.enter_context(nc.semaphore(f"trk_{d}"))
        self.cnt = {d: 0 for d in self.sem}
        self.lastw = {}
        self.readers = {}
        self.waited = {}

    def _wait(self, we, dom, val):
        if val <= 0 or val <= self.waited.get((we, dom), 0):
            return
        self.eng[we].wait_ge(self.sem[dom], val)
        self.waited[(we, dom)] = val

    def pre(self, e, reads=(), writes=()):
        for k in reads:
            lw = self.lastw.get(k)
            if lw is not None and not (lw[0] == e == "t"):
                self._wait(e, lw[0], lw[1])
        for k in writes:
            lw = self.lastw.get(k)
            if lw is not None and not (lw[0] == e == "t"):
                self._wait(e, lw[0], lw[1])
            for (re, rv) in self.readers.get(k, ()):
                if not (re == e == "t"):
                    self._wait(e, re, rv)

    def post(self, e, inst, reads=(), writes=()):
        amt = 16 if e in ("q0", "q1", "q2") else 1
        self.cnt[e] += amt
        inst.then_inc(self.sem[e], amt)
        cur = (e, self.cnt[e])
        for k in writes:
            self.lastw[k] = cur
            self.readers[k] = []
        for k in reads:
            self.readers.setdefault(k, []).append(cur)

    def ve(self, fn, *args, reads=(), writes=(), **kw):
        self.pre("v", reads, writes)
        inst = fn(*args, **kw)
        self.post("v", inst, reads, writes)
        return inst

    def se(self, fn, *args, reads=(), writes=(), **kw):
        self.pre("s", reads, writes)
        inst = fn(*args, **kw)
        self.post("s", inst, reads, writes)
        return inst

    def po(self, fn, *args, reads=(), writes=(), **kw):
        self.pre("p", reads, writes)
        inst = fn(*args, **kw)
        self.post("p", inst, reads, writes)
        return inst

    class TGroup:
        def __init__(self, trk, reads, writes):
            self.trk, self.reads, self.writes = trk, tuple(reads), tuple(writes)
            self.last = None

        def __enter__(self):
            self.trk.pre("t", self.reads, self.writes)
            return self

        def add(self, inst):
            self.last = inst
            return inst

        def __exit__(self, et, ev, tb):
            if et is None:
                assert self.last is not None
                self.trk.post("t", self.last, self.reads, self.writes)

    def tg(self, reads=(), writes=()):
        return Trk.TGroup(self, reads, writes)

    def dma(self, q, out, in_, reads=(), writes=(), transpose=False):
        we = {"q0": "q0", "q1": "s", "q2": "p"}[q]
        self.pre(we, reads, writes)
        inst = self.eng[q].dma_start(out=out, in_=in_, transpose=transpose)
        self.post(q, inst, reads, writes)
        return inst

    def join(self):
        for e in self.COMP + ("q0",):
            for d, c in self.cnt.items():
                if d != e:
                    self._wait(e, d, c)

    def wait_all_on(self, e):
        for d, c in self.cnt.items():
            if d != e:
                self._wait(e, d, c)


def build_program(phases="all"):
    nc = bass.Bass("TRN2", target_bir_lowering=False, num_devices=8,
                   detect_race_conditions=False)

    xs = nc.dram_tensor("xs", [TPC, D], F32, kind="ExternalInput")
    xo = nc.dram_tensor("xo", [TPC, D], F32, kind="ExternalOutput")
    wg_raw = nc.dram_tensor("wg_raw", [IFF // 2, D], F32, kind="ExternalInput")
    wu_raw = nc.dram_tensor("wu_raw", [IFF // 2, D], F32, kind="ExternalInput")
    wd_raw = nc.dram_tensor("wd_raw", [D // 2, IFF], F32, kind="ExternalInput")
    wqkv_raw = nc.dram_tensor("wqkv_raw", [D // 2, D], F32, kind="ExternalInput")
    wz_raw = nc.dram_tensor("wz_raw", [512, D], F32, kind="ExternalInput")
    wout_raw = nc.dram_tensor("wout_raw", [D // 2, 1024], F32, kind="ExternalInput")
    wba_raw = nc.dram_tensor("wba_raw", [32, D], F32, kind="ExternalInput")
    convw = nc.dram_tensor("convw", [D, 4], F32, kind="ExternalInput")
    smalls = nc.dram_tensor("smalls", [3, 64], F32, kind="ExternalInput")
    h2048c = nc.dram_tensor("h2048c", [D, D], BF16, kind="ExternalInput")
    b1c = nc.dram_tensor("b1c", [128, 128], BF16, kind="ExternalInput")
    h128c = nc.dram_tensor("h128c", [128, 128], BF16, kind="ExternalInput")
    idtc = nc.dram_tensor("idtc", [128, 128], BF16, kind="ExternalInput")
    attc = nc.dram_tensor("attc", [6, 64 * 64], F32, kind="ExternalInput")
    pid_t = nc.partition_id_tensor

    shWg = nc.dram_tensor("shWg", [IFF, D], BF16, addr_space="Shared")
    shWu = nc.dram_tensor("shWu", [IFF, D], BF16, addr_space="Shared")
    shWd = nc.dram_tensor("shWd", [D, IFF], BF16, addr_space="Shared")
    shWgT = nc.dram_tensor("shWgT", [D, IFF], BF16, addr_space="Shared")
    shWuT = nc.dram_tensor("shWuT", [D, IFF], BF16, addr_space="Shared")
    shWdT = nc.dram_tensor("shWdT", [IFF, D], BF16, addr_space="Shared")
    shWqkv = nc.dram_tensor("shWqkv", [D, D], BF16, addr_space="Shared")
    shWz = nc.dram_tensor("shWz", [1024, D], BF16, addr_space="Shared")
    shWout = nc.dram_tensor("shWout", [D, 1024], BF16, addr_space="Shared")
    shWba = nc.dram_tensor("shWba", [32, D], BF16, addr_space="Shared")
    shScale = nc.dram_tensor("shScale", [2, 16], F32, addr_space="Shared")
    shFlg = nc.dram_tensor("shFlg", [2, 32], I32, addr_space="Shared")
    shHalo = nc.dram_tensor("shHalo", [128, 48], F32, addr_space="Shared")
    shS16 = nc.dram_tensor("shS16", [64, 1024], F32, addr_space="Shared")
    if _DBGA:
        d_lnT = nc.dram_tensor("d_lnT", [128, 1024], BF16, kind="ExternalOutput")
        d_q = nc.dram_tensor("d_q", [64, 8192], BF16, kind="ExternalOutput")
        d_k = nc.dram_tensor("d_k", [64, 8192], BF16, kind="ExternalOutput")
        d_vcm = nc.dram_tensor("d_vcm", [64, 16384], BF16, kind="ExternalOutput")
        d_kcm = nc.dram_tensor("d_kcm", [64, 8192], BF16, kind="ExternalOutput")
        d_gb = nc.dram_tensor("d_gb", [16, 2048], F32, kind="ExternalOutput")
        d_cum = nc.dram_tensor("d_cum", [64, 1280], F32, kind="ExternalOutput")
        d_cumT = nc.dram_tensor("d_cumT", [128, 128], F32, kind="ExternalOutput")
        d_y = nc.dram_tensor("d_y", [128, 8192], BF16, kind="ExternalOutput")
        d_S = nc.dram_tensor("d_S", [64, 1024], F32, kind="ExternalOutput")
        d_z = nc.dram_tensor("d_z", [128, 1024], BF16, kind="ExternalOutput")
        d_c0 = nc.dram_tensor("d_c0", [128, 8192], F32, kind="ExternalOutput")
        d_c1 = nc.dram_tensor("d_c1", [128, 8192], F32, kind="ExternalOutput")
        d_cU = nc.dram_tensor("d_cU", [64, 16384], F32, kind="ExternalOutput")
        d_cS2 = nc.dram_tensor("d_cS2", [64, 16384], F32, kind="ExternalOutput")
        d_cE1 = nc.dram_tensor("d_cE1", [64, 16384], F32, kind="ExternalOutput")
        d_cE2 = nc.dram_tensor("d_cE2", [64, 16384], F32, kind="ExternalOutput")
        d_cP = nc.dram_tensor("d_cP", [64, 16384], F32, kind="ExternalOutput")
        d_cQ = nc.dram_tensor("d_cQ", [64, 16384], F32, kind="ExternalOutput")
    if _DBG:
        d_swcol = nc.dram_tensor("d_swcol", [128, 8], F32, kind="ExternalOutput")
        d_i1T = nc.dram_tensor("d_i1T", [128, 1024], BF16, kind="ExternalOutput")
        d_inv1 = nc.dram_tensor("d_inv1", [128, 8], F32, kind="ExternalOutput")
        d_h0 = nc.dram_tensor("d_h0", [128, 1024], BF16, kind="ExternalOutput")
        d_h2 = nc.dram_tensor("d_h2", [128, 1024], BF16, kind="ExternalOutput")
        d_i2T = nc.dram_tensor("d_i2T", [128, 512], I8, kind="ExternalOutput")
        d_wg = nc.dram_tensor("d_wg", [128, D], BF16, kind="ExternalOutput")
        d_wd = nc.dram_tensor("d_wd", [128, 1024], BF16, kind="ExternalOutput")
        d_xn = nc.dram_tensor("d_xn", [128, D], BF16, kind="ExternalOutput")
        d_xnT = nc.dram_tensor("d_xnT", [128, D], BF16, kind="ExternalOutput")
        d_f1 = nc.dram_tensor("d_f1", [128, D], F32, kind="ExternalOutput")
        d_hh = nc.dram_tensor("d_hh", [128, 3 * D], BF16, kind="ExternalOutput")
        d_pg = nc.dram_tensor("d_pg", [128, 512], F32, kind="ExternalOutput")
        d_pu = nc.dram_tensor("d_pu", [128, 512], F32, kind="ExternalOutput")
        d_cg = nc.dram_tensor("d_cg", [128, 8], F32, kind="ExternalOutput")
        d_wgs = nc.dram_tensor("d_wgs", [128, 2048], BF16, kind="ExternalOutput")
    xsp1 = nc.dram_tensor("xsp1", [TPC, D], F32)
    xsp2 = nc.dram_tensor("xsp2", [TPC, D], F32)
    scrd = nc.dram_tensor("scal_scr", [1, 64], F32)
    scrd2 = nc.dram_tensor("scrd2", [1, 2048], F32)
    zsp = nc.dram_tensor("zsp", [1024, 1024], BF16)

    st_top = ExitStack()
    _uid = [0]

    def un(base):
        _uid[0] += 1
        return f"{base}_{_uid[0]}"

    with st_top:
        trk = Trk(nc, st_top)
        g = nc.gpsimd

        def sb(name, shape, dt=F32):
            return st_top.enter_context(nc.sbuf_tensor(name, shape, dt))

        st_top.enter_context(nc.Block())

        i1Tbox = {}
        cB1 = sb("cB1", [128, 128], BF16)
        cH128 = sb("cH128", [128, 128], BF16)
        cIDT = sb("cIDT", [128, 128], BF16)
        inv1 = sb("inv1", [128, 8], F32)
        inv2 = sb("inv2", [128, 8], F32)
        cgc = sb("cgc", [128, 8], F32)
        cuc = sb("cuc", [128, 8], F32)
        cdc = sb("cdc", [128, 8], F32)
        swcol = sb("swcol", [128, 8], F32)
        onesf = sb("onesf", [128, 1], F32)
        ppacc = sb("ppacc", [128, 4], F32)
        pp1 = sb("pp1", [1, 64], F32)
        ftile = sb("ftile", [1, 32], I32)
        pfl = sb("pfl", [1, 32], I32)
        pidsb = sb("pidsb", [1, 1], mybir.dt.uint32)

        r_loop = g.alloc_register("r_loop")
        r_val = g.alloc_register("r_val")
        r_pw = g.alloc_register("r_pw")
        r_pid = g.alloc_register("r_pid")
        r_par = g.alloc_register("r_par")
        pollsem = st_top.enter_context(nc.semaphore("pollsem"))
        g.reg_mov(r_pw, 0)

        trk.dma("q2", pidsb[:], pid_t[0:1, 0:1], writes=[("pid",)])
        trk.pre("p", reads=[("pid",)])
        g.reg_load(r_pid, pidsb[0:1, 0:1])
        g.reg_alu(r_par, r_pid, 1, OP.bitwise_and)

        trk.dma("q0", cB1[:], b1c[:, :], writes=[("cB1",)])
        trk.dma("q0", cH128[:], h128c[:, :], writes=[("cH128",)])
        trk.dma("q0", cIDT[:], idtc[:, :], writes=[("cIDT",)])
        trk.ve(nc.vector.memset, onesf[:], 1.0, writes=[("ones",)])

        def gwm():
            g.reg_add(r_pw, r_pw, 16)
            g.wait_ge(pollsem, r_pw)

        def set_flag(widx):
            trk.ve(nc.vector.memset, ftile[0:1, widx:widx + 1], FLAGV + widx,
                   writes=[("ftile", widx)])
            trk.pre("p", reads=[("ftile", widx)])
            with g.If_eq(r_par, 0):
                g.dma_start(out=shFlg[0, widx:widx + 1], in_=ftile[0:1, widx:widx + 1]).then_inc(pollsem, 16)
            with g.Else():
                g.dma_start(out=shFlg[1, widx:widx + 1], in_=ftile[0:1, widx:widx + 1]).then_inc(pollsem, 16)
            gwm()

        def poll_flag(widx, key):
            g.reg_mov(r_loop, 1)
            with g.While(r_loop):
                with g.If_eq(r_par, 0):
                    g.dma_start(out=pfl[0:1, widx:widx + 1], in_=shFlg[1, widx:widx + 1]).then_inc(pollsem, 16)
                with g.Else():
                    g.dma_start(out=pfl[0:1, widx:widx + 1], in_=shFlg[0, widx:widx + 1]).then_inc(pollsem, 16)
                g.reg_add(r_pw, r_pw, 16)
                g.wait_ge(pollsem, r_pw)
                g.reg_load(r_val, pfl[0:1, widx:widx + 1])
                g.reg_alu(r_loop, r_val, FLAGV + widx, OP.bitwise_xor)
            inst = g.engine_nop()
            trk.post("p", inst, writes=[key])

        def pair_write(even_ap, odd_ap, src_ap, reads):
            trk.pre("p", reads=list(reads))
            with g.If_eq(r_par, 0):
                g.dma_start(out=even_ap, in_=src_ap).then_inc(pollsem, 16)
            with g.Else():
                g.dma_start(out=odd_ap, in_=src_ap).then_inc(pollsem, 16)
            gwm()
            inst = g.engine_nop()
            trk.post("p", inst, reads=list(reads))

        def pair_read_peer(dst_ap, even_ap, odd_ap, writes):
            trk.pre("p", writes=list(writes))
            with g.If_eq(r_par, 0):
                g.dma_start(out=dst_ap, in_=odd_ap).then_inc(pollsem, 16)
            with g.Else():
                g.dma_start(out=dst_ap, in_=even_ap).then_inc(pollsem, 16)
            gwm()
            inst = g.engine_nop()
            trk.post("p", inst, writes=list(writes))

        def half_write_tiles(pairs, reads):
            trk.pre("p", reads=list(reads))
            with g.If_eq(r_par, 0):
                for de, do, sr in pairs:
                    g.dma_start(out=de, in_=sr).then_inc(pollsem, 16)
            with g.Else():
                for de, do, sr in pairs:
                    g.dma_start(out=do, in_=sr).then_inc(pollsem, 16)
            g.reg_add(r_pw, r_pw, 16 * len(pairs))
            g.wait_ge(pollsem, r_pw)
            inst = g.engine_nop()
            trk.post("p", inst, reads=list(reads))

        def half_write(dst_even_ap, dst_odd_ap, src_ap, reads):
            trk.pre("p", reads=list(reads))
            with g.If_eq(r_par, 0):
                g.dma_start(out=dst_even_ap, in_=src_ap).then_inc(pollsem, 16)
            with g.Else():
                g.dma_start(out=dst_odd_ap, in_=src_ap).then_inc(pollsem, 16)
            gwm()
            inst = g.engine_nop()
            trk.post("p", inst, reads=list(reads))

        stPP = ExitStack()
        raws = stPP.enter_context(nc.sbuf_tensor(un("pp_raw"), [128, 2 * 2048], F32))
        qf = stPP.enter_context(nc.sbuf_tensor(un("pp_qf"), [128, 2048], F32))
        qout = stPP.enter_context(nc.sbuf_tensor(un("pp_q"), [128, 2 * 2048], BF16))
        rawD = stPP.enter_context(nc.sbuf_tensor(un("pp_rawD"), [128, IFF], F32))
        tD = stPP.enter_context(nc.sbuf_tensor(un("pp_tD"), [128, IFF], F32))
        qD = stPP.enter_context(nc.sbuf_tensor(un("pp_qD"), [128, IFF], BF16))
        ppsum = stPP.enter_context(nc.psum_tensor(un("pp_ps"), [1, 8], F32))
        ppsT = [stPP.enter_context(nc.psum_tensor(un("pp_psT"), [128, 128], BF16)) for _ in range(2)]
        tT = [stPP.enter_context(nc.sbuf_tensor(un("pp_tT"), [128, 2048], BF16)) for _ in range(2)]

        def pp_tiles(widx, raw, n_rows, n_cols):
            out = []
            for i in range(n_rows // 128):
                for j in range(n_cols // 2048):
                    out.append((i * 128, j * 2048))
            return out

        def pp_scale(widx, raw, n_rows, n_cols, flg_scale):
            tiles = pp_tiles(widx, raw, n_rows, n_cols)
            trk.ve(nc.vector.memset, ppacc[:, widx:widx + 1], 0.0, writes=[("ppacc", widx)])
            for i, (r0, c0) in enumerate(tiles):
                buf = i % 2
                rt = raws[:, buf * 2048:(buf + 1) * 2048]
                trk.dma("q0", rt, raw[r0:r0 + 128, c0:c0 + 2048], writes=[("ppraw", buf)])
                trk.ve(nc.vector.tensor_reduce, ppacc[:, 3:4], rt, axis=AX, op=OP.add,
                       apply_absolute_value=True, reads=[("ppraw", buf)], writes=[("ppt",)])
                trk.ve(nc.vector.tensor_tensor, ppacc[:, widx:widx + 1], ppacc[:, widx:widx + 1],
                       ppacc[:, 3:4], op=OP.add,
                       reads=[("ppt",), ("ppacc", widx)], writes=[("ppacc", widx)])
            with trk.tg(reads=[("ppacc", widx), ("ones",)], writes=[("pps", widx)]) as t:
                t.add(nc.tensor.matmul(ppsum[0:1, widx:widx + 1], ppacc[:, widx:widx + 1],
                                       onesf[:, 0:1], start=True, stop=True))
            trk.se(nc.scalar.activation, pp1[0:1, widx:widx + 1], ppsum[0:1, widx:widx + 1], AF.Copy,
                   reads=[("pps", widx)], writes=[("pp1", widx)])
            pair_write(shScale[0, widx:widx + 1], shScale[1, widx:widx + 1],
                       pp1[0:1, widx:widx + 1], reads=[("pp1", widx)])
            set_flag(flg_scale)
            poll_flag(flg_scale, ("flg", flg_scale))
            pair_read_peer(pp1[0:1, 16 + widx:17 + widx], shScale[0, widx:widx + 1],
                           shScale[1, widx:widx + 1], writes=[("pp1p", widx)])
            n_el = float(2 * n_rows * n_cols)
            trk.ve(nc.vector.tensor_tensor, pp1[0:1, 8 + widx:9 + widx], pp1[0:1, widx:widx + 1],
                   pp1[0:1, 16 + widx:17 + widx], op=OP.add,
                   reads=[("pp1", widx), ("pp1p", widx)], writes=[("swsum", widx)])
            trk.ve(nc.vector.tensor_scalar, pp1[0:1, 8 + widx:9 + widx], pp1[0:1, 8 + widx:9 + widx],
                   1.0 / n_el, 1e-5, op0=OP.mult, op1=OP.max,
                   reads=[("swsum", widx)], writes=[("sw", widx)])
            trk.ve(nc.vector.reciprocal, pp1[0:1, 24 + widx:25 + widx], pp1[0:1, 8 + widx:9 + widx],
                   reads=[("sw", widx)], writes=[("swi", widx)])
            trk.dma("q2", scrd[0:1, 2 * widx:2 * widx + 1], pp1[0:1, 8 + widx:9 + widx],
                    reads=[("sw", widx)], writes=[("scr_sw", widx)])
            trk.dma("q2", scrd[0:1, 2 * widx + 1:2 * widx + 2], pp1[0:1, 24 + widx:25 + widx],
                    reads=[("swi", widx)], writes=[("scr_swi", widx)])
            trk.dma("q2", swcol[:, widx:widx + 1], bass.AP(scrd, 2 * widx, [[0, 128], [1, 1]]),
                    reads=[("scr_sw", widx)], writes=[("swb", widx)])
            trk.dma("q2", swcol[:, 4 + widx:5 + widx], bass.AP(scrd, 2 * widx + 1, [[0, 128], [1, 1]]),
                    reads=[("scr_swi", widx)], writes=[("swib", widx)])

        def pp_quant_gu(widx, raw, sh_out, n_rows, flg_q, shkey):
            tiles = pp_tiles(widx, raw, n_rows, D)
            for i, (r0, c0) in enumerate(tiles):
                buf = i % 2
                rt = raws[:, buf * 2048:(buf + 1) * 2048]
                qt = qout[:, buf * 2048:(buf + 1) * 2048]
                trk.dma("q0", rt, raw[r0:r0 + 128, c0:c0 + 2048], writes=[("ppraw", buf)])
                trk.se(nc.scalar.activation, qf[:], rt, AF.Copy,
                       scale=swcol[:, 4 + widx:5 + widx], bias=MAGIC,
                       reads=[("ppraw", buf), ("swib", widx)], writes=[("ppqf",)])
                trk.ve(nc.vector.tensor_scalar, qf[:], qf[:], -MAGIC, 1.0, op0=OP.add, op1=OP.min,
                       reads=[("ppqf",)], writes=[("ppqf",)])
                trk.ve(nc.vector.tensor_scalar, qt, qf[:], -1.0, None, op0=OP.max,
                       reads=[("ppqf",)], writes=[("ppq", buf)])
                for k in range(16):
                    pb = k % 2
                    with trk.tg(reads=[("ppq", buf), ("cIDT",)], writes=[("ppsT", pb)]) as t:
                        t.add(nc.tensor.transpose(ppsT[pb][:], qt[:, k * 128:(k + 1) * 128], cIDT[:]))
                    if pb == 0:
                        trk.ve(nc.vector.tensor_copy, tT[buf][:, k * 128:(k + 1) * 128], ppsT[pb][:],
                               reads=[("ppsT", pb)], writes=[("ppT", buf)])
                    else:
                        trk.se(nc.scalar.activation, tT[buf][:, k * 128:(k + 1) * 128], ppsT[pb][:],
                               AF.Copy, reads=[("ppsT", pb)], writes=[("ppT", buf)])
                half_write_tiles(
                    [(sh_out[k * 128:(k + 1) * 128, r0:r0 + 128],
                      sh_out[k * 128:(k + 1) * 128, n_rows + r0:n_rows + r0 + 128],
                      tT[buf][:, k * 128:(k + 1) * 128]) for k in range(16)],
                    reads=[("ppT", buf)])
            set_flag(flg_q)
            poll_flag(flg_q, ("flg", flg_q))
            trk.lastw[shkey] = ("p", trk.cnt["p"])

        def pp_quant_wd(widx, raw, sh_out, n_rows, flg_q, shkey):
            for rb in range(n_rows // 128):
                r0 = rb * 128
                for j in range(4):
                    trk.dma("q0", rawD[:, j * 2048:(j + 1) * 2048],
                            raw[r0:r0 + 128, j * 2048:(j + 1) * 2048], writes=[("ppD", j)])
                trk.se(nc.scalar.activation, tD[:], rawD[:], AF.Copy,
                       scale=swcol[:, 4 + widx:5 + widx], bias=MAGIC,
                       reads=[("ppD", 0), ("ppD", 1), ("ppD", 2), ("ppD", 3), ("swib", widx)],
                       writes=[("pptD",)])
                trk.ve(nc.vector.tensor_scalar, tD[:], tD[:], -MAGIC, 1.0, op0=OP.add, op1=OP.min,
                       reads=[("pptD",)], writes=[("pptD",)])
                dst = bass.AP(qD, 0, [[IFF, 128], [1, 128], [128, 64]])
                src = bass.AP(tD, 0, [[IFF, 128], [64, 128], [1, 64]])
                trk.ve(nc.vector.tensor_scalar, dst, src, -1.0, None, op0=OP.max,
                       reads=[("pptD",)], writes=[("ppqD",)])
                for grp in range(4):
                    gb = grp % 2
                    for k in range(16):
                        kk = grp * 16 + k
                        pb = k % 2
                        with trk.tg(reads=[("ppqD",), ("cIDT",)], writes=[("ppsT", pb)]) as t:
                            t.add(nc.tensor.transpose(ppsT[pb][:], qD[:, kk * 128:(kk + 1) * 128],
                                                      cIDT[:]))
                        if pb == 0:
                            trk.ve(nc.vector.tensor_copy, tT[gb][:, k * 128:(k + 1) * 128], ppsT[pb][:],
                                   reads=[("ppsT", pb)], writes=[("ppT", gb)])
                        else:
                            trk.se(nc.scalar.activation, tT[gb][:, k * 128:(k + 1) * 128], ppsT[pb][:],
                                   AF.Copy, reads=[("ppsT", pb)], writes=[("ppT", gb)])
                    half_write_tiles(
                        [(sh_out[(grp * 16 + k) * 128:(grp * 16 + k + 1) * 128, r0:r0 + 128],
                          sh_out[(grp * 16 + k) * 128:(grp * 16 + k + 1) * 128,
                                 n_rows + r0:n_rows + r0 + 128],
                          tT[gb][:, k * 128:(k + 1) * 128]) for k in range(16)],
                        reads=[("ppT", gb)])
            set_flag(flg_q)
            poll_flag(flg_q, ("flg", flg_q))
            trk.lastw[shkey] = ("p", trk.cnt["p"])

        def pp_cast_attn():
            jobs = [(wqkv_raw, shWqkv, 1024, D), (wz_raw, shWz, 512, D),
                    (wout_raw, shWout, 1024, 1024)]
            for ji, (raw, sh_outt, n_rows, n_cols) in enumerate(jobs):
                nrr = n_rows if n_cols == 2048 else n_rows // 2
                for i in range(max(1, (n_rows * n_cols) // (128 * 2048))):
                    buf = i % 2
                    rt = raws[:, buf * 2048:(buf + 1) * 2048]
                    qt = qout[:, buf * 2048:(buf + 1) * 2048]
                    if n_cols == 2048:
                        src = raw[i * 128:(i + 1) * 128, :]
                    else:
                        src = raw.ap().rearrange("(a b) c -> a (b c)", b=2)[i * 128:(i + 1) * 128, :]
                    trk.dma("q0", rt, src, writes=[("ppraw", buf)])
                    trk.ve(nc.vector.tensor_copy, qt, rt, reads=[("ppraw", buf)], writes=[("ppq", buf)])
                    if n_cols == 2048:
                        de = sh_outt[i * 128:(i + 1) * 128, :]
                        do = sh_outt[n_rows + i * 128:n_rows + (i + 1) * 128, :]
                    else:
                        shv = sh_outt.ap().rearrange("(a b) c -> a (b c)", b=2)
                        de = shv[i * 128:(i + 1) * 128, :]
                        do = shv[(n_rows + i * 128) // 2 * 2 // 2:0] if False else shv[n_rows // 2 + i * 128:n_rows // 2 + (i + 1) * 128, :]
                    half_write(de, do, qt, reads=[("ppq", buf)])
            trk.dma("q0", raws[:, 0:2048].tensor[0:32, 0:2048] if False else raws[0:32, 0:2048],
                    wba_raw[:, :], writes=[("ppraw", 0)])
            trk.ve(nc.vector.tensor_copy, qout[0:32, 0:2048], raws[0:32, 0:2048],
                   reads=[("ppraw", 0)], writes=[("ppq", 0)])
            trk.pre("p", reads=[("ppq", 0)])
            g.dma_start(out=shWba[:, :], in_=qout[0:32, 0:2048]).then_inc(pollsem, 16)
            gwm()
            inst = g.engine_nop()
            trk.post("p", inst, reads=[("ppq", 0)])
            set_flag(6)
            poll_flag(6, ("flg", 6))
            for kk in (("shWqkvb",), ("shWzb",), ("shWoutb",), ("shWbab",)):
                trk.lastw[kk] = ("p", trk.cnt["p"])

        pp_scale(0, wg_raw, IFF // 2, D, 0)
        pp_scale(1, wu_raw, IFF // 2, D, 2)
        pp_scale(2, wd_raw, D // 2, IFF, 4)
        pp_quant_gu(0, wg_raw, shWgT, IFF // 2, 1, ("shWgq",))
        pp_quant_gu(1, wu_raw, shWuT, IFF // 2, 3, ("shWuq",))
        pp_quant_wd(2, wd_raw, shWdT, D // 2, 5, ("shWdq",))

        def stage_A(xsrc, src_key):
            stA = ExitStack()
            with stA:
                xt = stA.enter_context(nc.sbuf_tensor(un("A_xt"), [128, 2 * D], F32))
                sq = stA.enter_context(nc.sbuf_tensor(un("A_sq"), [128, D], F32))
                f1 = stA.enter_context(nc.sbuf_tensor(un("A_f1"), [128, D], F32))
                xn = stA.enter_context(nc.sbuf_tensor(un("A_xn"), [128, D], BF16))
                xnT = stA.enter_context(nc.sbuf_tensor(un("A_xnT"), [128, D], BF16))
                hh = stA.enter_context(nc.sbuf_tensor(un("A_h2048"), [128, 16 * D], BF16))
                lns = stA.enter_context(nc.sbuf_tensor(un("A_lns"), [128, 8], F32))
                psT = [stA.enter_context(nc.psum_tensor(un(f"A_psT{i}"), [128, 512], BF16)) for i in range(2)]
                psF = [stA.enter_context(nc.psum_tensor(un(f"A_psF{i}"), [128, 512], F32)) for i in range(4)]
                for k in range(16):
                    trk.dma("q1", hh[:, k * D:(k + 1) * D], h2048c[k * 128:(k + 1) * 128, :],
                            writes=[("hh", k)])
                rd_hh = [("hh", k) for k in range(16)]
                for tt in range(NTT):
                    xb = tt % 2
                    xv = xt[:, xb * D:(xb + 1) * D]
                    trk.dma("q0", xv, xsrc[tt * 128:(tt + 1) * 128, :],
                            reads=([(src_key, tt)] if src_key else []), writes=[("A_xt", xb)])
                    trk.ve(nc.vector.tensor_reduce, lns[:, 0:1], xv, axis=AX, op=OP.add,
                           reads=[("A_xt", xb)], writes=[("ln0",)])
                    trk.se(nc.scalar.activation, sq[:], xv, AF.Square,
                           reads=[("A_xt", xb)], writes=[("A_sq",)])
                    trk.ve(nc.vector.tensor_reduce, lns[:, 1:2], sq[:], axis=AX, op=OP.add,
                           reads=[("A_sq",)], writes=[("ln1",)])
                    trk.se(nc.scalar.activation, lns[:, 2:3], lns[:, 0:1], AF.Copy, scale=-1.0 / D,
                           reads=[("ln0",)], writes=[("negm",)])
                    trk.ve(nc.vector.tensor_scalar, lns[:, 3:4], lns[:, 1:2], 1.0 / D, None, op0=OP.mult,
                           reads=[("ln1",)], writes=[("ex2",)])
                    trk.ve(nc.vector.tensor_tensor, lns[:, 4:5], lns[:, 2:3], lns[:, 2:3], op=OP.mult,
                           reads=[("negm",)], writes=[("m2",)])
                    trk.ve(nc.vector.tensor_tensor, lns[:, 5:6], lns[:, 3:4], lns[:, 4:5], op=OP.subtract,
                           reads=[("ex2",), ("m2",)], writes=[("var",)])
                    trk.ve(nc.vector.tensor_scalar, lns[:, 5:6], lns[:, 5:6], 1e-5, None, op0=OP.add,
                           reads=[("var",)], writes=[("var",)])
                    trk.ve(nc.vector.reciprocal, lns[:, 6:7], lns[:, 5:6],
                           reads=[("var",)], writes=[("rv",)])
                    trk.se(nc.scalar.activation, lns[:, 7:8], lns[:, 6:7], AF.Sqrt,
                           reads=[("rv",)], writes=[("rstd",)])
                    trk.ve(nc.vector.tensor_scalar, xn[:], xv, lns[:, 2:3], lns[:, 7:8],
                           op0=OP.add, op1=OP.mult,
                           reads=[("A_xt", xb), ("negm",), ("rstd",)],
                           writes=[("A_xn",), ("A_i1",)])
                    for kb in range(4):
                        pb = kb % 2
                        with trk.tg(reads=[("A_xn",), ("cIDT",)], writes=[("A_psT", pb)]) as t:
                            for j in range(4):
                                k = kb * 4 + j
                                t.add(nc.tensor.transpose(psT[pb][:, j * 128:(j + 1) * 128],
                                                          xn[:, k * 128:(k + 1) * 128], cIDT[:]))
                        trk.ve(nc.vector.tensor_copy, xnT[:, kb * 512:(kb + 1) * 512], psT[pb][:],
                               reads=[("A_psT", pb)], writes=[("A_xnT", kb)])
                    rd_xnT = [("A_xnT", kb) for kb in range(4)]
                    for ncn in range(4):
                        with trk.tg(reads=rd_xnT + rd_hh, writes=[("A_psF", ncn)]) as t:
                            for k in range(16):
                                t.add(nc.tensor.matmul(psF[ncn][:],
                                                       xnT[:, k * 128:(k + 1) * 128],
                                                       hh[:, k * D + ncn * 512:k * D + (ncn + 1) * 512],
                                                       start=(k == 0), stop=(k == 15)))
                        trk.se(nc.scalar.activation, f1[:, ncn * 512:(ncn + 1) * 512], psF[ncn][:], AF.Copy,
                               reads=[("A_psF", ncn)], writes=[("A_f1", ncn)])
                    if _DBG and tt == 6:
                        trk.dma("q2", d_hh[:, 0:D], hh[:, 0:D], reads=[("hh", 0)])
                        trk.dma("q2", d_hh[:, D:2 * D], hh[:, 14 * D:15 * D], reads=[("hh", 14)])
                        trk.dma("q2", d_hh[:, 2 * D:3 * D], hh[:, 15 * D:16 * D], reads=[("hh", 15)])
                        trk.dma("q2", d_xn[:, :], xn[:], reads=[("A_xn",)])
                        trk.dma("q2", d_xnT[:, :], xnT[:], reads=[("A_xnT", kb2) for kb2 in range(4)])
                        trk.dma("q2", d_f1[:, :], f1[:], reads=[("A_f1", i2) for i2 in range(4)])
                    rd_f1 = [("A_f1", i) for i in range(4)]
                    trk.ve(nc.vector.tensor_reduce, lns[:, 0:1], f1[:], axis=AX, op=OP.max,
                           apply_absolute_value=True, reads=rd_f1, writes=[("mx",)])
                    trk.ve(nc.vector.tensor_scalar, lns[:, 1:2], lns[:, 0:1], C2048, 1e-5,
                           op0=OP.mult, op1=OP.max, reads=[("mx",)], writes=[("mxc",)])
                    trk.ve(nc.vector.reciprocal, lns[:, 2:3], lns[:, 1:2],
                           reads=[("mxc",)], writes=[("rec",)])
                    trk.ve(nc.vector.tensor_scalar, lns[:, 3:4], lns[:, 2:3], 127.0 * C2048, None,
                           op0=OP.mult, reads=[("rec",)], writes=[("seff",)])
                    trk.ve(nc.vector.tensor_scalar, inv1[:, tt:tt + 1], lns[:, 1:2], 1.0 / 127.0, None,
                           op0=OP.mult, reads=[("mxc",)], writes=[("inv1", tt)])
                    trk.se(nc.scalar.activation, f1[:], f1[:], AF.Copy,
                           scale=lns[:, 3:4], bias=MAGIC,
                           reads=rd_f1 + [("seff",)], writes=[("A_t1",)] + rd_f1)
                    trk.se(nc.scalar.activation, xn[:], f1[:], AF.Copy, bias=-MAGIC,
                           reads=[("A_t1",)], writes=[("A_i1",), ("A_xn",)])
                    for kb in range(4):
                        pb = kb % 2
                        with trk.tg(reads=[("A_i1",), ("cIDT",)], writes=[("A_psT", pb)]) as t:
                            for j in range(4):
                                k = kb * 4 + j
                                t.add(nc.tensor.transpose(psT[pb][:, j * 128:(j + 1) * 128],
                                                          xn[:, k * 128:(k + 1) * 128], cIDT[:]))
                        src = bass.AP(psT[pb], 0, [[512, 128], [128, 4], [1, 128]])
                        dst = bass.AP(i1Tbox["t"], (kb * 4) * 1024 + tt * 128,
                                      [[16 * 1024, 128], [1024, 4], [1, 128]])
                        trk.ve(nc.vector.tensor_copy, dst, src,
                               reads=[("A_psT", pb)],
                               writes=[("i1T", kb * 4 + j2) for j2 in range(4)])
            trk.join()

        def stage_BCD(th, xsrc, xdst, out_key, src_key):
            stH = ExitStack()
            with stH:
                i2T = stH.enter_context(nc.sbuf_tensor(un("C_i2T"), [128, 64 * 512], I8))
                stHh = ExitStack()
                h = [stHh.enter_context(nc.sbuf_tensor(un(f"B_h{m}"), [128, IFF], BF16)) for m in range(4)]
                stB = ExitStack()
                with stB:
                    wgS = [stB.enter_context(nc.sbuf_tensor(un(f"B_wg{i}"), [128, 16 * 512], BF16)) for i in range(2)]
                    wuS = [stB.enter_context(nc.sbuf_tensor(un(f"B_wu{i}"), [128, 16 * 512], BF16)) for i in range(2)]
                    gsc = stB.enter_context(nc.sbuf_tensor(un("B_gs"), [128, 512], F32))
                    usc = stB.enter_context(nc.sbuf_tensor(un("B_us"), [128, 512], F32))
                    psG = [stB.enter_context(nc.psum_tensor(un(f"B_psG{i}"), [128, 512], F32)) for i in range(2)]
                    psU = [stB.enter_context(nc.psum_tensor(un(f"B_psU{i}"), [128, 512], F32)) for i in range(2)]
                    rd_i1 = [("i1T", k) for k in range(16)]
                    for gg in range(16):
                        sbi = gg % 2
                        for k in range(16):
                            trk.dma("q0", wgS[sbi][:, k * 512:(k + 1) * 512],
                                    shWgT[k * 128:(k + 1) * 128, gg * 512:(gg + 1) * 512],
                                    reads=[("shWgq",)], writes=[("wgS", sbi, k)])
                            trk.dma("q0", wuS[sbi][:, k * 512:(k + 1) * 512],
                                    shWuT[k * 128:(k + 1) * 128, gg * 512:(gg + 1) * 512],
                                    reads=[("shWuq",)], writes=[("wuS", sbi, k)])
                        for m in range(4):
                            tt = th * 4 + m
                            pb = m % 2
                            rdg = [("wgS", sbi, k) for k in range(16)] + rd_i1
                            with trk.tg(reads=rdg, writes=[("psG", pb)]) as t:
                                for k in range(16):
                                    t.add(nc.tensor.matmul(psG[pb][:],
                                                           i1Tbox["t"][:, k * 1024 + tt * 128:k * 1024 + (tt + 1) * 128],
                                                           wgS[sbi][:, k * 512:(k + 1) * 512],
                                                           start=(k == 0), stop=(k == 15)))
                            rdu = [("wuS", sbi, k) for k in range(16)] + rd_i1
                            with trk.tg(reads=rdu, writes=[("psU", pb)]) as t:
                                for k in range(16):
                                    t.add(nc.tensor.matmul(psU[pb][:],
                                                           i1Tbox["t"][:, k * 1024 + tt * 128:k * 1024 + (tt + 1) * 128],
                                                           wuS[sbi][:, k * 512:(k + 1) * 512],
                                                           start=(k == 0), stop=(k == 15)))
                            if _DBG and th == 0 and gg == 0 and m == 0:
                                trk.dma("q2", d_wgs[:, :], wgS[0][:, 0:2048],
                                        reads=[("wgS", 0, kx) for kx in range(4)])
                                trk.ve(nc.vector.tensor_copy, gsc[:], psG[pb][:],
                                       reads=[("psG", pb)], writes=[("gs",)])
                                trk.dma("q2", d_pg[:, :], gsc[:], reads=[("gs",)])
                                trk.ve(nc.vector.tensor_copy, usc[:], psU[pb][:],
                                       reads=[("psU", pb)], writes=[("us",)])
                                trk.dma("q2", d_pu[:, :], usc[:], reads=[("us",)])
                                trk.dma("q2", d_cg[:, :], cgc[:, 0:8], reads=[("cg", tt)])
                            trk.se(nc.scalar.activation, gsc[:], psG[pb][:], AF.Silu,
                                   scale=cgc[:, tt:tt + 1],
                                   reads=[("psG", pb), ("cg", tt)], writes=[("gs",)])
                            trk.ve(nc.vector.tensor_scalar, usc[:], psU[pb][:], cuc[:, tt:tt + 1], None,
                                   op0=OP.mult, reads=[("psU", pb), ("cu", tt)], writes=[("us",)])
                            trk.ve(nc.vector.tensor_tensor, h[m][:, gg * 512:(gg + 1) * 512],
                                   gsc[:], usc[:], op=OP.mult,
                                   reads=[("gs",), ("us",)], writes=[("h", m, gg)])
                if _DBG and th == 0:
                    trk.dma("q2", d_h0[:, :], h[0][:, 0:1024], reads=[("h", 0, 0), ("h", 0, 1)])
                trk.join()
                stC = ExitStack()
                with stC:
                    h2b = [stC.enter_context(nc.sbuf_tensor(un(f"C_h2{i}"), [128, IFF], BF16)) for i in range(3)]
                    t2s = stC.enter_context(nc.sbuf_tensor(un("C_t2s"), [128, 2048], F32))
                    sA = [stC.enter_context(nc.sbuf_tensor(un(f"C_sA{i}"), [128, 128], BF16)) for i in range(2)]
                    sB = [stC.enter_context(nc.sbuf_tensor(un(f"C_sB{i}"), [128, 128], BF16)) for i in range(2)]
                    lns = stC.enter_context(nc.sbuf_tensor(un("C_lns"), [128, 12], F32))
                    ps1 = [stC.enter_context(nc.psum_tensor(un(f"C_ps1{i}"), [128, 128], BF16)) for i in range(2)]
                    ps2 = [stC.enter_context(nc.psum_tensor(un(f"C_ps2{i}"), [128, 128], F32)) for i in range(2)]
                    ps3 = [stC.enter_context(nc.psum_tensor(un(f"C_ps3{i}"), [128, 256], BF16)) for i in range(2)]

                    def c_front(m):
                        h2 = h2b[m % 3]
                        for q in range(64):
                            pb = q % 2
                            with trk.tg(reads=[("h", m, q // 4), ("cIDT",)], writes=[("ps1", pb)]) as t:
                                t.add(nc.tensor.transpose(ps1[pb][:], h[m][:, q * 128:(q + 1) * 128], cIDT[:]))
                            trk.ve(nc.vector.tensor_copy, sA[pb][:], ps1[pb][:],
                                   reads=[("ps1", pb)], writes=[("sA", pb)])
                            with trk.tg(reads=[("sA", pb), ("cB1",)], writes=[("ps2", pb)]) as t:
                                t.add(nc.tensor.matmul(ps2[pb][:], cB1[:], sA[pb][:], start=True, stop=True))
                            trk.se(nc.scalar.activation, sB[pb][:], ps2[pb][:], AF.Copy,
                                   reads=[("ps2", pb)], writes=[("sB", pb)])
                            with trk.tg(reads=[("sB", pb), ("cIDT",)], writes=[("ps3", pb)]) as t:
                                t.add(nc.tensor.transpose(ps3[pb][:, 0:128], sB[pb][:], cIDT[:]))
                            trk.ve(nc.vector.tensor_copy, h[m][:, q * 128:(q + 1) * 128], ps3[pb][:, 0:128],
                                   reads=[("ps3", pb)], writes=[("h", m, q // 4)])
                        rd_all_h = [("h", m, q) for q in range(16)]
                        for jp in range(64):
                            pb = jp % 2
                            src = bass.AP(h[m], jp, [[IFF, 128], [64, 128]])
                            with trk.tg(reads=rd_all_h + [("cIDT",)], writes=[("ps1", pb)]) as t:
                                t.add(nc.tensor.transpose(ps1[pb][:], src, cIDT[:]))
                            trk.ve(nc.vector.tensor_copy, sA[pb][:], ps1[pb][:],
                                   reads=[("ps1", pb)], writes=[("sA", pb)])
                            with trk.tg(reads=[("sA", pb), ("cH128",)], writes=[("ps2", pb)]) as t:
                                t.add(nc.tensor.matmul(ps2[pb][:], sA[pb][:], cH128[:], start=True, stop=True))
                            trk.se(nc.scalar.activation, h2[:, jp * 128:(jp + 1) * 128], ps2[pb][:], AF.Copy,
                                   reads=[("ps2", pb)], writes=[("h2m", m)])

                    def c_back(m):
                        tt = th * 4 + m
                        h2 = h2b[m % 3]
                        lb = (m % 3) * 4
                        trk.ve(nc.vector.tensor_reduce, lns[:, lb:lb + 1], h2[:], axis=AX, op=OP.max,
                               apply_absolute_value=True, reads=[("h2m", m)], writes=[("mx2", m)])
                        trk.ve(nc.vector.tensor_scalar, lns[:, lb + 1:lb + 2], lns[:, lb:lb + 1], C8192, 1e-5,
                               op0=OP.mult, op1=OP.max, reads=[("mx2", m)], writes=[("mxc2", m)])
                        trk.ve(nc.vector.reciprocal, lns[:, lb + 2:lb + 3], lns[:, lb + 1:lb + 2],
                               reads=[("mxc2", m)], writes=[("rec2", m)])
                        trk.ve(nc.vector.tensor_scalar, lns[:, lb + 3:lb + 4], lns[:, lb + 2:lb + 3],
                               127.0 * C8192, None,
                               op0=OP.mult, reads=[("rec2", m)], writes=[("seff2", m)])
                        trk.ve(nc.vector.tensor_scalar, inv2[:, tt:tt + 1], lns[:, lb + 1:lb + 2], 1.0 / 127.0,
                               None, op0=OP.mult, reads=[("mxc2", m)], writes=[("inv2", tt)])
                        for cq in range(4):
                            trk.se(nc.scalar.activation, t2s[:], h2[:, cq * 2048:(cq + 1) * 2048], AF.Copy,
                                   scale=lns[:, lb + 3:lb + 4], bias=MAGIC,
                                   reads=[("h2m", m), ("seff2", m)], writes=[("t2s",)])
                            trk.se(nc.scalar.activation, h2[:, cq * 2048:(cq + 1) * 2048], t2s[:], AF.Copy,
                                   bias=-MAGIC, reads=[("t2s",)], writes=[("h2m", m)])
                        for k2 in range(32):
                            pb = k2 % 2
                            with trk.tg(reads=[("h2m", m), ("cIDT",)], writes=[("ps3", pb)]) as t:
                                for j2 in range(2):
                                    k = k2 * 2 + j2
                                    t.add(nc.tensor.transpose(ps3[pb][:, j2 * 128:(j2 + 1) * 128],
                                                              h2[:, k * 128:(k + 1) * 128], cIDT[:]))
                            trk.ve(nc.vector.tensor_copy,
                                   bass.AP(i2T, (k2 * 2) * 512 + m * 128, [[64 * 512, 128], [512, 2], [1, 128]]),
                                   ps3[pb][:],
                                   reads=[("ps3", pb)], writes=[("i2T", k2 * 2, m), ("i2T", k2 * 2 + 1, m)])

                    for m in range(4):
                        c_front(m)
                        if m > 1:
                            c_back(m - 2)
                    c_back(2)
                    c_back(3)
                if _DBG and th == 0:
                    trk.dma("q2", d_h2[:, :], h2b[0][:, 0:1024], reads=[("h2m", 3)])
                    trk.dma("q2", d_i2T[:, :], i2T[:, 0:512], reads=[("i2T", 0, mm2) for mm2 in range(4)])
                trk.join()
                stHh.close()
                stD = ExitStack()
                with stD:
                    wdS = [stD.enter_context(nc.sbuf_tensor(un(f"D_wd{i}"), [128, 16 * 1024], BF16)) for i in range(2)]
                    cch = [stD.enter_context(nc.sbuf_tensor(un(f"D_cc{i}"), [128, 16 * 128], BF16)) for i in range(2)]
                    xrs = [stD.enter_context(nc.sbuf_tensor(un(f"D_xr{i}"), [128, D], F32)) for i in range(4)]
                    xot = [stD.enter_context(nc.sbuf_tensor(un(f"D_xo{i}"), [128, D], F32)) for i in range(4)]
                    psD = [stD.enter_context(nc.psum_tensor(un(f"D_ps{i}"), [128, 1024], F32)) for i in range(4)]
                    for m in range(4):
                        tt = th * 4 + m
                        trk.ve(nc.vector.tensor_tensor, cdc[:, tt:tt + 1], inv2[:, tt:tt + 1],
                               swcol[:, 2:3], op=OP.mult,
                               reads=[("inv2", tt), ("swb", 2)], writes=[("cd0", tt)])
                        trk.ve(nc.vector.tensor_scalar, cdc[:, tt:tt + 1], cdc[:, tt:tt + 1], 0.5, None,
                               op0=OP.mult, reads=[("cd0", tt)], writes=[("cd", tt)])
                        trk.dma("q1", xrs[m][:], xsrc[tt * 128:(tt + 1) * 128, :],
                                reads=([(src_key, tt)] if src_key else []), writes=[("xr", m)])
                    for ogh in range(2):
                        for kq in range(4):
                            sbi = kq % 2
                            for k in range(16):
                                kk = kq * 16 + k
                                trk.dma("q0",
                                        wdS[sbi][:, k * 1024:(k + 1) * 1024],
                                        shWdT[kk * 128:(kk + 1) * 128, ogh * 1024:(ogh + 1) * 1024],
                                        reads=[("shWdq",)], writes=[("wdS", sbi, k)])
                            for m in range(4):
                                cb = cch[m % 2]
                                src = bass.AP(i2T, (kq * 16) * 512 + m * 128,
                                              [[64 * 512, 128], [512, 16], [1, 128]])
                                dst = bass.AP(cb, 0, [[16 * 128, 128], [128, 16], [1, 128]])
                                trk.ve(nc.vector.tensor_copy, dst, src,
                                       reads=[("i2T", kq * 16 + kx, m) for kx in range(16)],
                                       writes=[("cch", m % 2)])
                                rd = [("wdS", sbi, k) for k in range(16)] + [("cch", m % 2)]
                                with trk.tg(reads=rd, writes=[("psD", m)]) as t:
                                    for k in range(16):
                                        st0 = (kq == 0 and k == 0)
                                        sp0 = (kq == 3 and k == 15)
                                        t.add(nc.tensor.matmul(psD[m][:, 0:512],
                                                               cb[:, k * 128:(k + 1) * 128],
                                                               wdS[sbi][:, k * 1024:k * 1024 + 512],
                                                               start=st0, stop=sp0))
                                        t.add(nc.tensor.matmul(psD[m][:, 512:1024],
                                                               cb[:, k * 128:(k + 1) * 128],
                                                               wdS[sbi][:, k * 1024 + 512:(k + 1) * 1024],
                                                               start=st0, stop=sp0))
                        for m in range(4):
                            tt = th * 4 + m
                            trk.se(nc.scalar.activation,
                                   xot[m][:, ogh * 1024:(ogh + 1) * 1024],
                                   psD[m][:], AF.Copy, scale=cdc[:, tt:tt + 1],
                                   reads=[("psD", m), ("cd", tt)], writes=[("xot", m, ogh)])
                        for m in range(4):
                            trk.ve(nc.vector.tensor_tensor,
                                   xot[m][:, ogh * 1024:(ogh + 1) * 1024],
                                   xot[m][:, ogh * 1024:(ogh + 1) * 1024],
                                   xrs[m][:, ogh * 1024:(ogh + 1) * 1024], op=OP.add,
                                   reads=[("xot", m, ogh), ("xr", m)],
                                   writes=[("xot", m, ogh)])
                    for m in range(4):
                        tt = th * 4 + m
                        trk.dma("q2", xdst[tt * 128:(tt + 1) * 128, :], xot[m][:],
                                reads=[("xot", m, 0), ("xot", m, 1)], writes=[(out_key, tt)])
                trk.join()

        def ffn(xsrc, xdst, out_key, src_key):
            stF0 = ExitStack()
            i1Tbox["t"] = stF0.enter_context(nc.sbuf_tensor(un("i1T"), [128, 16 * 1024], BF16))
            stage_A(xsrc, src_key)
            for tt in range(NTT):
                trk.ve(nc.vector.tensor_tensor, cgc[:, tt:tt + 1], inv1[:, tt:tt + 1],
                       swcol[:, 0:1], op=OP.mult,
                       reads=[("inv1", tt), ("swb", 0)], writes=[("cg", tt)])
                trk.ve(nc.vector.tensor_tensor, cuc[:, tt:tt + 1], inv1[:, tt:tt + 1],
                       swcol[:, 1:2], op=OP.mult,
                       reads=[("inv1", tt), ("swb", 1)], writes=[("cu", tt)])
            if _DBG and out_key == "XO" or _DBG and out_key == "X1":
                trk.dma("q2", d_i1T[:, :], i1T[:, 0:1024], reads=[("i1T", 0)])
                trk.dma("q2", d_inv1[:, :], inv1[:, 0:8], reads=[("inv1", tt2) for tt2 in range(8)])
            for th in range(2):
                stage_BCD(th, xsrc, xdst, out_key, src_key)
            trk.join()
            stF0.close()
            trk.join()



        def attention(xsrc, src_key, xdst, out_key):
            stAt = ExitStack()
            with stAt:
                A_ = lambda name, shape, dt=F32: stAt.enter_context(nc.sbuf_tensor(un(name), shape, dt))
                q_f64 = A_("at_qf64", [64, 8 * 1024], BF16)
                k_f64 = A_("at_kf64", [64, 8 * 1024], BF16)
                v_cm = A_("at_vcm", [64, 16 * 1024], BF16)
                k_cm = A_("at_kcm", [64, 16 * 512], BF16)
                g_fm = A_("at_g", [16, 1024], F32)
                b_fm = A_("at_b", [16, 1024], F32)
                gT2 = A_("at_gT2", [64, 256], F32)
                bT2 = A_("at_bT2", [64, 256], F32)
                cum2 = A_("at_cum2", [64, 256], F32)
                crev2 = A_("at_crev2", [64, 256], F32)
                gam2 = A_("at_gam2", [64, 256], F32)
                bg2 = A_("at_bg2", [64, 256], F32)
                ecc2 = A_("at_ecc2", [64, 256], F32)
                cumT2 = A_("at_cumT2", [128, 128], F32)
                mUI2 = A_("at_mUI", [64, 64], F32)
                mSL2 = A_("at_mSL", [64, 64], F32)
                bd2 = A_("at_bd2", [128, 2], F32)
                b2up = A_("at_b2up", [2, 128], F32)
                onesr = A_("at_ones", [1, 128], F32)
                l64u = A_("at_l64u", [64, 64], F32)
                l64s = A_("at_l64s", [64, 64], F32)
                idf = A_("at_idf", [128, 128], F32)
                dtb_sb = A_("at_dtb", [16, 1], F32)
                alog_sb = A_("at_alog", [16, 1], F32)
                negea = A_("at_negea", [16, 1], F32)
                nw_sb = A_("at_nw", [128, 1], F32)
                cwsb = A_("at_cw", [128, 64], F32)
                halo_sb = A_("at_halo", [128, 48], F32)
                halo_st = A_("at_halost", [128, 48], F32)
                cfl_sb = A_("at_cfl", [1, 1024], F32)
                eccr = A_("at_eccr", [1, 256], F32)

                scol = {}
                cnt0, cnt1 = 0, 0
                for h in range(16):
                    if (h // 2) % 2 == 0:
                        scol[h] = cnt0; cnt0 += 1
                    else:
                        scol[h] = cnt1; cnt1 += 1

                def bhof(h):
                    return ((h // 2) % 2) * 64

                attv = attc.ap().rearrange("a (r c) -> a r c", r=64)
                trk.dma("q1", mUI2[0:64, :], attv[3], writes=[("mui",)])
                trk.dma("q1", mSL2[0:64, :], attv[0], writes=[("msl",)])
                trk.dma("q1", bd2[:, :], bass.AP(attc, 4 * 4096, [[2, 128], [1, 2]]), writes=[("bd2",)])
                trk.dma("q1", b2up[:, :], bass.AP(attc, 5 * 4096, [[128, 2], [1, 128]]), writes=[("b2up",)])
                trk.dma("q1", l64u[:, :], attv[3], writes=[("l64u",)])
                trk.dma("q1", l64s[:, :], attv[0], writes=[("l64s",)])
                trk.ve(nc.vector.memset, onesr[:], 1.0, writes=[("onesr",)])
                trk.se(nc.scalar.activation, idf[:], cIDT[:], AF.Copy, reads=[("cIDT",)], writes=[("idf",)])
                trk.dma("q1", dtb_sb[:, :], bass.AP(smalls, 0, [[1, 16], [1, 1]]), writes=[("dtb",)])
                trk.dma("q1", alog_sb[:, :], bass.AP(smalls, 64, [[1, 16], [1, 1]]), writes=[("alog",)])
                trk.dma("q1", nw_sb[0:64, :], bass.AP(smalls, 128, [[1, 64], [1, 1]]), writes=[("nw",)])
                trk.dma("q1", nw_sb[64:128, :], bass.AP(smalls, 128, [[1, 64], [1, 1]]), writes=[("nw",)])
                for gi in range(16):
                    trk.dma("q1", cwsb[:, gi * 4:(gi + 1) * 4], convw[gi * 128:(gi + 1) * 128, :],
                            writes=[("cw", gi)])
                trk.pre("p", writes=[("halo_sb",)])
                g.memset(halo_sb[:], 0.0)
                with g.If_eq(r_par, 1):
                    g.reg_mov(r_loop, 1)
                    with g.While(r_loop):
                        g.dma_start(out=pfl[0:1, 7:8], in_=shFlg[0, 7:8]).then_inc(pollsem, 16)
                        g.reg_add(r_pw, r_pw, 16)
                        g.wait_ge(pollsem, r_pw)
                        g.reg_load(r_val, pfl[0:1, 7:8])
                        g.reg_alu(r_loop, r_val, FLAGV + 7, OP.bitwise_xor)
                    g.dma_start(out=halo_sb[:, :], in_=shHalo[:, :]).then_inc(pollsem, 16)
                    g.reg_add(r_pw, r_pw, 16)
                    g.wait_ge(pollsem, r_pw)
                inst = g.engine_nop()
                trk.post("p", inst, writes=[("halo_sb",)])

                stLN = ExitStack()
                ln2xT = stLN.enter_context(nc.sbuf_tensor(un("at_lnT"), [128, 16 * 1024], BF16))
                with ExitStack() as stL:
                    xt = stL.enter_context(nc.sbuf_tensor(un("at_xt"), [128, 2 * D], F32))
                    sq = stL.enter_context(nc.sbuf_tensor(un("at_sq"), [128, D], F32))
                    xn = stL.enter_context(nc.sbuf_tensor(un("at_xn"), [128, D], BF16))
                    lns = stL.enter_context(nc.sbuf_tensor(un("at_lns"), [128, 8], F32))
                    psT = [stL.enter_context(nc.psum_tensor(un("at_psT"), [128, 512], BF16)) for _ in range(2)]
                    for tt in range(NTT):
                        xb = tt % 2
                        xv = xt[:, xb * D:(xb + 1) * D]
                        trk.dma("q0", xv, xsrc[tt * 128:(tt + 1) * 128, :],
                                reads=([(src_key, tt)] if src_key else []), writes=[("at_xt", xb)])
                        trk.ve(nc.vector.tensor_reduce, lns[:, 0:1], xv, axis=AX, op=OP.add,
                               reads=[("at_xt", xb)], writes=[("aln0",)])
                        trk.se(nc.scalar.activation, sq[:], xv, AF.Square,
                               reads=[("at_xt", xb)], writes=[("at_sq",)])
                        trk.ve(nc.vector.tensor_reduce, lns[:, 1:2], sq[:], axis=AX, op=OP.add,
                               reads=[("at_sq",)], writes=[("aln1",)])
                        trk.se(nc.scalar.activation, lns[:, 2:3], lns[:, 0:1], AF.Copy, scale=-1.0 / D,
                               reads=[("aln0",)], writes=[("anegm",)])
                        trk.ve(nc.vector.tensor_scalar, lns[:, 3:4], lns[:, 1:2], 1.0 / D, None, op0=OP.mult,
                               reads=[("aln1",)], writes=[("aex2",)])
                        trk.ve(nc.vector.tensor_tensor, lns[:, 4:5], lns[:, 2:3], lns[:, 2:3], op=OP.mult,
                               reads=[("anegm",)], writes=[("am2",)])
                        trk.ve(nc.vector.tensor_tensor, lns[:, 5:6], lns[:, 3:4], lns[:, 4:5], op=OP.subtract,
                               reads=[("aex2",), ("am2",)], writes=[("avar",)])
                        trk.ve(nc.vector.tensor_scalar, lns[:, 5:6], lns[:, 5:6], 1e-5, None, op0=OP.add,
                               reads=[("avar",)], writes=[("avar",)])
                        trk.ve(nc.vector.reciprocal, lns[:, 6:7], lns[:, 5:6],
                               reads=[("avar",)], writes=[("arv",)])
                        trk.se(nc.scalar.activation, lns[:, 7:8], lns[:, 6:7], AF.Sqrt,
                               reads=[("arv",)], writes=[("arstd",)])
                        trk.ve(nc.vector.tensor_scalar, xn[:], xv, lns[:, 2:3], lns[:, 7:8],
                               op0=OP.add, op1=OP.mult,
                               reads=[("at_xt", xb), ("anegm",), ("arstd",)], writes=[("at_xn",)])
                        for kb in range(4):
                            pb = kb % 2
                            with trk.tg(reads=[("at_xn",), ("cIDT",)], writes=[("at_psT", pb)]) as t:
                                for j in range(4):
                                    k = kb * 4 + j
                                    t.add(nc.tensor.transpose(psT[pb][:, j * 128:(j + 1) * 128],
                                                              xn[:, k * 128:(k + 1) * 128], cIDT[:]))
                            src = bass.AP(psT[pb], 0, [[512, 128], [128, 4], [1, 128]])
                            dst = bass.AP(ln2xT, (kb * 4) * 1024 + tt * 128,
                                          [[16 * 1024, 128], [1024, 4], [1, 128]])
                            trk.ve(nc.vector.tensor_copy, dst, src,
                                   reads=[("at_psT", pb)],
                                   writes=[("lnT", kb * 4 + j2) for j2 in range(4)])
                trk.join()
                if _ACUT == 1:
                    return

                rd_lnT = [("lnT", k) for k in range(16)]
                with ExitStack() as stP:
                    qk_fm = stP.enter_context(nc.sbuf_tensor(un("at_qk"), [128, 8 * 1024], BF16))
                    wsl = [stP.enter_context(nc.sbuf_tensor(un("at_wsl"), [128, 16 * 256], BF16)) for _ in range(2)]
                    cin = stP.enter_context(nc.sbuf_tensor(un("at_cin"), [128, 1028], F32))
                    cvo = stP.enter_context(nc.sbuf_tensor(un("at_cvo"), [128, 1024], F32))
                    sil = stP.enter_context(nc.sbuf_tensor(un("at_sil"), [128, 1024], F32))
                    sq2 = stP.enter_context(nc.sbuf_tensor(un("at_sq2"), [128, 1024], F32))
                    nr2 = stP.enter_context(nc.sbuf_tensor(un("at_nr2"), [2, 1024], F32))
                    vt_s = stP.enter_context(nc.sbuf_tensor(un("at_vts"), [128, 1024], BF16))
                    zt_s = stP.enter_context(nc.sbuf_tensor(un("at_zts"), [128, 1024], BF16))
                    psP = [stP.enter_context(nc.psum_tensor(un("at_psP"), [128, 1024], F32)) for _ in range(2)]
                    psN = stP.enter_context(nc.psum_tensor(un("at_psN"), [2, 512], F32))
                    psB = stP.enter_context(nc.psum_tensor(un("at_psB"), [128, 512], F32))
                    psV = stP.enter_context(nc.psum_tensor(un("at_psV"), [128, 512], BF16))
                    groups = [("qkv", shWqkv, gi, ("shWqkvb",)) for gi in range(8)] + \
                             [("z", shWz, gi, ("shWzb",)) for gi in range(4)]
                    for grpidx, (kind, wsh, gi, wkey) in enumerate(groups):
                        rot = grpidx % 2
                        for k in range(16):
                            trk.dma("q0", wsl[rot][:, k * 256:(k + 1) * 256],
                                    wsh[gi * 256:(gi + 1) * 256, k * 128:(k + 1) * 128],
                                    reads=[wkey], writes=[("wsl", rot, k)], transpose=True)
                        for ct in range(2):
                            gti0 = gi * 2 + ct
                            pp = psP[(grpidx * 2 + ct) % 2]
                            ppk = ("psP", (grpidx * 2 + ct) % 2)
                            rdw = [("wsl", rot, k) for k in range(16)]
                            with trk.tg(reads=rdw + rd_lnT, writes=[ppk]) as t:
                                for k in range(16):
                                    for th2 in range(2):
                                        t.add(nc.tensor.matmul(
                                            pp[:, th2 * 512:(th2 + 1) * 512],
                                            wsl[rot][:, k * 256 + ct * 128:k * 256 + (ct + 1) * 128],
                                            ln2xT[:, k * 1024 + th2 * 512:k * 1024 + (th2 + 1) * 512],
                                            start=(k == 0), stop=(k == 15)))
                            if kind == "z":
                                zi = gi * 2 + ct
                                trk.se(nc.scalar.activation, zt_s[:], pp[:], AF.Silu,
                                       reads=[ppk], writes=[("zt",)])
                                trk.dma("q2", zsp[zi * 128:(zi + 1) * 128, :], zt_s[:],
                                        reads=[("zt",)], writes=[("zsp", zi)])
                                continue
                            gti = gti0
                            trk.se(nc.scalar.activation, cin[:, 3:1027], pp[:], AF.Copy,
                                   reads=[ppk], writes=[("cin",)])
                            trk.ve(nc.vector.tensor_copy, halo_st[:, gti * 3:(gti + 1) * 3],
                                   cin[:, 1024:1027], reads=[("cin",)], writes=[("hst", gti)])
                            trk.ve(nc.vector.tensor_copy, cin[:, 0:3], halo_sb[:, gti * 3:(gti + 1) * 3],
                                   reads=[("halo_sb",)], writes=[("cin",)])
                            trk.ve(nc.vector.tensor_scalar, cvo[:], cin[:, 0:1024],
                                   cwsb[:, gti * 4:gti * 4 + 1], None, op0=OP.mult,
                                   reads=[("cin",), ("cw", gti)], writes=[("cvo",)])
                            for tap in range(1, 4):
                                trk.ve(nc.vector.scalar_tensor_tensor, cvo[:], cin[:, tap:1024 + tap],
                                       cwsb[:, gti * 4 + tap:gti * 4 + tap + 1], cvo[:],
                                       op0=OP.mult, op1=OP.add,
                                       reads=[("cin",), ("cw", gti), ("cvo",)], writes=[("cvo",)])
                            trk.se(nc.scalar.activation, sil[:], cvo[:], AF.Silu,
                                   reads=[("cvo",)], writes=[("sil",)])
                            if gti < 8:
                                trk.se(nc.scalar.activation, sq2[:], sil[:], AF.Square,
                                       reads=[("sil",)], writes=[("sq2",)])
                                for th2 in range(2):
                                    with trk.tg(reads=[("sq2",), ("bd2",)], writes=[("psN",)]) as t:
                                        t.add(nc.tensor.matmul(psN[:, :], bd2[:],
                                                               sq2[:, th2 * 512:(th2 + 1) * 512],
                                                               start=True, stop=True))
                                    trk.ve(nc.vector.tensor_scalar, nr2[:, 0:512], psN[:], 1e-6, None,
                                           op0=OP.add, reads=[("psN",)], writes=[("nr2",)])
                                    trk.ve(nc.vector.reciprocal, nr2[:, 0:512], nr2[:, 0:512],
                                           reads=[("nr2",)], writes=[("nr2",)])
                                    scl = 0.015625 if gti < 4 else 1.0
                                    trk.se(nc.scalar.activation, nr2[:, 0:512], nr2[:, 0:512], AF.Sqrt,
                                           scale=scl, reads=[("nr2",)], writes=[("nr2",)])
                                    with trk.tg(reads=[("nr2",), ("b2up",)], writes=[("psB",)]) as t:
                                        t.add(nc.tensor.matmul(psB[:, :], b2up[:], nr2[:, 0:512],
                                                               start=True, stop=True))
                                    trk.ve(nc.vector.tensor_tensor,
                                           qk_fm[:, gti * 1024 + th2 * 512:gti * 1024 + (th2 + 1) * 512],
                                           sil[:, th2 * 512:(th2 + 1) * 512], psB[:], op=OP.mult,
                                           reads=[("sil",), ("psB",)], writes=[("qk", gti)])
                                dstf = q_f64 if gti < 4 else k_f64
                                dk0 = ("qf64",) if gti < 4 else ("kf64",)
                                hb2 = (gti % 4) * 2
                                trk.ve(nc.vector.tensor_copy, dstf[:, hb2 * 1024:(hb2 + 1) * 1024],
                                       qk_fm[0:64, gti * 1024:(gti + 1) * 1024],
                                       reads=[("qk", gti)], writes=[dk0])
                                for th2 in range(2):
                                    with trk.tg(reads=[("qk", gti), ("cIDT",)], writes=[("psB",)]) as t:
                                        t.add(nc.tensor.matmul(
                                            psB[0:64, :], cIDT[64:128, 64:128],
                                            bass.AP(qk_fm, 64 * 8192 + gti * 1024 + th2 * 512,
                                                    [[8192, 64], [1, 512]]),
                                            start=True, stop=True))
                                    trk.ve(nc.vector.tensor_copy,
                                           dstf[:, (hb2 + 1) * 1024 + th2 * 512:(hb2 + 1) * 1024 + (th2 + 1) * 512],
                                           psB[0:64, :], reads=[("psB",)], writes=[dk0])
                                if gti >= 4:
                                    kt = gti - 4
                                    for cc in range(16):
                                        with trk.tg(reads=[("qk", gti), ("cIDT",)], writes=[("psV",)]) as t:
                                            t.add(nc.tensor.transpose(
                                                psV[0:64, 0:128],
                                                qk_fm[:, gti * 1024 + cc * 64:gti * 1024 + (cc + 1) * 64],
                                                cIDT[:]))
                                        trk.ve(nc.vector.tensor_copy,
                                               k_cm[:, cc * 512 + kt * 128:cc * 512 + (kt + 1) * 128],
                                               psV[0:64, 0:128],
                                               reads=[("psV",)], writes=[("kcm", cc, kt)])
                            else:
                                vi = gti - 8
                                trk.ve(nc.vector.tensor_copy, vt_s[:], sil[:],
                                       reads=[("sil",)], writes=[("vts",)])
                                for cc in range(16):
                                    with trk.tg(reads=[("vts",), ("cIDT",)], writes=[("psV",)]) as t:
                                        t.add(nc.tensor.transpose(psV[0:64, 0:128],
                                                                  vt_s[:, cc * 64:(cc + 1) * 64], cIDT[:]))
                                    trk.ve(nc.vector.tensor_copy,
                                           v_cm[:, cc * 1024 + vi * 128:cc * 1024 + (vi + 1) * 128],
                                           psV[0:64, 0:128],
                                           reads=[("psV",)], writes=[("vcm", cc, vi)])
                    trk.pre("p", reads=[("hst", gtj) for gtj in range(16)])
                    with g.If_eq(r_par, 0):
                        g.dma_start(out=shHalo[:, :], in_=halo_st[:, :]).then_inc(pollsem, 16)
                        g.reg_add(r_pw, r_pw, 16)
                        g.wait_ge(pollsem, r_pw)
                        g.memset(ftile[0:1, 7:8], FLAGV + 7)
                        g.dma_start(out=shFlg[0, 7:8], in_=ftile[0:1, 7:8]).then_inc(pollsem, 16)
                        g.reg_add(r_pw, r_pw, 16)
                        g.wait_ge(pollsem, r_pw)
                    inst = g.engine_nop()
                    trk.post("p", inst, reads=[("hst", gtj) for gtj in range(16)])
                    for k in range(16):
                        trk.dma("q0", wsl[0][:, k * 256:k * 256 + 32],
                                shWba[0:32, k * 128:(k + 1) * 128],
                                reads=[("shWbab",)], writes=[("wsl", 0, k)], transpose=True)
                    with trk.tg(reads=[("wsl", 0, k) for k in range(16)] + rd_lnT, writes=[("psP", 0)]) as t:
                        for k in range(16):
                            for th2 in range(2):
                                t.add(nc.tensor.matmul(psP[0][0:16, th2 * 512:(th2 + 1) * 512],
                                                       wsl[0][:, k * 256:k * 256 + 16],
                                                       ln2xT[:, k * 1024 + th2 * 512:k * 1024 + (th2 + 1) * 512],
                                                       start=(k == 0), stop=(k == 15)))
                    with trk.tg(reads=[("wsl", 0, k) for k in range(16)] + rd_lnT, writes=[("psP", 1)]) as t:
                        for k in range(16):
                            for th2 in range(2):
                                t.add(nc.tensor.matmul(psP[1][0:16, th2 * 512:(th2 + 1) * 512],
                                                       wsl[0][:, k * 256 + 16:k * 256 + 32],
                                                       ln2xT[:, k * 1024 + th2 * 512:k * 1024 + (th2 + 1) * 512],
                                                       start=(k == 0), stop=(k == 15)))
                    trk.se(nc.scalar.activation, b_fm[:], psP[0][0:16, :], AF.Sigmoid,
                           reads=[("psP", 0)], writes=[("bfm",)])
                    trk.ve(nc.vector.tensor_scalar, g_fm[:], psP[1][0:16, :], dtb_sb[:, 0:1], None,
                           op0=OP.add, reads=[("psP", 1), ("dtb",)], writes=[("ga",)])
                    trk.se(nc.scalar.activation, sq2[0:16, :], g_fm[:], AF.Abs,
                           reads=[("ga",)], writes=[("gau",)])
                    trk.se(nc.scalar.activation, sq2[0:16, :], sq2[0:16, :], AF.Exp, scale=-1.0,
                           reads=[("gau",)], writes=[("gau",)])
                    trk.se(nc.scalar.activation, sq2[0:16, :], sq2[0:16, :], AF.Ln, bias=1.0,
                           reads=[("gau",)], writes=[("gau",)])
                    trk.ve(nc.vector.tensor_scalar, sil[0:16, :], g_fm[:], 0.0, None, op0=OP.max,
                           reads=[("ga",)], writes=[("gr",)])
                    trk.ve(nc.vector.tensor_tensor, g_fm[:], sil[0:16, :], sq2[0:16, :], op=OP.add,
                           reads=[("gr",), ("gau",)], writes=[("gfm",)])
                    trk.se(nc.scalar.activation, negea[:], alog_sb[:], AF.Exp,
                           reads=[("alog",)], writes=[("nea",)])
                    trk.ve(nc.vector.tensor_scalar, negea[:], negea[:], -1.0, None, op0=OP.mult,
                           reads=[("nea",)], writes=[("nea",)])
                    trk.ve(nc.vector.tensor_scalar, g_fm[:], g_fm[:], negea[:, 0:1], None, op0=OP.mult,
                           reads=[("gsp",), ("nea",)], writes=[("gfm",)])
                trk.join()
                stLN.close()
                trk.join()
                if _DBGA:
                    trk.dma("q2", d_lnT[:, :], ln2xT[:, 0:1024], reads=[("lnT", 0)])
                    trk.dma("q2", d_q[:, :], q_f64[0:64, :], reads=[("qf64",)])
                    trk.dma("q2", d_k[:, :], k_f64[0:64, :], reads=[("kf64",)])
                    trk.dma("q2", d_vcm[:, :], v_cm[0:64, :],
                            reads=[("vcm", cc2, vi2) for cc2 in range(16) for vi2 in range(8)])
                    trk.dma("q2", d_kcm[:, :], k_cm[0:64, :],
                            reads=[("kcm", cc2, kt2) for cc2 in range(16) for kt2 in range(4)])
                    trk.dma("q2", d_gb[:, 0:1024], g_fm[:, :], reads=[("gfm",)])
                    trk.dma("q2", d_gb[:, 1024:2048], b_fm[:, :], reads=[("bfm",)])
                    trk.dma("q2", d_z[:, :], zsp[0:128, :], reads=[("zsp", 0)])
                if _ACUT == 2:
                    return

                with ExitStack() as stQ:
                    psC = stQ.enter_context(nc.psum_tensor(un("at_psC"), [128, 256], F32))
                    psC2 = stQ.enter_context(nc.psum_tensor(un("at_psC2"), [128, 256], F32))
                    for c in range(16):
                        with trk.tg(reads=[("gfm",), ("idf",)], writes=[("psC",)]) as t:
                            t.add(nc.tensor.transpose(psC[0:64, c * 16:(c + 1) * 16],
                                                      g_fm[:, c * 64:(c + 1) * 64], idf[0:16, 0:16]))
                        with trk.tg(reads=[("bfm",), ("idf",)], writes=[("psC2",)]) as t:
                            t.add(nc.tensor.transpose(psC2[0:64, c * 16:(c + 1) * 16],
                                                      b_fm[:, c * 64:(c + 1) * 64], idf[0:16, 0:16]))
                    trk.ve(nc.vector.tensor_copy, gT2[:], psC[0:64, :], reads=[("psC",)], writes=[("gT2",)])
                    trk.ve(nc.vector.tensor_copy, bT2[:], psC2[0:64, :], reads=[("psC2",)], writes=[("bT2",)])
                    with trk.tg(reads=[("gT2",), ("l64u",)], writes=[("psC",)]) as t:
                        t.add(nc.tensor.matmul(psC[0:64, :], l64u[:],
                                               gT2[0:64, :], start=True, stop=True))
                    with trk.tg(reads=[("gT2",), ("l64s",)], writes=[("psC2",)]) as t:
                        t.add(nc.tensor.matmul(psC2[0:64, :], l64s[:],
                                               gT2[0:64, :], start=True, stop=True))
                    trk.ve(nc.vector.tensor_copy, cum2[:], psC[0:64, :],
                           reads=[("psC",)], writes=[("cum2",)])
                    trk.se(nc.scalar.activation, crev2[:], psC2[0:64, :], AF.Exp,
                           reads=[("psC2",)], writes=[("crev2",)])
                    trk.se(nc.scalar.activation, gam2[:], cum2[:], AF.Exp,
                           reads=[("cum2",)], writes=[("gam2",)])
                    trk.ve(nc.vector.tensor_tensor, bg2[:], bT2[:], gam2[:], op=OP.mult,
                           reads=[("bT2",), ("gam2",)], writes=[("bg2",)])
                    trk.dma("q2", scrd2[0:1, 0:256], cum2[63:64, :], reads=[("cum2",)], writes=[("scr2",)])
                    trk.dma("q2", eccr[:, :], scrd2[0:1, 0:256], reads=[("scr2",)], writes=[("eccr",)])
                    with trk.tg(reads=[("eccr",), ("onesr",)], writes=[("psC",)]) as t:
                        t.add(nc.tensor.matmul(psC[:, :], onesr[:], eccr[:], start=True, stop=True))
                    trk.se(nc.scalar.activation, ecc2[:], psC[0:64, :], AF.Exp,
                           reads=[("psC",)], writes=[("ecc2",)])
                    for half in range(2):
                        with trk.tg(reads=[("cum2",), ("idf",)], writes=[("psC2",)]) as t:
                            t.add(nc.tensor.transpose(psC2[:, half * 64:(half + 1) * 64],
                                                      cum2[0:64, half * 128:(half + 1) * 128],
                                                      idf[0:64, 0:64]))
                    trk.ve(nc.vector.tensor_copy, cumT2[:], psC2[:, 0:128],
                           reads=[("psC2",)], writes=[("cumT2",)])
                trk.join()
                if _DBGA:
                    trk.dma("q2", d_cum[:, 0:256], cum2[0:64, :], reads=[("cum2",)])
                    trk.dma("q2", d_cum[:, 256:512], crev2[0:64, :], reads=[("crev2",)])
                    trk.dma("q2", d_cum[:, 512:768], gam2[0:64, :], reads=[("gam2",)])
                    trk.dma("q2", d_cum[:, 768:1024], bg2[0:64, :], reads=[("bg2",)])
                    trk.dma("q2", d_cum[:, 1024:1280], ecc2[0:64, :], reads=[("ecc2",)])
                    trk.dma("q2", d_cumT[:, :], cumT2[:, :], reads=[("cumT2",)])
                if _ACUT == 3:
                    return

                y_fm = A_("at_y", [128, 8 * 1024], BF16)
                S_run = A_("at_Srun", [64, 16 * 64], F32)
                S_bf = A_("at_Sbf", [64, 16 * 64], BF16)
                ssc_bf = A_("at_ssc", [64, 16 * 64], BF16)
                trk.ve(nc.vector.memset, S_run[:], 0.0, writes=[("Srun",)])
                trk.pre("p", writes=[("Srun",)])
                with g.If_eq(r_par, 1):
                    g.reg_mov(r_loop, 1)
                    with g.While(r_loop):
                        g.dma_start(out=pfl[0:1, 8:9], in_=shFlg[0, 8:9]).then_inc(pollsem, 16)
                        g.reg_add(r_pw, r_pw, 16)
                        g.wait_ge(pollsem, r_pw)
                        g.reg_load(r_val, pfl[0:1, 8:9])
                        g.reg_alu(r_loop, r_val, FLAGV + 8, OP.bitwise_xor)
                    g.dma_start(out=S_run[0:64, :], in_=shS16[0:64, :]).then_inc(pollsem, 16)
                    g.reg_add(r_pw, r_pw, 16)
                    g.wait_ge(pollsem, r_pw)
                inst = g.engine_nop()
                trk.post("p", inst, writes=[("Srun",)])
                trk.ve(nc.vector.tensor_copy, S_bf[:], S_run[:], reads=[("Srun",)], writes=[("Sbf",)])

                if _ACUT == 4:
                    return
                with ExitStack() as stK:
                    K_ = lambda name, shape, dt=F32: stK.enter_context(nc.sbuf_tensor(un(name), shape, dt))
                    rX = [K_("at_rX", [128, 16 * 128], BF16) for _ in range(4)]
                    rP = [K_("at_rP", [128, 16 * 64], BF16) for _ in range(4)]
                    rQg = [K_("at_rQg", [128, 16 * 64], BF16) for _ in range(4)]
                    rKd = [K_("at_rKd", [128, 16 * 64], BF16) for _ in range(4)]
                    sc1 = K_("at_sc1", [128, 1024], F32)
                    cpR = K_("at_cpR", [64, 1024], F32)
                    ssc_f = K_("at_sscf", [64, 1024], F32)
                    e1_sb = K_("at_e1", [128, 1024], F32)
                    e2_sb = K_("at_e2", [128, 1024], F32)
                    gam_sb = K_("at_gamrep", [128, 1024], F32)
                    mkk_sb = K_("at_mkk", [128, 512], BF16)
                    mkq_sb = K_("at_mkq", [128, 512], BF16)
                    negA = K_("at_negA", [128, 16 * 64], BF16)
                    negAT = K_("at_negAT", [128, 16 * 64], BF16)
                    nbuf = K_("at_nb", [128, 16 * 64], BF16)
                    ntbuf = K_("at_ntb", [128, 16 * 64], BF16)
                    tkgT = K_("at_tkgT", [128, 16 * 64], BF16)
                    u_sb = K_("at_u", [128, 16 * 64], BF16)
                    psR = stK.enter_context(nc.psum_tensor(un("at_psR"), [128, 1024], F32))
                    psM = stK.enter_context(nc.psum_tensor(un("at_psM"), [128, 512], F32))
                    psM2 = stK.enter_context(nc.psum_tensor(un("at_psM2"), [128, 512], F32))
                    psS1 = stK.enter_context(nc.psum_tensor(un("at_psS1"), [128, 512], F32))
                    psS2 = stK.enter_context(nc.psum_tensor(un("at_psS2"), [128, 512], F32))
                    psX = [stK.enter_context(nc.psum_tensor(un("at_psX"), [128, 512], F32)) for _ in range(2)]
                    psS1b = psS1.bitcast(BF16)
                    psS2b = psS2.bitcast(BF16)
                    for c in range(_CCUT):
                        p = 0
                        tb = c // 2
                        rb = c % 4
                        chb = c * 16
                        half = c // 8
                        if _P1MASK & 1:
                            trk.dma("q2", scrd2[0:1, 256:1280],
                                    bass.AP(cumT2, ((c % 8) * 16) * 128 + half * 64, [[128, 16], [1, 64]]),
                                    reads=[("cumT2",)], writes=[("cflr",)])
                            trk.dma("q2", cfl_sb[:, :], scrd2[0:1, 256:1280],
                                    reads=[("cflr",)], writes=[("cfl",)])
                        if _P1MASK & 2:
                            with trk.tg(reads=[("cfl",), ("onesr",)], writes=[("psR",)]) as t:
                                for nn2 in range(2):
                                    t.add(nc.tensor.matmul(psR[:, nn2 * 512:(nn2 + 1) * 512], onesr[:],
                                                           cfl_sb[:, nn2 * 512:(nn2 + 1) * 512],
                                                           start=True, stop=True))
                        if _P1MASK & 8:
                            trk.ve(nc.vector.tensor_copy, cpR[:],
                                   bass.AP(psR, 0, [[1024, 64], [1, 1024]]),
                                   reads=[("psR",)], writes=[("cpR",)])
                            trk.se(nc.scalar.activation, gam_sb[0:64, :], cpR[:], AF.Exp,
                                   reads=[("cpR",)], writes=[("gamr",)])
                        edp = e2_sb[p:p + 64, :]
                        mui = bass.AP(mUI2, p * 64, [[64, 64], [0, 16], [1, 64]])
                        msl = bass.AP(mSL2, p * 64, [[64, 64], [0, 16], [1, 64]])
                        if _P1MASK & 4:
                            in1 = bass.AP(cum2, p * 256 + chb, [[256, 64], [1, 16], [0, 64]])
                            trk.ve(nc.vector.tensor_tensor, e2_sb[p:p + 64, :],
                                   cpR[:], in1, op=OP.subtract,
                                   reads=[("cpR",), ("cum2",)], writes=[("e2",)])
                        if _P1MASK & 16:
                            trk.ve(nc.vector.tensor_tensor, sc1[p:p + 64, :], edp, mui, op=OP.mult,
                                   reads=[("e2",), ("mui",)], writes=[("sc1",)])
                            trk.se(nc.scalar.activation, e1_sb[p:p + 64, :], sc1[p:p + 64, :], AF.Exp,
                                   reads=[("sc1",)], writes=[("e1",)])
                        if _P1MASK & 32:
                            trk.ve(nc.vector.scalar_tensor_tensor, sc1[p:p + 64, :], edp, -1.0, msl,
                                   op0=OP.mult, op1=OP.mult,
                                   reads=[("e2",), ("msl",)], writes=[("sc1",)])
                            trk.se(nc.scalar.activation, e2_sb[p:p + 64, :], sc1[p:p + 64, :], AF.Exp,
                                   reads=[("sc1",)], writes=[("e2",)])
                        if _KCUT < 2:
                            continue
                        with trk.tg(reads=[("kf64",)], writes=[("psM",)]) as t:
                            for kh in range(8):
                                ksl = k_f64[:, kh * 1024 + c * 64:kh * 1024 + (c + 1) * 64]
                                t.add(nc.tensor.matmul(psM[0:64, kh * 64:(kh + 1) * 64], ksl, ksl,
                                                      start=True, stop=True))
                        with trk.tg(reads=[("kf64",), ("qf64",)], writes=[("psM2",)]) as t:
                            for kh in range(8):
                                ksl = k_f64[:, kh * 1024 + c * 64:kh * 1024 + (c + 1) * 64]
                                qsl = q_f64[:, kh * 1024 + c * 64:kh * 1024 + (c + 1) * 64]
                                t.add(nc.tensor.matmul(psM2[0:64, kh * 64:(kh + 1) * 64], ksl, qsl,
                                                      start=True, stop=True))
                        trk.ve(nc.vector.tensor_copy, mkk_sb[p:p + 64, :], psM[p:p + 64, :],
                               reads=[("psM",)], writes=[("mkk",)])
                        trk.ve(nc.vector.tensor_copy, mkq_sb[p:p + 64, :], psM2[p:p + 64, :],
                               reads=[("psM2",)], writes=[("mkq",)])
                        mkqb = bass.AP(mkq_sb, 0, [[512, 64], [64, 8], [1, 64]])
                        trk.ve(nc.vector.tensor_tensor, sc1[p:p + 64, :], e1_sb[p:p + 64, :], mui, op=OP.mult,
                               reads=[("e1",), ("mui",)], writes=[("sc1",)])
                        for hl in range(2):
                            trk.ve(nc.vector.tensor_tensor,
                                   bass.AP(rP[rb], hl * 64, [[1024, 64], [128, 8], [1, 64]]),
                                   bass.AP(sc1, hl * 64, [[1024, 64], [128, 8], [1, 64]]), mkqb,
                                   op=OP.mult, reads=[("sc1",), ("mkq",)], writes=[("rP", rb)])
                        btb = bass.AP(bT2, p * 256 + chb, [[256, 64], [1, 16], [0, 64]])
                        trk.ve(nc.vector.tensor_tensor, sc1[p:p + 64, :], e2_sb[p:p + 64, :], btb, op=OP.mult,
                               reads=[("e2",), ("bT2",)], writes=[("sc1",)])
                        trk.ve(nc.vector.tensor_tensor, sc1[p:p + 64, :], sc1[p:p + 64, :], msl, op=OP.mult,
                               reads=[("sc1",), ("msl",)], writes=[("sc1",)])
                        mkkb = bass.AP(mkk_sb, 0, [[512, 64], [64, 8], [1, 64]])
                        for hl in range(2):
                            trk.ve(nc.vector.scalar_tensor_tensor,
                                   bass.AP(negA, hl * 64, [[1024, 64], [128, 8], [1, 64]]),
                                   bass.AP(sc1, hl * 64, [[1024, 64], [128, 8], [1, 64]]),
                                   -1.0, mkkb, op0=OP.mult, op1=OP.mult,
                                   reads=[("sc1",), ("mkk",)], writes=[("negAk",)])
                        with trk.tg(reads=[("negAk",), ("cIDT",)], writes=[("psS1",)]) as t:
                            for h in range(16):
                                t.add(nc.tensor.transpose(psS1b[p:p + 64, h * 64:(h + 1) * 64],
                                                          negA[p:p + 64, h * 64:(h + 1) * 64],
                                                          cIDT[p:p + 64, p:p + 64]))
                        trk.ve(nc.vector.tensor_copy, negAT[p:p + 64, :], psS1b[p:p + 64, :],
                               reads=[("psS1",)], writes=[("negATk",)])
                        vsl = bass.AP(v_cm, c * 1024, [[16 * 1024, 64], [1, 1024]])
                        btc = bass.AP(bT2, p * 256 + chb, [[256, 64], [1, 16], [0, 64]])
                        trk.ve(nc.vector.tensor_tensor,
                               bass.AP(rX[rb], p * 2048, [[2048, 64], [128, 16], [1, 64]]),
                               vsl, btc, op=OP.mult,
                               reads=[("vcm", c, vi2) for vi2 in range(8)] + [("bT2",)],
                               writes=[("rX", rb)])
                        ksl2 = bass.AP(k_cm, c * 512, [[16 * 512, 64], [64, 8], [1, 64]])
                        for hl in range(2):
                            trk.ve(nc.vector.tensor_tensor,
                                   bass.AP(rX[rb], 64 + hl * 128, [[2048, 64], [256, 8], [1, 64]]),
                                   ksl2, bass.AP(bg2, chb + hl, [[256, 64], [2, 8], [0, 64]]),
                                   op=OP.mult,
                                   reads=[("kcm", c, kt2) for kt2 in range(4)] + [("bg2",)],
                                   writes=[("rX", rb)])
                            trk.ve(nc.vector.tensor_tensor,
                                   bass.AP(rKd[rb], hl * 64, [[1024, 64], [128, 8], [1, 64]]),
                                   ksl2, bass.AP(crev2, chb + hl, [[256, 64], [2, 8], [0, 64]]),
                                   op=OP.mult,
                                   reads=[("kcm", c, kt2) for kt2 in range(4)] + [("crev2",)],
                                   writes=[("rKd", rb)])
                            trk.ve(nc.vector.tensor_tensor,
                                   bass.AP(rQg[rb], hl * 64, [[1024, 64], [128, 8], [1, 64]]),
                                   bass.AP(q_f64, c * 64, [[8192, 64], [1024, 8], [1, 64]]),
                                   bass.AP(gam_sb, hl * 64, [[1024, 64], [128, 8], [1, 64]]),
                                   op=OP.mult,
                                   reads=[("qf64",), ("gamr",)], writes=[("rQg", rb)])
                        if _KCUT < 3:
                            continue
                        curN, curNT = negA, negAT
                        ckN, ckNT = ("negAk",), ("negATk",)
                        for lvl in range(6):
                            for wave in range(2):
                                for hh in range(8):
                                    h = wave * 8 + hh
                                    bank = hh // 4
                                    win = (hh % 4) * 128
                                    with trk.tg(reads=[ckN, ckNT, ("rX", rb)],
                                                writes=[("psX", bank)]) as t:
                                        t.add(nc.tensor.matmul(psX[bank][p:p + 64, win:win + 128],
                                                               curNT[p:p + 64, h * 64:(h + 1) * 64],
                                                               rX[rb][p:p + 64, h * 128:(h + 1) * 128],
                                                               start=True, stop=True))
                                for bank in range(2):
                                    trk.ve(nc.vector.tensor_tensor,
                                           rX[rb][p:p + 64, wave * 1024 + bank * 512:wave * 1024 + (bank + 1) * 512],
                                           psX[bank][p:p + 64, :],
                                           rX[rb][p:p + 64, wave * 1024 + bank * 512:wave * 1024 + (bank + 1) * 512],
                                           op=OP.add,
                                           reads=[("psX", bank), ("rX", rb)], writes=[("rX", rb)])
                            if lvl < 5:
                                if lvl % 2 == 0:
                                    nn, nt, knn, knt = nbuf, ntbuf, ("nb", 0), ("ntb", 0)
                                else:
                                    nn, nt, knn, knt = negA, negAT, ("negAk",), ("negATk",)
                                with trk.tg(reads=[ckN, ckNT], writes=[("psS1",)]) as t:
                                    for h in range(8):
                                        t.add(nc.tensor.matmul(psS1[p:p + 64, h * 64:(h + 1) * 64],
                                                               curNT[p:p + 64, h * 64:(h + 1) * 64],
                                                               curN[p:p + 64, h * 64:(h + 1) * 64],
                                                               start=True, stop=True))
                                with trk.tg(reads=[ckN, ckNT], writes=[("psS2",)]) as t:
                                    for h in range(8, 16):
                                        t.add(nc.tensor.matmul(psS2[p:p + 64, (h - 8) * 64:(h - 7) * 64],
                                                               curNT[p:p + 64, h * 64:(h + 1) * 64],
                                                               curN[p:p + 64, h * 64:(h + 1) * 64],
                                                               start=True, stop=True))
                                with trk.tg(reads=[ckN, ckNT], writes=[("psM",)]) as t:
                                    for h in range(8):
                                        t.add(nc.tensor.matmul(psM[p:p + 64, h * 64:(h + 1) * 64],
                                                               curN[p:p + 64, h * 64:(h + 1) * 64],
                                                               curNT[p:p + 64, h * 64:(h + 1) * 64],
                                                               start=True, stop=True))
                                with trk.tg(reads=[ckN, ckNT], writes=[("psM2",)]) as t:
                                    for h in range(8, 16):
                                        t.add(nc.tensor.matmul(psM2[p:p + 64, (h - 8) * 64:(h - 7) * 64],
                                                               curN[p:p + 64, h * 64:(h + 1) * 64],
                                                               curNT[p:p + 64, h * 64:(h + 1) * 64],
                                                               start=True, stop=True))
                                trk.ve(nc.vector.tensor_copy, nn[p:p + 64, 0:512], psS1[p:p + 64, :],
                                       reads=[("psS1",)], writes=[knn])
                                trk.ve(nc.vector.tensor_copy, nn[p:p + 64, 512:1024], psS2[p:p + 64, :],
                                       reads=[("psS2",)], writes=[knn])
                                trk.ve(nc.vector.tensor_copy, nt[p:p + 64, 0:512], psM[p:p + 64, :],
                                       reads=[("psM",)], writes=[knt])
                                trk.ve(nc.vector.tensor_copy, nt[p:p + 64, 512:1024], psM2[p:p + 64, :],
                                       reads=[("psM2",)], writes=[knt])
                                curN, curNT = nn, nt
                                ckN, ckNT = knn, knt
                        if _KCUT < 4:
                            continue
                        with trk.tg(reads=[("rX", rb), ("cIDT",)], writes=[("psS1",)]) as t:
                            for h in range(16):
                                t.add(nc.tensor.transpose(
                                    psS1b[0:64, h * 64:(h + 1) * 64],
                                    rX[rb][0:64, h * 128 + 64:(h + 1) * 128],
                                    cIDT[0:64, 0:64]))
                        trk.se(nc.scalar.activation, tkgT[0:64, :], psS1b[0:64, :], AF.Copy, scale=-1.0,
                               reads=[("psS1",)], writes=[("tkgTk",)])
                        for wave in range(2):
                            for hh in range(8):
                                h = wave * 8 + hh
                                bh = bhof(h)
                                bank = hh // 4
                                win = (hh % 4) * 64
                                with trk.tg(reads=[("tkgTk",), ("Sbf",)],
                                            writes=[("psX", bank)]) as t:
                                    t.add(nc.tensor.matmul(psX[bank][0:64, win:win + 64],
                                                           tkgT[0:64, h * 64:(h + 1) * 64],
                                                           S_bf[0:64, h * 64:(h + 1) * 64],
                                                           start=True, stop=True))
                            for bank in range(2):
                                h0 = wave * 8 + bank * 4
                                trk.ve(nc.vector.tensor_tensor,
                                       bass.AP(u_sb, (wave * 512 + bank * 256),
                                               [[1024, 64], [64, 4], [1, 64]]),
                                       bass.AP(psX[bank], 0, [[512, 64], [64, 4], [1, 64]]),
                                       bass.AP(rX[rb], h0 * 128, [[2048, 64], [128, 4], [1, 64]]),
                                       op=OP.add,
                                       reads=[("psX", bank), ("rX", rb)], writes=[("u",)])
                        with trk.tg(reads=[("Sbf",), ("rQg", rb), ("u",), ("rP", rb)], writes=[("psS1",)]) as t:
                            for h in range(0, 16, 2):
                                win = (h // 2) * 64
                                t.add(nc.tensor.matmul(psS1[0:64, win:win + 64],
                                                       S_bf[0:64, h * 64:(h + 1) * 64],
                                                       rQg[rb][0:64, h * 64:(h + 1) * 64],
                                                       start=True, stop=False))
                                t.add(nc.tensor.matmul(psS1[0:64, win:win + 64],
                                                       u_sb[0:64, h * 64:(h + 1) * 64],
                                                       rP[rb][0:64, h * 64:(h + 1) * 64],
                                                       start=False, stop=True))
                        with trk.tg(reads=[("Sbf",), ("rQg", rb), ("u",), ("rP", rb)], writes=[("psS2",)]) as t:
                            for h in range(1, 16, 2):
                                win = (h // 2) * 64
                                t.add(nc.tensor.matmul(psS2[64:128, win:win + 64],
                                                       S_bf[0:64, h * 64:(h + 1) * 64],
                                                       rQg[rb][0:64, h * 64:(h + 1) * 64],
                                                       start=True, stop=False))
                                t.add(nc.tensor.matmul(psS2[64:128, win:win + 64],
                                                       u_sb[0:64, h * 64:(h + 1) * 64],
                                                       rP[rb][0:64, h * 64:(h + 1) * 64],
                                                       start=False, stop=True))
                        src0 = bass.AP(psS1, 0, [[512, 64], [64, 8], [1, 64]])
                        dst0 = bass.AP(y_fm, c * 64, [[8192, 64], [1024, 8], [1, 64]])
                        trk.ve(nc.vector.tensor_copy, dst0, src0,
                               reads=[("psS1",)], writes=[("y", c, 0)])
                        src1 = bass.AP(psS2, 64 * 512, [[512, 64], [64, 8], [1, 64]])
                        dst1 = bass.AP(y_fm, 64 * 8192 + c * 64, [[8192, 64], [1024, 8], [1, 64]])
                        trk.ve(nc.vector.tensor_copy, dst1, src1,
                               reads=[("psS2",)], writes=[("y", c, 1)])
                        if _KCUT < 5:
                            continue
                        trk.ve(nc.vector.tensor_tensor, ssc_f[0:64, :], S_run[0:64, :],
                               bass.AP(ecc2, chb, [[256, 64], [1, 16], [0, 64]]), op=OP.mult,
                               reads=[("Srun",), ("ecc2",)], writes=[("ssc",)])
                        with trk.tg(reads=[("rKd", rb), ("u",)], writes=[("psM",)]) as t:
                            for h in range(8):
                                win = h * 64
                                t.add(nc.tensor.matmul(psM[0:64, win:win + 64],
                                                       rKd[rb][0:64, h * 64:(h + 1) * 64],
                                                       u_sb[0:64, h * 64:(h + 1) * 64],
                                                       start=True, stop=True))
                        with trk.tg(reads=[("rKd", rb), ("u",)], writes=[("psM2",)]) as t:
                            for h in range(8, 16):
                                win = h * 64
                                t.add(nc.tensor.matmul(psM2[0:64, win - 512:win - 512 + 64],
                                                       rKd[rb][0:64, h * 64:(h + 1) * 64],
                                                       u_sb[0:64, h * 64:(h + 1) * 64],
                                                       start=True, stop=True))
                        trk.ve(nc.vector.tensor_tensor, S_run[0:64, 0:512], psM[0:64, :],
                               ssc_f[0:64, 0:512], op=OP.add,
                               reads=[("psM",), ("ssc",)], writes=[("Srun",)])
                        trk.ve(nc.vector.tensor_tensor, S_run[0:64, 512:1024], psM2[0:64, :],
                               ssc_f[0:64, 512:1024], op=OP.add,
                               reads=[("psM2",), ("ssc",)], writes=[("Srun",)])
                        trk.ve(nc.vector.tensor_copy, S_bf[0:64, :], S_run[0:64, :],
                               reads=[("Srun",)], writes=[("Sbf",)])
                        if _DBGA:
                            trk.dma("q2", d_cU[:, c * 1024:(c + 1) * 1024], u_sb[0:64, :], reads=[("u",)])
                            trk.dma("q2", d_cS2[:, c * 1024:(c + 1) * 1024], S_run[0:64, :], reads=[("Srun",)])
                            trk.dma("q2", d_cE1[:, c * 1024:(c + 1) * 1024], e1_sb[0:64, :], reads=[("e1",)])
                            trk.dma("q2", d_cE2[:, c * 1024:(c + 1) * 1024], e2_sb[0:64, :], reads=[("e2",)])
                            trk.dma("q2", d_cP[:, c * 1024:(c + 1) * 1024], rP[rb][0:64, :], reads=[("rP", rb)])
                            trk.dma("q2", d_cQ[:, c * 1024:(c + 1) * 1024], rQg[rb][0:64, :], reads=[("rQg", rb)])
                        if _DBGA and c in (0, 1):
                            d_cx = d_c0 if c == 0 else d_c1
                            trk.dma("q2", d_cx[:, 0:1024], u_sb[:, :], reads=[("u",)])
                            trk.dma("q2", d_cx[:, 1024:2048], tkgT[:, :], reads=[("tkgTk",)])
                            trk.dma("q2", d_cx[:, 2048:3072], negA[:, :], reads=[("negAk",)])
                            trk.dma("q2", d_cx[:, 3072:4096], negAT[:, :], reads=[("negATk",)])
                            trk.dma("q2", d_cx[:, 4096:5120], rKd[rb][:, :], reads=[("rKd", rb)])
                            trk.dma("q2", d_cx[:, 5120:6144], rQg[rb][:, :], reads=[("rQg", rb)])
                            trk.dma("q2", d_cx[:, 6144:7168], rP[rb][:, :], reads=[("rP", rb)])
                            trk.dma("q2", d_cx[0:64, 7168:8192], S_run[0:64, :], reads=[("Srun",)])
                    trk.pre("p", reads=[("Srun",)])
                    with g.If_eq(r_par, 0):
                        g.dma_start(out=shS16[0:64, :], in_=S_run[0:64, :]).then_inc(pollsem, 16)
                        g.reg_add(r_pw, r_pw, 16)
                        g.wait_ge(pollsem, r_pw)
                        g.memset(ftile[0:1, 8:9], FLAGV + 8)
                        g.dma_start(out=shFlg[0, 8:9], in_=ftile[0:1, 8:9]).then_inc(pollsem, 16)
                        g.reg_add(r_pw, r_pw, 16)
                        g.wait_ge(pollsem, r_pw)
                    inst = g.engine_nop()
                    trk.post("p", inst, reads=[("Srun",)])
                trk.join()
                if _DBGA:
                    trk.dma("q2", d_y[:, :], y_fm[:, :],
                            reads=[("y", c2, ph2) for c2 in range(16) for ph2 in range(2)])
                    trk.dma("q2", d_S[:, :], S_run[0:64, :], reads=[("Srun",)])
                if _ACUT == 5:
                    return

                with ExitStack() as stF:
                    wos = stF.enter_context(nc.sbuf_tensor(un("at_wos"), [128, 8 * 2048], BF16))
                    yq = stF.enter_context(nc.sbuf_tensor(un("at_yq"), [128, 8 * 1024], BF16))
                    sqf = stF.enter_context(nc.sbuf_tensor(un("at_sqf"), [128, 1024], F32))
                    nrf = stF.enter_context(nc.sbuf_tensor(un("at_nrf"), [2, 512], F32))
                    t1f = stF.enter_context(nc.sbuf_tensor(un("at_t1f"), [128, 512], F32))
                    zr = stF.enter_context(nc.sbuf_tensor(un("at_zr"), [128, 1024], BF16))
                    xrf = stF.enter_context(nc.sbuf_tensor(un("at_xrf"), [128, D], F32))
                    xof = stF.enter_context(nc.sbuf_tensor(un("at_xof"), [128, D], F32))
                    psF2 = stF.enter_context(nc.psum_tensor(un("at_psF2"), [2, 512], F32))
                    psB2 = stF.enter_context(nc.psum_tensor(un("at_psB2"), [128, 512], F32))
                    psW = [stF.enter_context(nc.psum_tensor(un("at_psW"), [128, 1024], F32)) for _ in range(2)]
                    rd_y = [("y", c2, ph2) for c2 in range(16) for ph2 in range(2)]
                    for yt in range(8):
                        ysl = y_fm[:, yt * 1024:(yt + 1) * 1024]
                        trk.se(nc.scalar.activation, sqf[:], ysl, AF.Square,
                               reads=rd_y, writes=[("sqf",)])
                        trk.dma("q1", zr[:, :], zsp[yt * 128:(yt + 1) * 128, :],
                                reads=[("zsp", yt)], writes=[("zr",)])
                        for th2 in range(2):
                            with trk.tg(reads=[("sqf",), ("bd2",)], writes=[("psF2",)]) as t:
                                t.add(nc.tensor.matmul(psF2[:, :], bd2[:],
                                                       sqf[:, th2 * 512:(th2 + 1) * 512],
                                                       start=True, stop=True))
                            trk.ve(nc.vector.tensor_scalar, nrf[:, :], psF2[:, :], 1.0 / 64.0, 1e-6,
                                   op0=OP.mult, op1=OP.add, reads=[("psF2",)], writes=[("nrf",)])
                            trk.ve(nc.vector.reciprocal, nrf[:, :], nrf[:, :],
                                   reads=[("nrf",)], writes=[("nrf",)])
                            trk.se(nc.scalar.activation, nrf[:, :], nrf[:, :], AF.Sqrt,
                                   reads=[("nrf",)], writes=[("nrf",)])
                            with trk.tg(reads=[("nrf",), ("b2up",)], writes=[("psB2",)]) as t:
                                t.add(nc.tensor.matmul(psB2[:, :], b2up[:], nrf[:, :],
                                                       start=True, stop=True))
                            trk.ve(nc.vector.tensor_tensor, t1f[:, :],
                                   ysl[:, th2 * 512:(th2 + 1) * 512] if False else y_fm[:, yt * 1024 + th2 * 512:yt * 1024 + (th2 + 1) * 512],
                                   psB2[:, :], op=OP.mult,
                                   reads=rd_y + [("psB2",)], writes=[("t1f",)])
                            trk.ve(nc.vector.tensor_scalar, t1f[:, :], t1f[:, :], nw_sb[:, 0:1], None,
                                   op0=OP.mult, reads=[("t1f",), ("nw",)], writes=[("t1f",)])
                            trk.ve(nc.vector.tensor_tensor,
                                   yq[:, yt * 1024 + th2 * 512:yt * 1024 + (th2 + 1) * 512],
                                   t1f[:, :], zr[:, th2 * 512:(th2 + 1) * 512],
                                   op=OP.mult, reads=[("t1f",), ("zr",)], writes=[("yqk", yt)])
                    for k in range(8):
                        trk.dma("q0", wos[:, k * 2048:(k + 1) * 2048],
                                shWout[0:2048, k * 128:(k + 1) * 128],
                                reads=[("shWoutb",)], writes=[("wos", k)], transpose=True)
                    for m in range(8):
                        pw = psW[m % 2]
                        pwk = ("psW", m % 2)
                        rdw = [("wos", k2) for k2 in range(8)] + [("yqk", y2) for y2 in range(8)]
                        trk.dma("q1", xrf[:, :], xsrc[m * 128:(m + 1) * 128, :],
                                reads=([(src_key, m)] if src_key else []), writes=[("xrf",)])
                        for half in range(2):
                            with trk.tg(reads=rdw, writes=[pwk]) as t:
                                for k in range(8):
                                    for nn2 in range(2):
                                        t.add(nc.tensor.matmul(
                                            pw[:, nn2 * 512:(nn2 + 1) * 512],
                                            yq[:, k * 1024 + m * 128:k * 1024 + (m + 1) * 128],
                                            wos[:, k * 2048 + half * 1024 + nn2 * 512:k * 2048 + half * 1024 + (nn2 + 1) * 512],
                                            start=(k == 0), stop=(k == 7)))
                            trk.ve(nc.vector.tensor_tensor, xof[:, half * 1024:(half + 1) * 1024],
                                   pw[:], xrf[:, half * 1024:(half + 1) * 1024], op=OP.add,
                                   reads=[pwk, ("xrf",)], writes=[("xofk", half)])
                        trk.dma("q2", xdst[m * 128:(m + 1) * 128, :], xof[:],
                                reads=[("xofk", 0), ("xofk", 1)], writes=[(out_key, m)])
                trk.join()

        if _DBG:
            trk.dma("q2", d_swcol[:, :], swcol[:, 0:8], reads=[("swb", 0), ("swb", 1), ("swb", 2)])
            trk.dma("q2", d_wg[:, :], shWg[0:128, :], reads=[("shWgq",)])
            trk.dma("q2", d_wd[:, :], shWd[0:128, 0:1024], reads=[("shWdq",)])
        trk.join()
        stPP.close()
        trk.join()
        if phases == "ffn1":
            ffn(xs, xo, "XO", None)
        elif phases == "cast":
            ffn(xs, xo, "XO", None)
            pp_cast_attn()
        elif phases == "attn":
            ffn(xs, xsp1, "X1", None)
            pp_cast_attn()
            attention(xsp1, "X1", xo, "XO")
        elif phases == "attnonly":
            pp_cast_attn()
            attention(xsp1, None, xo, "XO")
        elif phases == "noattn":
            ffn(xs, xsp1, "X1", None)
            ffn(xsp1, xo, "XO", "X1")
        else:
            ffn(xs, xsp1, "X1", None)
            pp_cast_attn()
            attention(xsp1, "X1", xsp2, "X2")
            ffn(xsp2, xo, "XO", "X2")
        trk.wait_all_on("p")

    return nc


def kernel(L0_attn_x, L0_attn_W_qkv, L0_attn_W_z, L0_attn_W_b, L0_attn_W_a,
           L0_attn_W_out, L0_attn_conv_w, L0_attn_dt_bias, L0_attn_A_log,
           L0_attn_norm_weight, L0_ffn_w_gate, L0_ffn_w_up, L0_ffn_w_down):
    x = np.ascontiguousarray(np.asarray(L0_attn_x, dtype=np.float32)).reshape(8192, D)
    wg = np.ascontiguousarray(np.asarray(L0_ffn_w_gate, dtype=np.float32))
    wu = np.ascontiguousarray(np.asarray(L0_ffn_w_up, dtype=np.float32))
    wd = np.ascontiguousarray(np.asarray(L0_ffn_w_down, dtype=np.float32))
    wqkv = np.ascontiguousarray(np.asarray(L0_attn_W_qkv, dtype=np.float32))
    wz = np.ascontiguousarray(np.asarray(L0_attn_W_z, dtype=np.float32))
    wb = np.asarray(L0_attn_W_b, dtype=np.float32)
    wa = np.asarray(L0_attn_W_a, dtype=np.float32)
    wout = np.ascontiguousarray(np.asarray(L0_attn_W_out, dtype=np.float32))
    cw = np.ascontiguousarray(np.asarray(L0_attn_conv_w, dtype=np.float32).reshape(D, 4))
    sm = np.zeros((3, 64), np.float32)
    sm[0, :16] = np.asarray(L0_attn_dt_bias, dtype=np.float32)
    sm[1, :16] = np.asarray(L0_attn_A_log, dtype=np.float32)
    sm[2, :64] = np.asarray(L0_attn_norm_weight, dtype=np.float32)
    wba = np.ascontiguousarray(np.concatenate([wb, wa], axis=0))

    H2048 = _hadamard(2048).astype(ml_dtypes.bfloat16)
    B1 = np.kron(np.eye(2), _hadamard(64)).astype(ml_dtypes.bfloat16)
    H128 = _hadamard(128).astype(ml_dtypes.bfloat16)
    IDT = np.eye(128).astype(ml_dtypes.bfloat16)
    attcn = np.zeros((6, 64 * 64), np.float32)
    attcn[0] = np.tril(np.ones((64, 64)), -1).reshape(-1)
    attcn[1] = np.tril(np.ones((64, 64)), 0).reshape(-1)
    attcn[2] = np.triu(np.ones((64, 64)), 1).reshape(-1)
    attcn[3] = np.triu(np.ones((64, 64)), 0).reshape(-1)
    bd2c = np.zeros((128, 2), np.float32)
    bd2c[0:64, 0] = 1.0
    bd2c[64:128, 1] = 1.0
    attcn[4, 0:256] = bd2c.reshape(-1)
    attcn[5, 0:256] = bd2c.T.reshape(-1)

    in_maps = []
    for c in range(8):
        par = c % 2
        m = dict(
            xs=x[c * TPC:(c + 1) * TPC],
            wg_raw=wg[par * (IFF // 2):(par + 1) * (IFF // 2)],
            wu_raw=wu[par * (IFF // 2):(par + 1) * (IFF // 2)],
            wd_raw=wd[par * (D // 2):(par + 1) * (D // 2)],
            wqkv_raw=wqkv[par * 1024:(par + 1) * 1024],
            wz_raw=wz[par * 512:(par + 1) * 512],
            wout_raw=wout[par * 1024:(par + 1) * 1024],
            wba_raw=wba,
            convw=cw,
            smalls=sm,
            h2048c=H2048, b1c=B1, h128c=H128, idtc=IDT, attc=attcn,
        )
        in_maps.append(m)

    nc = build_program(_PHASES)
    res = run_bass_kernel_spmd(nc, in_maps, core_ids=list(range(8)))
    global _LAST_RES
    _LAST_RES = res
    out = np.concatenate([np.asarray(r["xo"]) for r in res.results], axis=0)
    return out.reshape(4, 2048, D).astype(np.float32)
